# version 03
# baseline (speedup 1.0000x reference)
from contextlib import ExitStack

import numpy as np
import ml_dtypes

import concourse.bass as bass
import concourse.mybir as mybir
from concourse.bass_utils import run_bass_kernel_spmd
from concourse.bass_isa import ReduceOp

F32 = mybir.dt.float32
BF16 = mybir.dt.bfloat16
I32 = mybir.dt.int32
ALU = mybir.AluOpType
ACTF = mybir.ActivationFunctionType
AX = mybir.AxisListType.X

B, T, D = 4, 4096, 1024
INNER = 4096
LAT = 256
NT = 16
ST = 4
NST = NT // ST
MROUND = 12582912.0
LN_EPS = 1e-5
MAXPOLL = 30_000_000

ALL_PHASES = ("wprep", "ffn1", "qlat", "attn", "wo", "ffn2")


class _Fake:
    def __call__(self, *a, **k):
        return self

    def __getattr__(self, name):
        return self

    def __enter__(self):
        return self

    def __exit__(self, *a):
        return False


_FAKE = _Fake()


class Ctr:
    def __init__(self, name, sem, step=1):
        self.name, self.sem, self.v, self.step = name, sem, 0, step

    def inc(self, inst, n=None):
        n = self.step if n is None else n
        self.v += n
        if inst is not _FAKE:
            inst.then_inc(self.sem, n)
        return self.v


class Marks:

    def __init__(self):
        self.m0 = None
        self.w = {}

    def start_pass(self, final):
        if final:
            self.m0 = dict(self.w)
        self.w = {}

    def __setitem__(self, k, v):
        if self.m0 is not None and k in self.m0:
            assert self.m0[k] == v, f"mark drift {k}: {self.m0[k]} != {v}"
        self.w[k] = v

    def __contains__(self, k):
        return k in self.w

    def __getitem__(self, k):
        if self.m0 is not None:
            return self.m0[k]
        return self.w.get(k, ("none", 0))


def build_nc(phases=ALL_PHASES, dbg=False):
    nc = bass.Bass("TRN2", target_bir_lowering=False, debug=False,
                   detect_race_conditions=False)

    def din(n, s, dt=F32):
        return nc.dram_tensor(n, s, dt, kind="ExternalInput").ap()

    xin = din("xin", [2048, D])
    wq_d = din("wq", [D, D])
    wdkv_d = din("wdkv", [D, LAT])
    wuk_d = din("wuk", [LAT, D])
    wuv_d = din("wuv", [LAT, D])
    wo_d = din("wo", [D, D])
    gam_d = din("gam", [128, LAT])
    bet_d = din("bet", [128, LAT])
    wg_d = din("wg", [INNER, D])
    wu_d = din("wu", [INNER, D])
    wd_d = din("wd", [D, INNER])
    qcos_d = din("qcos", [2048, 512], BF16)
    qsin_d = din("qsin", [2048, 512], BF16)
    kcos_d = din("kcos", [T, 256], BF16)
    ksin_d = din("ksin", [T, 256], BF16)
    masks_d = din("masks", [128, 256], BF16)
    ident_d = din("ident", [128, 128], BF16)
    xoffs_d = din("xoffs", [1, 8], I32)
    magic_d = din("magic", [1, 2], I32)

    yout = nc.dram_tensor("yout", [2048, D], F32, kind="ExternalOutput").ap()
    dbg_d = (nc.dram_tensor("dbg", [128, 16384], F32, kind="ExternalOutput").ap()
             if dbg else None)

    lat_sh = nc.dram_tensor("lat_sh", [T, LAT], BF16, kind="Internal",
                            addr_space="Shared").ap()
    flag_sh = nc.dram_tensor("flag_sh", [2, 2], I32, kind="Internal",
                             addr_space="Shared").ap()
    x1sp = nc.dram_tensor("x1sp", [2048, D], F32, kind="Internal").ap()
    lat_pv = nc.dram_tensor("lat_pv", [2048, LAT], BF16, kind="Internal").ap()
    wgT_s = nc.dram_tensor("wgT_s", [8, 128, INNER], BF16, kind="Internal").ap()
    wuT_s = nc.dram_tensor("wuT_s", [8, 128, INNER], BF16, kind="Internal").ap()
    wdT_s = nc.dram_tensor("wdT_s", [32, 128, D], BF16, kind="Internal").ap()

    ctx = ExitStack()
    e = ctx.enter_context

    ident_sb = e(nc.sbuf_tensor("ident_sb", [128, 128], BF16))
    col = e(nc.sbuf_tensor("col", [128, 40], F32))
    ag_sb = e(nc.sbuf_tensor("ag_sb", [128, NT], F32))
    au_sb = e(nc.sbuf_tensor("au_sb", [128, NT], F32))
    adh_sb = e(nc.sbuf_tensor("adh_sb", [128, NT], F32))
    mclip = e(nc.sbuf_tensor("mclip", [128, 4], F32))
    wrecip = e(nc.sbuf_tensor("wrecip", [128, 4], F32))
    gam_sb = e(nc.sbuf_tensor("gam_sb", [128, LAT], F32))
    bet_sb = e(nc.sbuf_tensor("bet_sb", [128, LAT], F32))
    masks_sb = e(nc.sbuf_tensor("masks_sb", [128, 256], BF16))
    ones_sb = e(nc.sbuf_tensor("ones_sb", [128, 128], F32))
    offs_sb = e(nc.sbuf_tensor("offs_sb", [1, 8], I32))
    magic_sb = e(nc.sbuf_tensor("magic_sb", [1, 2], I32))

    pb03 = e(nc.psum_tensor("pb03", [128, 2048], F32))
    pb45 = e(nc.psum_tensor("pb45", [128, 1024], F32))
    pb6t = e(nc.psum_tensor("pb6", [128, 512], F32))
    pb7t = e(nc.psum_tensor("pb7", [128, 512], F32))
    pb = ([pb03[:, i * 512:(i + 1) * 512] for i in range(4)]
          + [pb45[:, i * 512:(i + 1) * 512] for i in range(2)]
          + [pb6t[:], pb7t[:]])

    sem_T = e(nc.semaphore("sem_T"))
    sem_V = e(nc.semaphore("sem_V"))
    sem_S = e(nc.semaphore("sem_S"))
    sem_G = e(nc.semaphore("sem_G"))
    DMA_SLOTS = ("cP0", "wpA", "wpB", "wsA", "wsB", "fxA", "fxB", "fgA",
                 "fgB", "fd0", "fd1", "fd2", "fd3", "fstA", "fstB", "cP3w",
                 "cQT", "p3A", "p3B", "lst", "flg", "cLT", "cP4w", "ktA",
                 "ktB", "cP5w", "p5A", "p5B", "cp")
    dma_sems = {n: e(nc.semaphore("sem_" + n)) for n in DMA_SLOTS}

    block = e(nc.Block())
    M = Marks()

    sbuf_cache = {}

    def emit(final):
        C = {}
        C["T"] = Ctr("T", sem_T)
        C["V"] = Ctr("V", sem_V)
        C["S"] = Ctr("S", sem_S)
        C["G"] = Ctr("G", sem_G)
        for n, sem in dma_sems.items():
            C[n] = Ctr(n, sem, 16)
        sT, sV, sS, sG = C["T"], C["V"], C["S"], C["G"]
        M.start_pass(final)

        def dma(E, slot, out, in_, **kw):
            inst = E.dma_start(out=out, in_=in_, **kw)
            C[slot].inc(inst)
            return (slot, C[slot].v)

        dbg_col = [0]
        dbg_buf = None

        def tap(E, ap, n):
            if dbg_d is None:
                return
            c0 = dbg_col[0]
            dbg_col[0] += n
            dma(E, "cp", dbg_d[:, c0:c0 + n], ap)
            if E is not _FAKE:
                E.wait_ge(C["cp"].sem, C["cp"].v)

        bank_readers = {i: [] for i in range(8)}
        bank_writer = {i: ("T", 0) for i in range(8)}

        def wt(E, mark):
            tag, v = mark
            if tag == "none" or v == 0:
                return
            if E is not _FAKE:
                E.wait_ge(C[tag].sem, v)

        def jnk(E):
            E.tensor_copy(out=col[:, 38:39], in_=ones_sb[:, 0:1])

        def vdrain(E):
            if E is not _FAKE:
                E.wait_ge(sem_V, C["V"].v)

        def t_acquire(E, b):
            pass

        def t_wrote(b):
            bank_writer[b] = ("T", sT.v)

        def r_wait(E, b):
            wt(E, bank_writer[b])

        def r_done(b, tag):
            pass

        def body(engname):
            def deco(fn):
                if final:
                    getattr(block, engname)(fn)
                else:
                    fn(_FAKE)
            return deco

        def RAP(tensor, off, pat):
            if not final:
                return None
            return bass.AP(tensor, off, pat)

        def sbt(name, shape, dt):
            if name not in sbuf_cache:
                guard = nc.sbuf_tensor(name, shape, dt)
                handle = guard.__enter__()
                sbuf_cache[name] = (guard, handle)
            return sbuf_cache[name][1]

        if dbg_d is not None:
            dbg_buf = sbt("dbg_buf", [128, D], BF16)
            dbgc = sbt("dbgc", [128, 32], F32)

        def sbt_free(*names):
            if final:
                return
            for name in reversed(names):
                guard, _ = sbuf_cache[name]
                guard.__exit__(None, None, None)

        def barrier():
            vals = [(k, c.v) for k, c in C.items()]

            def waits(E, skip):
                for k, v in vals:
                    if k != skip and v > 0 and E is not _FAKE:
                        E.wait_ge(C[k].sem, v)

            for engname, skip in (("tensor", "T"), ("vector", "V"),
                                  ("scalar", "S"), ("gpsimd", "G"),
                                  ("sync", None)):
                @body(engname)
                def _(E, skip=skip):
                    waits(E, skip)

        @body("sync")
        def _(E):
            for dst, srct in ((ident_sb, ident_d), (gam_sb, gam_d),
                              (bet_sb, bet_d), (masks_sb, masks_d),
                              (offs_sb, xoffs_d), (magic_sb, magic_d)):
                mk = dma(E, "cP0", dst[:], srct[:])
            M["p0"] = mk

        @body("vector")
        def _(E):
            sV.inc(E.memset(ones_sb[:], 1.0))
            M["ones"] = ("V", sV.v)

        if "wprep" in phases:
            wp_f32 = sbt("wp_f32", [128, 2 * INNER], F32)
            wp_i32 = sbt("wp_i32", [128, INNER], F32)
            wp_t = sbt("wp_t", [128, INNER], F32)
            wp_m = sbt("wp_m", [128, 4 * INNER], BF16)
            wp_st = sbt("wp_st", [128, 2 * 512], BF16)
            wp_acc = sbt("wp_acc", [128, 4], F32)
            wp_tmp = sbt("wp_tmp", [128, 4], F32)
            wp_strip = sbt("wp_strip", [128, 72], F32)

            WCFG = [("g", wg_d, wgT_s, 32, D),
                    ("u", wu_d, wuT_s, 32, D),
                    ("d", wd_d, wdT_s, 8, INNER)]
            CHUNKS = [(wi, tag, wdram, ch, K)
                      for wi, (tag, wdram, _w, nch, K) in enumerate(WCFG)
                      for ch in range(nch)]

            @body("scalar")
            def _(E):
                for gi, (wi, tag, wdram, ch, K) in enumerate(CHUNKS):
                    half = gi % 2
                    if gi >= 2:
                        wt(E, M[f"wp1v_{gi - 2}"])
                    M[f"wp1ld_{gi}"] = dma(
                        E, "wpA" if half == 0 else "wpB",
                        wp_f32[:, half * INNER:half * INNER + K],
                        wdram[ch * 128:(ch + 1) * 128, :])

            @body("vector")
            def _(E):
                for gi, (wi, tag, wdram, ch, K) in enumerate(CHUNKS):
                    half = gi % 2
                    wt(E, M[f"wp1ld_{gi}"])
                    src = wp_f32[:, half * INNER:half * INNER + K]
                    sV.inc(E.reduce_sum(
                        out=wp_strip[:, gi:gi + 1], in_=src, axis=AX,
                        apply_absolute_value=True))
                    M[f"wp1v_{gi}"] = ("V", sV.v)
                E.reduce_sum(out=wp_acc[:, 0:1], in_=wp_strip[:, 0:32], axis=AX)
                E.reduce_sum(out=wp_acc[:, 1:2], in_=wp_strip[:, 32:64], axis=AX)
                sV.inc(E.reduce_sum(out=wp_acc[:, 2:3], in_=wp_strip[:, 64:72],
                                    axis=AX))
                M["wp1_done"] = ("V", sV.v)

            @body("tensor")
            def _(E):
                wt(E, M["wp1_done"])
                wt(E, M["ones"])
                t_acquire(E, 6)
                sT.inc(E.matmul(out=pb[6][0:1, 0:3], lhsT=ones_sb[:, 0:1],
                                rhs=wp_acc[:, 0:3], start=True, stop=True))
                t_wrote(6)
                M["wp_sum"] = ("T", sT.v)

            @body("vector")
            def _(E):
                wt(E, M["wp_sum"])
                sV.inc(E.tensor_copy(out=wp_tmp[0:1, 0:3], in_=pb[6][0:1, 0:3]))
                r_done(6, "V")
                M["wp_sumrow"] = ("V", sV.v)

            @body("tensor")
            def _(E):
                wt(E, M["wp_sumrow"])
                t_acquire(E, 6)
                sT.inc(E.matmul(out=pb[6][:, 0:3], lhsT=ones_sb[0:1, 0:128],
                                rhs=wp_tmp[0:1, 0:3], start=True, stop=True))
                t_wrote(6)
                M["wp_bc"] = ("T", sT.v)

            @body("vector")
            def _(E):
                wt(E, M["wp_bc"])
                E.tensor_scalar(out=wp_tmp[:, 0:3], in0=pb[6][:, 0:3],
                                scalar1=1.0 / 4194304.0, scalar2=1e-5,
                                op0=ALU.mult, op1=ALU.max)
                r_done(6, "V")
                E.tensor_copy(out=col[:, 38:39], in_=ones_sb[:, 0:1])
                E.tensor_copy(out=mclip[:, 0:3], in_=wp_tmp[:, 0:3])
                sV.inc(E.reciprocal(out=wrecip[:, 0:3], in_=wp_tmp[:, 0:3]))
                M["wscale"] = ("V", sV.v)

            if dbg_d is not None:
                @body("gpsimd")
                def _(E):
                    wt(E, M["wscale"])
                    tap(E, mclip[:, 0:3], 3)
                    tap(E, wrecip[:, 0:3], 3)

            @body("sync")
            def _(E):
                for gi, (wi, tag, wdram, ch, K) in enumerate(CHUNKS):
                    half = gi % 2
                    if gi >= 2:
                        wt(E, M[f"wp2q_{gi - 2}"])
                    else:
                        wt(E, M[f"wp1v_{len(CHUNKS) - 2 + gi}"])
                    M[f"wp2ld_{gi}"] = dma(
                        E, "wpA" if half == 0 else "wpB",
                        wp_f32[:, half * INNER:half * INNER + K],
                        wdram[ch * 128:(ch + 1) * 128, :])

            GRPS = []
            _off = 0
            for wi, (tag, wdram, wscr, nch, K) in enumerate(WCFG):
                for grp in range(nch // 4):
                    GRPS.append((wi, wscr, grp, K // 128, _off + grp * 4))
                _off += nch

            for gg, (wi, wscr, grp, nib, gi0) in enumerate(GRPS):
                @body("vector")
                def _(E, gg=gg, wi=wi, grp=grp, nib=nib, gi0=gi0):
                    if gg == 0:
                        wt(E, M["wscale"])
                    for g4 in range(4):
                        gi = gi0 + g4
                        K = CHUNKS[gi][4]
                        half = gi % 2
                        wt(E, M[f"wp2ld_{gi}"])
                        if g4 == 0 and gg >= 1:
                            wt(E, M[f"wpflips_{gg - 1}"])
                        srcw = wp_f32[:, half * INNER:half * INNER + K]
                        nC = K // 512
                        stages = []
                        for cc in range(nC):
                            stages.append(("a", cc))
                            if cc >= 1:
                                stages.append(("b", cc - 1))
                            if cc >= 2:
                                stages.append(("c", cc - 2))
                        stages += [("b", nC - 1), ("c", nC - 2), ("c", nC - 1)]
                        for st_, cc in stages:
                            lo, hi = cc * 512, (cc + 1) * 512
                            if st_ == "a":
                                inst = E.tensor_scalar(
                                    out=wp_i32[:, lo:hi], in0=srcw[:, lo:hi],
                                    scalar1=wrecip[:, wi:wi + 1],
                                    scalar2=MROUND, op0=ALU.mult, op1=ALU.add)
                            elif st_ == "b":
                                inst = E.tensor_scalar(
                                    out=wp_t[:, lo:hi], in0=wp_i32[:, lo:hi],
                                    scalar1=-MROUND, scalar2=1.0,
                                    op0=ALU.add, op1=ALU.min)
                            else:
                                inst = E.tensor_scalar(
                                    out=wp_m[:, g4 * INNER + lo:
                                             g4 * INNER + hi],
                                    in0=wp_t[:, lo:hi], scalar1=-1.0,
                                    scalar2=None, op0=ALU.max)
                        sV.inc(inst)
                        M[f"wp2q_{gi}"] = ("V", sV.v)

                @body("tensor")
                def _(E, gg=gg, nib=nib, gi0=gi0):
                    wt(E, M[f"wp2q_{gi0 + 3}"])
                    for ib in range(nib):
                        b = 4 + (ib % 2)
                        if ib >= 2:
                            wt(E, M[f"wpev_{gg}_{ib - 2}"])
                        elif gg > 0:
                            pnib = GRPS[gg - 1][3]
                            wt(E, M[f"wpev_{gg - 1}_{pnib - 2 + ib}"])
                        for j in range(4):
                            mm = E.matmul(
                                out=pb[b][:, j * 128:(j + 1) * 128],
                                lhsT=wp_m[:, j * INNER + ib * 128:
                                          j * INNER + ib * 128 + 128],
                                rhs=ident_sb[:], start=True, stop=True)
                        sT.inc(mm)
                        M[f"wpflip_{gg}_{ib}"] = ("T", sT.v)
                    M[f"wpflips_{gg}"] = ("T", sT.v)

                @body("vector")
                def _(E, gg=gg, nib=nib):
                    for ib in range(nib):
                        b = 4 + (ib % 2)
                        wt(E, M[f"wpflip_{gg}_{ib}"])
                        if ib >= 2:
                            wt(E, M[f"wpstd_{gg}_{ib - 2}"])
                        elif gg > 0:
                            wt(E, M[f"wpstd_{gg - 1}_last{ib}"])
                        sV.inc(E.tensor_copy(
                            out=wp_st[:, (ib % 2) * 512:(ib % 2) * 512 + 512],
                            in_=pb[b][:]))
                        M[f"wpev_{gg}_{ib}"] = ("V", sV.v)

                @body("gpsimd")
                def _(E, gg=gg, wscr=wscr, grp=grp, nib=nib):
                    for ib in range(nib):
                        wt(E, M[f"wpev_{gg}_{ib}"])
                        M[f"wpstd_{gg}_{ib}"] = dma(
                            E, "wsA" if ib % 2 == 0 else "wsB",
                            wscr[ib, :, grp * 512:(grp + 1) * 512],
                            wp_st[:, (ib % 2) * 512:(ib % 2) * 512 + 512])
                        if ib == nib - 2:
                            M[f"wpstd_{gg}_last0"] = M[f"wpstd_{gg}_{ib}"]
                        if ib == nib - 1:
                            M[f"wpstd_{gg}_last1"] = M[f"wpstd_{gg}_{ib}"]

            @body("gpsimd")
            def _(E):
                M["wprep_done"] = M[f"wpstd_{len(GRPS) - 1}_{31}"]

            barrier()
            sbt_free("wp_f32", "wp_i32", "wp_t", "wp_m", "wp_st", "wp_acc",
                     "wp_tmp", "wp_strip")

        def emit_ffn(fid, src_dram, dst_dram):
            pref = f"f{fid}"
            x_st = sbt(f"{pref}x", [128, 2 * ST * D], F32)
            h_sb = sbt(f"{pref}h", [128, ST * INNER], F32)
            hqT = sbt(f"{pref}hqT", [128, ST * INNER], BF16)
            xqT = sbt(f"{pref}xqT", [128, ST * D], BF16)
            wbg = sbt(f"{pref}wbg", [128, 2 * INNER], BF16)
            wbu = sbt(f"{pref}wbu", [128, 2 * INNER], BF16)
            wbd = sbt(f"{pref}wbd", [128, 4 * D], BF16)
            scr = sbt(f"{pref}scr", [128, D], F32)
            hq_bf = sbt(f"{pref}hqb", [128, INNER], BF16)
            xq_bf = sbt(f"{pref}xqb", [128, D], BF16)
            gate = sbt(f"{pref}gate", [128, 512], F32)

            @body("sync")
            def _(E):
                for s in range(NST):
                    if s >= 2:
                        wt(E, M[f"{pref}res_{s - 2}"])
                        wt(E, M[f"{pref}xst_{s - 2}"])
                    M[f"{pref}xld_{s}"] = dma(
                        E, "fxA" if s % 2 == 0 else "fxB",
                        x_st[:, (s % 2) * ST * D:(s % 2 + 1) * ST * D],
                        src_dram[s * ST * 128:(s + 1) * ST * 128, :])
                    for oc in range(8):
                        half = oc % 2
                        if s > 0 or oc >= 2:
                            po = oc - 2 if oc >= 2 else 6 + oc
                            ps = s if oc >= 2 else s - 1
                            wt(E, M[f"{pref}gu_done_{ps}_{po}"])
                        for wscr, buf in ((wgT_s, wbg), (wuT_s, wbu)):
                            mk = dma(
                                E, "fgA" if half == 0 else "fgB",
                                buf[:, half * INNER:(half + 1) * INNER]
                                .rearrange("p (i c) -> p i c", i=8),
                                wscr[:, :, oc * 512:(oc + 1) * 512]
                                .rearrange("i p c -> p i c"))
                        M[f"{pref}wgu_{s}_{oc}"] = mk

            for s in range(NST):
                xoff = (s % 2) * ST * D

                if dbg_d is not None and fid == 1 and s == 0:
                    @body("gpsimd")
                    def _(E):
                        wt(E, M[f"{pref}sil_0_0_0"])
                        tap(E, gate[:], 512)
                        M["dbg_gate_done"] = ("cp", C["cp"].v)
                        wt(E, M[f"{pref}hcmb_0_0_0"])
                        tap(E, h_sb[:, 0:512], 512)
                        M["dbg_h_done"] = ("cp", C["cp"].v)
                        wt(E, M["dbghq"])
                        tap(E, dbg_buf[:, 0:D], D)
                        tap(E, dbgc[:, 0:3], 3)
                        M["dbg_hq_done"] = ("cp", C["cp"].v)

                @body("gpsimd")
                def _(E, s=s):
                    for ib in range(32):
                        if s > 0 or ib >= 4:
                            pi = ib - 4 if ib >= 4 else 28 + ib
                            ps = s if ib >= 4 else s - 1
                            wt(E, M[f"{pref}dn_{ps}_{pi}"])
                        M[f"{pref}wdl_{s}_{ib}"] = dma(
                            E, f"fd{ib % 4}",
                            wbd[:, (ib % 4) * D:(ib % 4 + 1) * D],
                            wdT_s[ib, :, :])

                @body("vector")
                def _(E, s=s, xoff=xoff):
                    wt(E, M[f"{pref}xld_{s}"])
                    if fid == 1 and s == 0 and "wprep" in phases:
                        wt(E, M["wscale"])
                    for ti in range(ST):
                        p = s * ST + ti
                        xt = x_st[:, xoff + ti * D:xoff + (ti + 1) * D]
                        cs = (ti % 2) * 16
                        E.reduce_sum(out=col[:, cs:cs + 1], in_=xt, axis=AX)
                        jnk(E)
                        sV.inc(E.tensor_scalar(
                            out=col[:, cs + 1:cs + 2], in0=col[:, cs:cs + 1],
                            scalar1=-1.0 / D, scalar2=None, op0=ALU.mult))
                        M[f"{pref}mneg_{p}"] = ("V", sV.v)
                        wt(E, M[f"{pref}sq_{p}"])
                        E.tensor_scalar(out=col[:, cs + 2:cs + 3],
                                        in0=col[:, cs + 3:cs + 4],
                                        scalar1=1.0 / D, scalar2=LN_EPS,
                                        op0=ALU.mult, op1=ALU.add)
                        sV.inc(E.reduce_max(out=col[:, cs + 4:cs + 5],
                                            in_=scr[:, 0:D], axis=AX))
                        M[f"{pref}vm_{p}"] = ("V", sV.v)
                        wt(E, M[f"{pref}sqrt_{p}"])
                        E.reciprocal(out=col[:, cs + 7:cs + 8],
                                     in_=col[:, cs + 5:cs + 6])
                        E.tensor_scalar(out=col[:, cs + 12:cs + 13],
                                        in0=col[:, cs + 6:cs + 7],
                                        scalar1=mclip[:, 0:1],
                                        scalar2=1.0 / 127.0,
                                        op0=ALU.mult, op1=ALU.mult)
                        E.tensor_scalar(out=col[:, cs + 13:cs + 14],
                                        in0=col[:, cs + 6:cs + 7],
                                        scalar1=mclip[:, 1:2],
                                        scalar2=1.0 / 127.0,
                                        op0=ALU.mult, op1=ALU.mult)
                        E.tensor_mul(out=col[:, cs + 8:cs + 9],
                                     in0=col[:, cs + 7:cs + 8],
                                     in1=col[:, cs + 6:cs + 7])
                        E.tensor_mul(out=ag_sb[:, p:p + 1],
                                     in0=col[:, cs + 12:cs + 13],
                                     in1=col[:, cs + 7:cs + 8])
                        E.tensor_mul(out=au_sb[:, p:p + 1],
                                     in0=col[:, cs + 13:cs + 14],
                                     in1=col[:, cs + 7:cs + 8])
                        E.reciprocal(out=col[:, cs + 9:cs + 10],
                                     in_=col[:, cs + 8:cs + 9])
                        jnk(E)
                        E.tensor_scalar(out=col[:, cs + 9:cs + 10],
                                        in0=col[:, cs + 9:cs + 10],
                                        scalar1=127.0, scalar2=1.27e7,
                                        op0=ALU.mult, op1=ALU.min)
                        jnk(E)
                        E.tensor_mul(out=col[:, cs + 10:cs + 11],
                                     in0=col[:, cs + 9:cs + 10],
                                     in1=col[:, cs + 7:cs + 8])
                        jnk(E)
                        E.tensor_mul(out=col[:, cs + 11:cs + 12],
                                     in0=col[:, cs + 1:cs + 2],
                                     in1=col[:, cs + 10:cs + 11])
                        jnk(E)
                        if p > 0:
                            wt(E, M[f"{pref}xqf_{p - 1}"])
                        E.tensor_scalar(out=scr[:, 0:512],
                                        in0=xt[:, 0:512],
                                        scalar1=col[:, cs + 10:cs + 11],
                                        scalar2=col[:, cs + 11:cs + 12],
                                        op0=ALU.mult, op1=ALU.add)
                        E.tensor_scalar(out=scr[:, 512:1024],
                                        in0=xt[:, 512:1024],
                                        scalar1=col[:, cs + 10:cs + 11],
                                        scalar2=col[:, cs + 11:cs + 12],
                                        op0=ALU.mult, op1=ALU.add)
                        E.tensor_scalar(out=xq_bf[:, 0:512],
                                        in0=scr[:, 0:512],
                                        scalar1=MROUND, scalar2=MROUND,
                                        op0=ALU.add, op1=ALU.subtract)
                        sV.inc(E.tensor_scalar(out=xq_bf[:, 512:1024],
                                               in0=scr[:, 512:1024],
                                               scalar1=MROUND, scalar2=MROUND,
                                               op0=ALU.add, op1=ALU.subtract))
                        M[f"{pref}xq_{p}"] = ("V", sV.v)
                        for gfl in range(2):
                            b = 4 + gfl
                            wt(E, M[f"{pref}xqfg_{p}_{gfl}"])
                            sV.inc(E.tensor_copy(
                                out=xqT[:, ti * D + gfl * 512:
                                        ti * D + gfl * 512 + 512],
                                in_=pb[b][:]))
                            M[f"{pref}xqTe_{p}_{gfl}"] = ("V", sV.v)
                        M[f"{pref}xqT_{p}"] = ("V", sV.v)


                @body("scalar")
                def _(E, s=s, xoff=xoff):
                    for ti in range(ST):
                        p = s * ST + ti
                        cs = (ti % 2) * 16
                        xt = x_st[:, xoff + ti * D:xoff + (ti + 1) * D]
                        wt(E, M[f"{pref}mneg_{p}"])
                        sS.inc(E.activation(out=scr[:, 0:D], in_=xt,
                                            func=ACTF.Square,
                                            bias=col[:, cs + 1:cs + 2],
                                            scale=1.0,
                                            accum_out=col[:, cs + 3:cs + 4]))
                        M[f"{pref}sq_{p}"] = ("S", sS.v)
                        wt(E, M[f"{pref}vm_{p}"])
                        E.activation(out=col[:, cs + 5:cs + 6],
                                     in_=col[:, cs + 2:cs + 3], func=ACTF.Sqrt)
                        sS.inc(E.activation(out=col[:, cs + 6:cs + 7],
                                            in_=col[:, cs + 4:cs + 5],
                                            func=ACTF.Sqrt))
                        M[f"{pref}sqrt_{p}"] = ("S", sS.v)

                @body("tensor")
                def _(E, s=s):
                    for ti in range(ST):
                        p = s * ST + ti
                        wt(E, M[f"{pref}xq_{p}"])
                        if ti == 0 and s > 0:
                            wt(E, M[f"{pref}res_{s - 1}"])
                        for gfl in range(2):
                            b = 4 + gfl
                            if p > 0:
                                wt(E, M[f"{pref}xqTe_{p - 1}_{gfl}"])
                            for j in range(4):
                                mm = E.matmul(
                                    out=pb[b][:, j * 128:(j + 1) * 128],
                                    lhsT=xq_bf[:, gfl * 512 + j * 128:
                                               gfl * 512 + j * 128 + 128],
                                    rhs=ident_sb[:], start=True, stop=True)
                            sT.inc(mm)
                            t_wrote(b)
                            M[f"{pref}xqfg_{p}_{gfl}"] = ("T", sT.v)
                        M[f"{pref}xqf_{p}"] = ("T", sT.v)

                @body("tensor")
                def _(E, s=s):
                    for oc in range(8):
                        half = oc % 2
                        wt(E, M[f"{pref}wgu_{s}_{oc}"])
                        for ti in range(ST):
                            p = s * ST + ti
                            if oc == 0:
                                wt(E, M[f"{pref}xqT_{p}"])
                            bg, bu = (ti % 2) * 2, (ti % 2) * 2 + 1
                            if ti >= 2:
                                wt(E, M[f"{pref}sil_{s}_{oc}_{ti - 2}"])
                                wt(E, M[f"{pref}hcmb_{s}_{oc}_{ti - 2}"])
                            elif oc >= 1:
                                wt(E, M[f"{pref}sil_{s}_{oc - 1}_{ti + 2}"])
                                wt(E, M[f"{pref}hcmb_{s}_{oc - 1}_{ti + 2}"])
                            t_acquire(E, bg)
                            for ib in range(8):
                                mm = E.matmul(
                                    out=pb[bg][:],
                                    lhsT=xqT[:, ti * D + ib * 128:
                                             ti * D + ib * 128 + 128],
                                    rhs=wbg[:, half * INNER + ib * 512:
                                            half * INNER + ib * 512 + 512],
                                    start=(ib == 0), stop=(ib == 7))
                            sT.inc(mm)
                            t_wrote(bg)
                            M[f"{pref}g_{s}_{oc}_{ti}"] = ("T", sT.v)
                            t_acquire(E, bu)
                            for ib in range(8):
                                mm = E.matmul(
                                    out=pb[bu][:],
                                    lhsT=xqT[:, ti * D + ib * 128:
                                             ti * D + ib * 128 + 128],
                                    rhs=wbu[:, half * INNER + ib * 512:
                                            half * INNER + ib * 512 + 512],
                                    start=(ib == 0), stop=(ib == 7))
                            sT.inc(mm)
                            t_wrote(bu)
                            M[f"{pref}u_{s}_{oc}_{ti}"] = ("T", sT.v)
                        M[f"{pref}gu_done_{s}_{oc}"] = ("T", sT.v)

                @body("scalar")
                def _(E, s=s):
                    for oc in range(8):
                        for ti in range(ST):
                            p = s * ST + ti
                            bg = (ti % 2) * 2
                            wt(E, M[f"{pref}g_{s}_{oc}_{ti}"])
                            if (dbg_d is not None and fid == 1 and s == 0
                                    and oc == 0 and ti == 1):
                                wt(E, M["dbg_gate_done"])
                            if oc > 0 or ti > 0:
                                pti = ti - 1 if ti > 0 else ST - 1
                                poc = oc if ti > 0 else oc - 1
                                wt(E, M[f"{pref}hcmb_{s}_{poc}_{pti}"])
                            sS.inc(E.activation(out=gate[:], in_=pb[bg][:],
                                                func=ACTF.Silu,
                                                scale=ag_sb[:, p:p + 1]))
                            r_done(bg, "S")
                            M[f"{pref}sil_{s}_{oc}_{ti}"] = ("S", sS.v)

                @body("vector")
                def _(E, s=s):
                    for oc in range(8):
                        for ti in range(ST):
                            p = s * ST + ti
                            bu = (ti % 2) * 2 + 1
                            wt(E, M[f"{pref}sil_{s}_{oc}_{ti}"])
                            wt(E, M[f"{pref}u_{s}_{oc}_{ti}"])
                            sV.inc(E.scalar_tensor_tensor(
                                out=h_sb[:, ti * INNER + oc * 512:
                                         ti * INNER + oc * 512 + 512],
                                in0=pb[bu][:], scalar=au_sb[:, p:p + 1],
                                in1=gate[:], op0=ALU.mult, op1=ALU.mult))
                            r_done(bu, "V")
                            M[f"{pref}hcmb_{s}_{oc}_{ti}"] = ("V", sV.v)

                for ti in range(ST):
                    @body("vector")
                    def _(E, s=s, ti=ti):
                        p = s * ST + ti
                        wt(E, M[f"{pref}hcmb_{s}_7_{ti}"])
                        if (dbg_d is not None and fid == 1 and s == 0
                                and ti == 0):
                            wt(E, M["dbg_h_done"])
                        ht = h_sb[:, ti * INNER:(ti + 1) * INNER]
                        E.reduce_max(out=col[:, 0:1], in_=ht, axis=AX,
                                     apply_absolute_value=True)
                        jnk(E)
                        E.reciprocal(out=col[:, 1:2], in_=col[:, 0:1])
                        E.tensor_scalar(out=adh_sb[:, p:p + 1],
                                        in0=col[:, 0:1],
                                        scalar1=mclip[:, 2:3],
                                        scalar2=0.5 / 127.0,
                                        op0=ALU.mult, op1=ALU.mult)
                        E.tensor_scalar(out=col[:, 1:2], in0=col[:, 1:2],
                                        scalar1=127.0, scalar2=1.27e7,
                                        op0=ALU.mult, op1=ALU.min)
                        jnk(E)
                        if ti > 0:
                            wt(E, M[f"{pref}hqf_{s}_{ti - 1}"])
                        elif s > 0:
                            wt(E, M[f"{pref}hqf_{s - 1}_{ST - 1}"])
                        if (dbg_d is not None and fid == 1 and s == 0
                                and ti == 1):
                            wt(E, M["dbg_hq_done"])
                        for cc in range(8):
                            E.tensor_scalar(
                                out=ht[:, cc * 512:(cc + 1) * 512],
                                in0=ht[:, cc * 512:(cc + 1) * 512],
                                scalar1=col[:, 1:2], scalar2=None,
                                op0=ALU.mult)
                            if cc >= 1:
                                E.tensor_scalar(
                                    out=hq_bf[:, (cc - 1) * 512:cc * 512],
                                    in0=ht[:, (cc - 1) * 512:cc * 512],
                                    scalar1=MROUND, scalar2=MROUND,
                                    op0=ALU.add, op1=ALU.subtract)
                        sV.inc(E.tensor_scalar(out=hq_bf[:, 7 * 512:8 * 512],
                                               in0=ht[:, 7 * 512:8 * 512],
                                               scalar1=MROUND, scalar2=MROUND,
                                               op0=ALU.add, op1=ALU.subtract))
                        M[f"{pref}hq_{s}_{ti}"] = ("V", sV.v)
                        if (dbg_d is not None and fid == 1 and s == 0
                                and ti == 0):
                            E.tensor_copy(out=dbg_buf[:, 0:D],
                                          in_=hq_bf[:, 0:D])
                            E.tensor_copy(out=dbgc[:, 0:2], in_=col[:, 0:2])
                            sV.inc(E.tensor_copy(out=dbgc[:, 2:3],
                                                 in_=adh_sb[:, 0:1]))
                            M["dbghq"] = ("V", sV.v)

                    @body("tensor")
                    def _(E, s=s, ti=ti):
                        wt(E, M[f"{pref}hq_{s}_{ti}"])
                        for gfl in range(8):
                            b = 4 + (gfl % 2)
                            if gfl >= 2:
                                wt(E, M[f"{pref}hqTe_{s}_{ti}_{gfl - 2}"])
                            elif ti > 0:
                                wt(E, M[f"{pref}hqTe_{s}_{ti - 1}_{6 + gfl}"])
                            else:
                                wt(E, M[f"{pref}xqTe_{s * ST + ST - 1}_{gfl}"])
                            for j in range(4):
                                mm = E.matmul(
                                    out=pb[b][:, j * 128:(j + 1) * 128],
                                    lhsT=hq_bf[:, gfl * 512 + j * 128:
                                               gfl * 512 + j * 128 + 128],
                                    rhs=ident_sb[:], start=True, stop=True)
                            sT.inc(mm)
                            t_wrote(b)
                            M[f"{pref}hqfg_{s}_{ti}_{gfl}"] = ("T", sT.v)
                        M[f"{pref}hqf_{s}_{ti}"] = ("T", sT.v)

                    @body("vector")
                    def _(E, s=s, ti=ti):
                        for gfl in range(8):
                            b = 4 + (gfl % 2)
                            wt(E, M[f"{pref}hqfg_{s}_{ti}_{gfl}"])
                            sV.inc(E.tensor_copy(
                                out=hqT[:, ti * INNER + gfl * 512:
                                        ti * INNER + gfl * 512 + 512],
                                in_=pb[b][:]))
                            M[f"{pref}hqTe_{s}_{ti}_{gfl}"] = ("V", sV.v)
                        M[f"{pref}hqT_{s}_{ti}"] = ("V", sV.v)

                @body("tensor")
                def _(E, s=s):
                    for ti in range(ST):
                        wt(E, M[f"{pref}hqT_{s}_{ti}"])
                    for b in range(8):
                        t_acquire(E, b)
                    for ib in range(32):
                        wt(E, M[f"{pref}wdl_{s}_{ib}"])
                        sl = ib % 4
                        for ti in range(ST):
                            for og in range(2):
                                mm = E.matmul(
                                    out=pb[ti * 2 + og][:],
                                    lhsT=hqT[:, ti * INNER + ib * 128:
                                             ti * INNER + ib * 128 + 128],
                                    rhs=wbd[:, sl * D + og * 512:
                                            sl * D + og * 512 + 512],
                                    start=(ib == 0), stop=(ib == 31))
                        sT.inc(mm)
                        M[f"{pref}dn_{s}_{ib}"] = ("T", sT.v)
                    for b in range(8):
                        t_wrote(b)
                    M[f"{pref}dn_done_{s}"] = ("T", sT.v)

                @body("vector")
                def _(E, s=s, xoff=xoff):
                    wt(E, M[f"{pref}dn_done_{s}"])
                    for ti in range(ST):
                        p = s * ST + ti
                        for og in range(2):
                            xo = xoff + ti * D + og * 512
                            sV.inc(E.scalar_tensor_tensor(
                                out=x_st[:, xo:xo + 512],
                                in0=pb[ti * 2 + og][:],
                                scalar=adh_sb[:, p:p + 1],
                                in1=x_st[:, xo:xo + 512],
                                op0=ALU.mult, op1=ALU.add))
                            r_done(ti * 2 + og, "V")
                    M[f"{pref}res_{s}"] = ("V", sV.v)

                @body("gpsimd")
                def _(E, s=s, xoff=xoff):
                    wt(E, M[f"{pref}res_{s}"])
                    M[f"{pref}xst_{s}"] = dma(
                        E, "fstA" if s % 2 == 0 else "fstB",
                        dst_dram[s * ST * 128:(s + 1) * ST * 128, :],
                        x_st[:, xoff:xoff + ST * D])

            barrier()
            sbt_free(f"{pref}x", f"{pref}h", f"{pref}hqT", f"{pref}xqT",
                     f"{pref}wbg", f"{pref}wbu", f"{pref}wbd", f"{pref}scr",
                     f"{pref}hqb", f"{pref}xqb", f"{pref}gate")

        if "ffn1" in phases:
            dst1 = x1sp if len(phases) > 2 else yout
            emit_ffn(1, xin, dst1)

        need_ao = ("qlat" in phases)
        if need_ao:
            attnout = sbt("attnout", [128, NT * D], BF16)
            q_sb = sbt("q_sb", [128, NT * D], BF16)

            qtab_c = sbt("qtab_c", [128, NT * 512], BF16)
            qtab_s = sbt("qtab_s", [128, NT * 512], BF16)
            wq_sb = sbt("wq_sb", [128, 8 * D], BF16)
            wdkv_sb = sbt("wdkv_sb", [128, 8 * LAT], BF16)
            x1b = sbt("x1b", [128, 2 * D], F32)
            xa_bf = sbt("xa_bf", [128, D], BF16)
            xaT = sbt("xaT", [128, D], BF16)
            ropes = sbt("ropes", [128, 4 * 512], BF16)
            lat_f = sbt("lat_f", [128, LAT], F32)
            lat_bf = sbt("lat_bf", [128, LAT], BF16)
            scr3 = sbt("scr3", [128, D], F32)

            @body("gpsimd")
            def _(E):
                dma(E, "cP3w", wq_sb[:].rearrange("p (i c) -> p i c", i=8),
                    wq_d.rearrange("(i p) c -> p i c", p=128))
                M["p3w"] = dma(
                    E, "cP3w", wdkv_sb[:].rearrange("p (i c) -> p i c", i=8),
                    wdkv_d.rearrange("(i p) c -> p i c", p=128))

            @body("sync")
            def _(E):
                dma(E, "cQT", qtab_c[:].rearrange("r (p c) -> r p c", p=NT),
                    qcos_d.rearrange("(p r) c -> r p c", r=128))
                M["p3tab"] = dma(
                    E, "cQT", qtab_s[:].rearrange("r (p c) -> r p c", p=NT),
                    qsin_d.rearrange("(p r) c -> r p c", r=128))
                for p in range(NT):
                    if p >= 2:
                        wt(E, M[f"p3xa_{p - 2}"])
                        wt(E, M[f"p3sq_{p - 2}"])
                    M[f"p3xld_{p}"] = dma(
                        E, "p3A" if p % 2 == 0 else "p3B",
                        x1b[:, (p % 2) * D:(p % 2 + 1) * D],
                        x1sp[p * 128:(p + 1) * 128, :])

            @body("vector")
            def _(E):
                wt(E, M["p3tab"])
                wt(E, M["p0"])
                for p in range(NT):
                    cs = (p % 2) * 16
                    wt(E, M[f"p3xld_{p}"])
                    xt = x1b[:, (p % 2) * D:(p % 2 + 1) * D]
                    E.reduce_sum(out=col[:, cs:cs + 1], in_=xt, axis=AX)
                    jnk(E)
                    sV.inc(E.tensor_scalar(out=col[:, cs + 1:cs + 2],
                                           in0=col[:, cs:cs + 1],
                                           scalar1=-1.0 / D, scalar2=None,
                                           op0=ALU.mult))
                    M[f"p3mneg_{p}"] = ("V", sV.v)
                    wt(E, M[f"p3sq_{p}"])
                    sV.inc(E.tensor_scalar(out=col[:, cs + 2:cs + 3],
                                           in0=col[:, cs + 3:cs + 4],
                                           scalar1=1.0 / D, scalar2=LN_EPS,
                                           op0=ALU.mult, op1=ALU.add))
                    M[f"p3var_{p}"] = ("V", sV.v)
                    wt(E, M[f"p3sqrt_{p}"])
                    E.reciprocal(out=col[:, cs + 7:cs + 8],
                                 in_=col[:, cs + 5:cs + 6])
                    jnk(E)
                    jnk(E)
                    if p > 0:
                        wt(E, M[f"p3lat_{p - 1}"])
                    sV.inc(E.tensor_scalar(out=xa_bf[:], in0=xt,
                                           scalar1=col[:, cs + 1:cs + 2],
                                           scalar2=col[:, cs + 7:cs + 8],
                                           op0=ALU.add, op1=ALU.mult))
                    M[f"p3xa_{p}"] = ("V", sV.v)
                    for gfl in range(2):
                        b = 4 + gfl
                        wt(E, M[f"p3xafg_{p}_{gfl}"])
                        sV.inc(E.tensor_copy(
                            out=xaT[:, gfl * 512:gfl * 512 + 512],
                            in_=pb[b][:]))
                        r_done(b, "V")
                    M[f"p3xaT_{p}"] = ("V", sV.v)
                    for oc2 in range(2):
                        wt(E, M[f"p3q_{p}_{oc2}"])
                        sV.inc(E.tensor_copy(
                            out=q_sb[:, p * D + oc2 * 512:
                                     p * D + oc2 * 512 + 512],
                            in_=pb[oc2][:]))
                        r_done(oc2, "V")
                    vdrain(E)
                    qt = q_sb[:, p * D:(p + 1) * D]
                    q3 = qt.rearrange("p (h d) -> p h d", h=16)
                    q1 = q3[:, :, 0:32]
                    q2 = q3[:, :, 32:64]
                    tc = qtab_c[:, p * 512:(p + 1) * 512]
                    ts2 = qtab_s[:, p * 512:(p + 1) * 512]
                    Av = ropes[:, 0:512]
                    Bv = ropes[:, 512:1024]
                    Cv = ropes[:, 1024:1536]
                    Dv = ropes[:, 1536:2048]
                    E.tensor_mul(out=Av, in0=q1, in1=tc)
                    E.tensor_mul(out=Bv, in0=q2, in1=ts2)
                    E.tensor_mul(out=Cv, in0=q1, in1=ts2)
                    E.tensor_mul(out=Dv, in0=q2, in1=tc)
                    E.tensor_sub(out=q1, in0=Av, in1=Bv)
                    sV.inc(E.tensor_add(out=q2, in0=Cv, in1=Dv))
                    M[f"p3rope_{p}"] = ("V", sV.v)
                    wt(E, M[f"p3lat_{p}"])
                    E.reduce_sum(out=col[:, cs + 12:cs + 13],
                                 in_=pb[2][:, 0:LAT], axis=AX)
                    jnk(E)
                    sV.inc(E.tensor_scalar(out=col[:, cs + 13:cs + 14],
                                           in0=col[:, cs + 12:cs + 13],
                                           scalar1=-1.0 / LAT, scalar2=None,
                                           op0=ALU.mult))
                    M[f"p3lm_{p}"] = ("V", sV.v)
                    wt(E, M[f"p3lsq_{p}"])
                    sV.inc(E.tensor_scalar(out=col[:, cs + 14:cs + 15],
                                           in0=col[:, cs + 15:cs + 16],
                                           scalar1=1.0 / LAT, scalar2=LN_EPS,
                                           op0=ALU.mult, op1=ALU.add))
                    M[f"p3lv_{p}"] = ("V", sV.v)
                    wt(E, M[f"p3lsqrt_{p}"])
                    E.reciprocal(out=col[:, cs + 12:cs + 13],
                                 in_=col[:, cs + 14:cs + 15])
                    jnk(E)
                    jnk(E)
                    E.tensor_scalar(out=lat_f[:], in0=pb[2][:, 0:LAT],
                                    scalar1=col[:, cs + 13:cs + 14],
                                    scalar2=col[:, cs + 12:cs + 13],
                                    op0=ALU.add, op1=ALU.mult)
                    r_done(2, "V")
                    E.tensor_mul(out=lat_f[:], in0=lat_f[:], in1=gam_sb[:])
                    if p > 0:
                        wt(E, M[f"p3latst_{p - 1}"])
                    sV.inc(E.tensor_add(out=lat_bf[:], in0=lat_f[:],
                                        in1=bet_sb[:]))
                    M[f"p3latbf_{p}"] = ("V", sV.v)

            @body("scalar")
            def _(E):
                for p in range(NT):
                    cs = (p % 2) * 16
                    xt = x1b[:, (p % 2) * D:(p % 2 + 1) * D]
                    wt(E, M[f"p3mneg_{p}"])
                    sS.inc(E.activation(out=scr3[:, 0:D], in_=xt,
                                        func=ACTF.Square,
                                        bias=col[:, cs + 1:cs + 2], scale=1.0,
                                        accum_out=col[:, cs + 3:cs + 4]))
                    M[f"p3sq_{p}"] = ("S", sS.v)
                    wt(E, M[f"p3var_{p}"])
                    sS.inc(E.activation(out=col[:, cs + 5:cs + 6],
                                        in_=col[:, cs + 2:cs + 3],
                                        func=ACTF.Sqrt))
                    M[f"p3sqrt_{p}"] = ("S", sS.v)
                    wt(E, M[f"p3lm_{p}"])
                    sS.inc(E.activation(out=scr3[:, 0:LAT], in_=pb[2][:, 0:LAT],
                                        func=ACTF.Square,
                                        bias=col[:, cs + 13:cs + 14], scale=1.0,
                                        accum_out=col[:, cs + 15:cs + 16]))
                    r_done(2, "S")
                    M[f"p3lsq_{p}"] = ("S", sS.v)
                    wt(E, M[f"p3lv_{p}"])
                    sS.inc(E.activation(out=col[:, cs + 14:cs + 15],
                                        in_=col[:, cs + 14:cs + 15],
                                        func=ACTF.Sqrt))
                    M[f"p3lsqrt_{p}"] = ("S", sS.v)

            @body("tensor")
            def _(E):
                wt(E, M["p3w"])
                for p in range(NT):
                    wt(E, M[f"p3xa_{p}"])
                    for gfl in range(2):
                        b = 4 + gfl
                        t_acquire(E, b)
                        for j in range(4):
                            mm = E.matmul(
                                out=pb[b][:, j * 128:(j + 1) * 128],
                                lhsT=xa_bf[:, gfl * 512 + j * 128:
                                           gfl * 512 + j * 128 + 128],
                                rhs=ident_sb[:], start=True, stop=True)
                        sT.inc(mm)
                        t_wrote(b)
                        M[f"p3xafg_{p}_{gfl}"] = ("T", sT.v)
                    wt(E, M[f"p3xaT_{p}"])
                    for oc2 in range(2):
                        t_acquire(E, oc2)
                        for ib in range(8):
                            mm = E.matmul(
                                out=pb[oc2][:],
                                lhsT=xaT[:, ib * 128:(ib + 1) * 128],
                                rhs=wq_sb[:, ib * D + oc2 * 512:
                                         ib * D + oc2 * 512 + 512],
                                start=(ib == 0), stop=(ib == 7))
                        sT.inc(mm)
                        t_wrote(oc2)
                        M[f"p3q_{p}_{oc2}"] = ("T", sT.v)
                    t_acquire(E, 2)
                    for ib in range(8):
                        mm = E.matmul(out=pb[2][:, 0:LAT],
                                      lhsT=xaT[:, ib * 128:(ib + 1) * 128],
                                      rhs=wdkv_sb[:, ib * LAT:(ib + 1) * LAT],
                                      start=(ib == 0), stop=(ib == 7))
                    sT.inc(mm)
                    t_wrote(2)
                    M[f"p3lat_{p}"] = ("T", sT.v)

            @body("gpsimd")
            def _(E):
                wt(E, M["p0"])
                for p in range(NT):
                    wt(E, M[f"p3latbf_{p}"])
                    M[f"p3latst_{p}"] = dma(
                        E, "lst", lat_pv[p * 128:(p + 1) * 128, :], lat_bf[:])
                M["p3done"] = M[f"p3latst_{NT - 1}"]

            barrier()
            sbt_free("qtab_c", "qtab_s", "wq_sb", "wdkv_sb", "x1b", "xa_bf",
                     "xaT", "ropes", "lat_f", "lat_bf", "scr3")

        if "attn" in phases:
            latT = sbt("latT", [128, 2 * T], BF16)
            wuk_sb = sbt("wuk_sb", [128, 2 * D], BF16)
            wuv_sb = sbt("wuv_sb", [128, 2 * D], BF16)
            kT_sb = sbt("kT_sb", [128, 4 * T], BF16)
            v_sb = sbt("v_sb", [128, 32 * 520], BF16)
            ktab = sbt("ktab", [128, 2 * 512], BF16)
            k_tok = sbt("k_tok", [128, 512], BF16)
            qT_sb = sbt("qT_sb", [128, 512], BF16)
            pT_sb = sbt("pT_sb", [128, 2048], BF16)
            ropes4 = sbt("ropes4", [128, 1024], BF16)

            @body("gpsimd")
            def _(E):
                with ExitStack() as rs:
                    r = rs.enter_context
                    goff = r(E.register("goff"))
                    wfo = r(E.register("wfo"))
                    rfo = r(E.register("rfo"))
                    m0r = r(E.register("m0r"))
                    m1r = r(E.register("m1r"))
                    f0 = r(E.register("f0"))
                    f1 = r(E.register("f1"))
                    e0 = r(E.register("e0"))
                    e1 = r(E.register("e1"))
                    more = r(E.register("more"))
                    cnt = r(E.register("cnt"))
                    wt(E, M["p3done"])
                    E.reg_load(goff, offs_sb[0:1, 0:1])
                    E.reg_load(wfo, offs_sb[0:1, 1:2])
                    E.reg_load(rfo, offs_sb[0:1, 2:3])
                    E.reg_load(m0r, magic_sb[0:1, 0:1])
                    E.reg_load(m1r, magic_sb[0:1, 1:2])
                    M["latgat"] = dma(
                        E, "lst",
                        RAP(lat_sh.tensor, goff,
                            [[32768, 32], [LAT, 64], [1, LAT]]),
                        lat_pv.rearrange("(lg i) c -> lg i c", i=64))
                    wt(E, M["latgat"])
                    dma(E, "flg", RAP(flag_sh.tensor, wfo, [[2, 1], [1, 2]]),
                        magic_sb[:])
                    E.reg_mov(more, 1)
                    E.reg_mov(cnt, MAXPOLL)
                    with E.While(more):
                        E.reg_load((f0, f1),
                                   RAP(flag_sh.tensor, rfo, [[2, 1], [1, 2]]))
                        E.reg_alu(e0, f0, m0r, ALU.is_equal)
                        E.reg_alu(e1, f1, m1r, ALU.is_equal)
                        E.reg_alu(e0, e0, e1, ALU.bitwise_and)
                        E.reg_alu(more, e0, 1, ALU.bitwise_xor)
                        E.reg_add(cnt, cnt, -1)
                        E.reg_alu(e1, cnt, 0, ALU.is_gt)
                        E.reg_alu(more, more, e1, ALU.bitwise_and)
                    sG.inc(E.nop())
                    M["xchg"] = ("G", sG.v)
                dma(E, "cP4w", wuk_sb[:].rearrange("p (i c) -> p i c", i=2),
                    wuk_d.rearrange("(i p) c -> p i c", p=128))
                M["p4w"] = dma(
                    E, "cP4w", wuv_sb[:].rearrange("p (i c) -> p i c", i=2),
                    wuv_d.rearrange("(i p) c -> p i c", p=128))
                sG.inc(E.memset(
                    RAP(v_sb, 64, [[520 * 32, 128], [520, 32], [65, 8]]), 1.0))
                M["vones"] = ("G", sG.v)

            @body("sync")
            def _(E):
                wt(E, M["xchg"])
                dma(E, "cLT", latT[:, 0:T], lat_sh[:, 0:128], transpose=True)
                M["latT"] = dma(E, "cLT", latT[:, T:2 * T], lat_sh[:, 128:256],
                                transpose=True)

            pt_guard = {0: ("none", 0), 1: ("none", 0)}
            qT_guard = ("none", 0)
            av_guard = {6: ("none", 0), 7: ("none", 0)}
            sc_guard = {0: ("none", 0), 1: ("none", 0)}
            scpair = [pb03[:, 0:1024], pb45[:, 0:1024]]
            bctr = [0, 0, 0]

            def batches_of(p):
                nkb = 2 * p + 2
                return nkb, [(b, list(range(8 * b, min(8 * b + 8, nkb))))
                             for b in range((nkb + 7) // 8)]

            for half in range(2):
                @body("sync")
                def _(E, half=half):
                    for kb in range(32):
                        if kb >= 2:
                            wt(E, M[f"krope_{half}_{kb - 2}"])
                        elif half > 0:
                            wt(E, M[f"krope_{half - 1}_{30 + kb}"])
                        sl = "ktA" if kb % 2 == 0 else "ktB"
                        dma(E, sl,
                            ktab[:, (kb % 2) * 512:(kb % 2) * 512 + 256],
                            kcos_d[kb * 128:(kb + 1) * 128, :])
                        M[f"ktab_{half}_{kb}"] = dma(
                            E, sl,
                            ktab[:, (kb % 2) * 512 + 256:(kb % 2) * 512 + 512],
                            ksin_d[kb * 128:(kb + 1) * 128, :])

                @body("tensor")
                def _(E, half=half):
                    if half == 0:
                        wt(E, M["latT"])
                        wt(E, M["p4w"])
                    else:
                        wt(E, sc_guard[0])
                        wt(E, sc_guard[1])
                    for kb in range(32):
                        t_acquire(E, 0)
                        for c in range(2):
                            mm = E.matmul(
                                out=pb[0][:],
                                lhsT=latT[:, c * T + kb * 128:
                                         c * T + kb * 128 + 128],
                                rhs=wuk_sb[:, c * D + half * 512:
                                         c * D + half * 512 + 512],
                                start=(c == 0), stop=(c == 1))
                        sT.inc(mm)
                        t_wrote(0)
                        M[f"kmm_{half}_{kb}"] = ("T", sT.v)
                        if kb > 0:
                            wt(E, M[f"vev_{half}_{kb - 1}"])
                        elif half > 0:
                            wt(E, M[f"vev_{half - 1}_31"])
                        t_acquire(E, 1)
                        for c in range(2):
                            mm = E.matmul(
                                out=pb[1][:],
                                lhsT=latT[:, c * T + kb * 128:
                                         c * T + kb * 128 + 128],
                                rhs=wuv_sb[:, c * D + half * 512:
                                         c * D + half * 512 + 512],
                                start=(c == 0), stop=(c == 1))
                        sT.inc(mm)
                        t_wrote(1)
                        M[f"vmm_{half}_{kb}"] = ("T", sT.v)
                        wt(E, M[f"krope_{half}_{kb}"])
                        t_acquire(E, 2)
                        for j in range(4):
                            mm = E.matmul(out=pb[2][:, j * 128:(j + 1) * 128],
                                          lhsT=k_tok[:, j * 128:(j + 1) * 128],
                                          rhs=ident_sb[:], start=True,
                                          stop=True)
                        sT.inc(mm)
                        t_wrote(2)
                        M[f"kflip_{half}_{kb}"] = ("T", sT.v)

                @body("scalar")
                def _(E, half=half):
                    if half == 0:
                        wt(E, M["vones"])
                    for kb in range(32):
                        wt(E, M[f"vmm_{half}_{kb}"])
                        vdst = RAP(v_sb, kb * 520,
                                   [[520 * 32, 128], [65, 8], [1, 64]])
                        sS.inc(E.copy(out=vdst, in_=pb[1][:]))
                        r_done(1, "S")
                        M[f"vev_{half}_{kb}"] = ("S", sS.v)

                @body("vector")
                def _(E, half=half):
                    for kb in range(32):
                        wt(E, M[f"kmm_{half}_{kb}"])
                        wt(E, M[f"ktab_{half}_{kb}"])
                        if kb > 0:
                            wt(E, M[f"kflip_{half}_{kb - 1}"])
                        elif half > 0:
                            wt(E, M[f"kflip_{half - 1}_31"])
                        sV.inc(E.tensor_copy(out=k_tok[:], in_=pb[0][:]))
                        r_done(0, "V")
                        vdrain(E)
                        kt3 = k_tok[:].rearrange("p (h d) -> p h d", h=8)
                        k1 = kt3[:, :, 0:32]
                        k2 = kt3[:, :, 32:64]
                        tc = ktab[:, (kb % 2) * 512:(kb % 2) * 512 + 256]
                        ts2 = ktab[:, (kb % 2) * 512 + 256:(kb % 2) * 512 + 512]
                        Av = ropes4[:, 0:256]
                        Bv = ropes4[:, 256:512]
                        Cv = ropes4[:, 512:768]
                        Dv = ropes4[:, 768:1024]
                        E.tensor_mul(out=Av, in0=k1, in1=tc)
                        E.tensor_mul(out=Bv, in0=k2, in1=ts2)
                        E.tensor_mul(out=Cv, in0=k1, in1=ts2)
                        E.tensor_mul(out=Dv, in0=k2, in1=tc)
                        E.tensor_sub(out=k1, in0=Av, in1=Bv)
                        sV.inc(E.tensor_add(out=k2, in0=Cv, in1=Dv))
                        M[f"krope_{half}_{kb}"] = ("V", sV.v)
                        wt(E, M[f"kflip_{half}_{kb}"])
                        kdst = RAP(kT_sb, kb * 128,
                                   [[4 * T, 128], [T, 4], [1, 128]])
                        sV.inc(E.tensor_copy(out=kdst, in_=pb[2][:]))
                        r_done(2, "V")
                        M[f"kTev_{half}_{kb}"] = ("V", sV.v)

                @body("tensor")
                def _(E, half=half):
                    wt(E, M[f"kTev_{half}_31"])
                    wt(E, M[f"vev_{half}_31"])
                    for p in range(NT):
                        t_acquire(E, 3)
                        for j in range(4):
                            jj = 4 * half + j
                            mm = E.matmul(
                                out=pb[3][:, j * 128:(j + 1) * 128],
                                lhsT=q_sb[:, p * D + jj * 128:
                                         p * D + jj * 128 + 128],
                                rhs=ident_sb[:], start=True, stop=True)
                        sT.inc(mm)
                        t_wrote(3)
                        M[f"qTf_{half}_{p}"] = ("T", sT.v)
                        wt(E, M[f"qTe_{half}_{p}"])
                        nkb, bts = batches_of(p)
                        for hw in range(8):
                            jj, hi = hw // 2, hw % 2
                            bav = 6 + (hw % 2)
                            wt(E, av_guard[bav])

                            def av_emit(b, pr, kbs):
                                wt(E, M[f"pTrdy_{half}_{p}_{hw}_{b}"])
                                for i, kb in enumerate(kbs):
                                    mm = E.matmul(
                                        out=pb[bav][:, 0:65],
                                        lhsT=pT_sb[:, pr * 1024 + i * 128:
                                                   pr * 1024 + i * 128 + 128],
                                        rhs=v_sb[:, kb * 520 + hw * 65:
                                                 kb * 520 + hw * 65 + 65],
                                        start=(kb == 0),
                                        stop=(kb == nkb - 1))
                                sT.inc(mm)
                                M[f"avg_{half}_{p}_{hw}_{b}"] = ("T", sT.v)

                            prev = None
                            for b, kbs in bts:
                                pr = bctr[0] % 2
                                bctr[0] += 1
                                wt(E, sc_guard[pr])
                                for i, kb in enumerate(kbs):
                                    mm = E.matmul(
                                        out=scpair[pr][:, i * 128:
                                                       (i + 1) * 128],
                                        lhsT=kT_sb[64 * hi:64 * hi + 64,
                                                   jj * T + kb * 128:
                                                   jj * T + kb * 128 + 128],
                                        rhs=qT_sb[64 * hi:64 * hi + 64,
                                                  jj * 128:jj * 128 + 128],
                                        start=True, stop=True)
                                sT.inc(mm)
                                M[f"sc_{half}_{p}_{hw}_{b}"] = ("T", sT.v)
                                sc_guard[pr] = M[f"exp_{half}_{p}_{hw}_{b}"]
                                if prev is not None:
                                    av_emit(*prev)
                                prev = (b, pr, kbs)
                            av_emit(*prev)
                            t_wrote(bav)
                            M[f"av_{half}_{p}_{hw}"] = ("T", sT.v)
                            av_guard[bav] = M[f"norm_{half}_{p}_{hw}"]
                        M[f"scl_{half}_{p}"] = ("T", sT.v)

                @body("scalar")
                def _(E, half=half):
                    for p in range(NT):
                        nkb, bts = batches_of(p)
                        for hw in range(8):
                            for b, kbs in bts:
                                pr = bctr[1] % 2
                                wt(E, M[f"sc_{half}_{p}_{hw}_{b}"])
                                wt(E, pt_guard[pr])
                                sS.inc(E.activation(
                                    out=pT_sb[:, pr * 1024:
                                              pr * 1024 + len(kbs) * 128],
                                    in_=scpair[pr][:, 0:len(kbs) * 128],
                                    func=ACTF.Exp))
                                M[f"exp_{half}_{p}_{hw}_{b}"] = ("S", sS.v)
                                pt_guard[pr] = M[f"avg_{half}_{p}_{hw}_{b}"]
                                bctr[1] += 1

                @body("vector")
                def _(E, half=half):
                    nonlocal qT_guard
                    for p in range(NT):
                        wt(E, M[f"qTf_{half}_{p}"])
                        wt(E, qT_guard)
                        sV.inc(E.tensor_copy(out=qT_sb[:], in_=pb[3][:]))
                        r_done(3, "V")
                        M[f"qTe_{half}_{p}"] = ("V", sV.v)
                        qT_guard = M[f"scl_{half}_{p}"]
                        nkb, bts = batches_of(p)
                        for hw in range(8):
                            for b, kbs in bts:
                                pr = bctr[2] % 2
                                bctr[2] += 1
                                dia = [i for i, kb in enumerate(kbs)
                                       if kb in (2 * p, 2 * p + 1)]
                                if dia:
                                    wt(E, M[f"exp_{half}_{p}_{hw}_{b}"])
                                    for i in dia:
                                        kb = kbs[i]
                                        mi = kb - 2 * p
                                        sl = pT_sb[:, pr * 1024 + i * 128:
                                                   pr * 1024 + i * 128 + 128]
                                        mm = E.tensor_mul(
                                            out=sl, in0=sl,
                                            in1=masks_sb[:, mi * 128:
                                                         (mi + 1) * 128])
                                    sV.inc(mm)
                                    M[f"pTrdy_{half}_{p}_{hw}_{b}"] = ("V", sV.v)
                                else:
                                    M[f"pTrdy_{half}_{p}_{hw}_{b}"] = \
                                        M[f"exp_{half}_{p}_{hw}_{b}"]
                            bav = 6 + (hw % 2)
                            wt(E, M[f"av_{half}_{p}_{hw}"])
                            E.reciprocal(out=col[:, 20 + hw:21 + hw],
                                         in_=pb[bav][:, 64:65])
                            if hw > 0:
                                pv = 6 + ((hw - 1) % 2)
                                h_abs = 8 * half + hw - 1
                                sV.inc(E.tensor_scalar(
                                    out=attnout[:, p * D + h_abs * 64:
                                                p * D + h_abs * 64 + 64],
                                    in0=pb[pv][:, 0:64],
                                    scalar1=col[:, 19 + hw:20 + hw],
                                    scalar2=None, op0=ALU.mult))
                                r_done(pv, "V")
                                M[f"norm_{half}_{p}_{hw - 1}"] = ("V", sV.v)
                        jnk(E)
                        h_abs = 8 * half + 7
                        sV.inc(E.tensor_scalar(
                            out=attnout[:, p * D + h_abs * 64:
                                        p * D + h_abs * 64 + 64],
                            in0=pb[7][:, 0:64], scalar1=col[:, 27:28],
                            scalar2=None, op0=ALU.mult))
                        r_done(7, "V")
                        M[f"norm_{half}_{p}_7"] = ("V", sV.v)

            barrier()
            sbt_free("latT", "wuk_sb", "wuv_sb", "kT_sb", "v_sb", "ktab",
                     "k_tok", "qT_sb", "pT_sb", "ropes4")

        if "wo" in phases:
            wo_sb = sbt("wo_sb", [128, 8 * D], BF16)
            x1b5 = sbt("x1b5", [128, 2 * D], F32)
            x2b5 = sbt("x2b5", [128, 2 * D], F32)
            aoT = sbt("aoT", [128, D], BF16)

            @body("gpsimd")
            def _(E):
                M["p5w"] = dma(
                    E, "cP5w", wo_sb[:].rearrange("p (i c) -> p i c", i=8),
                    wo_d.rearrange("(i p) c -> p i c", p=128))

            @body("sync")
            def _(E):
                for p in range(NT):
                    if p >= 2:
                        wt(E, M[f"p5res_{p - 2}"])
                    M[f"p5xld_{p}"] = dma(
                        E, "p3A" if p % 2 == 0 else "p3B",
                        x1b5[:, (p % 2) * D:(p % 2 + 1) * D],
                        x1sp[p * 128:(p + 1) * 128, :])

            @body("tensor")
            def _(E):
                wt(E, M["p5w"])
                for p in range(NT):
                    wt(E, M[f"norm_1_{p}_7"])
                    for gfl in range(2):
                        b = 4 + gfl
                        t_acquire(E, b)
                        for j in range(4):
                            mm = E.matmul(
                                out=pb[b][:, j * 128:(j + 1) * 128],
                                lhsT=attnout[:, p * D + gfl * 512 + j * 128:
                                            p * D + gfl * 512 + j * 128 + 128],
                                rhs=ident_sb[:], start=True, stop=True)
                        sT.inc(mm)
                        t_wrote(b)
                        M[f"p5aof_{p}_{gfl}"] = ("T", sT.v)
                    wt(E, M[f"p5aoT_{p}"])
                    for oc2 in range(2):
                        t_acquire(E, oc2)
                        for ib in range(8):
                            mm = E.matmul(
                                out=pb[oc2][:],
                                lhsT=aoT[:, ib * 128:(ib + 1) * 128],
                                rhs=wo_sb[:, ib * D + oc2 * 512:
                                         ib * D + oc2 * 512 + 512],
                                start=(ib == 0), stop=(ib == 7))
                        sT.inc(mm)
                        t_wrote(oc2)
                        M[f"p5mm_{p}_{oc2}"] = ("T", sT.v)

            @body("vector")
            def _(E):
                for p in range(NT):
                    for gfl in range(2):
                        b = 4 + gfl
                        wt(E, M[f"p5aof_{p}_{gfl}"])
                        sV.inc(E.tensor_copy(out=aoT[:, gfl * 512:gfl * 512 + 512],
                                             in_=pb[b][:]))
                        r_done(b, "V")
                    M[f"p5aoT_{p}"] = ("V", sV.v)
                    wt(E, M[f"p5xld_{p}"])
                    for oc2 in range(2):
                        wt(E, M[f"p5mm_{p}_{oc2}"])
                        if p >= 2:
                            wt(E, M[f"p5xst_{p - 2}"])
                        sV.inc(E.tensor_add(
                            out=x2b5[:, (p % 2) * D + oc2 * 512:
                                     (p % 2) * D + oc2 * 512 + 512],
                            in0=pb[oc2][:],
                            in1=x1b5[:, (p % 2) * D + oc2 * 512:
                                     (p % 2) * D + oc2 * 512 + 512]))
                        r_done(oc2, "V")
                    M[f"p5res_{p}"] = ("V", sV.v)

            @body("gpsimd")
            def _(E):
                for p in range(NT):
                    wt(E, M[f"p5res_{p}"])
                    M[f"p5xst_{p}"] = dma(
                        E, "p5A" if p % 2 == 0 else "p5B",
                        x1sp[p * 128:(p + 1) * 128, :],
                        x2b5[:, (p % 2) * D:(p % 2 + 1) * D])

            barrier()
            sbt_free("wo_sb", "x1b5", "x2b5", "aoT")

        if need_ao:
            sbt_free("attnout", "q_sb")

        if "ffn2" in phases:
            emit_ffn(2, x1sp, yout)
            final_is_yout = True
        else:
            final_is_yout = (len(phases) <= 2 and "ffn1" in phases)

        if not final_is_yout:
            cpbuf = sbt("cpbuf", [128, D], F32)

            @body("gpsimd")
            def _(E):
                srcd = x1sp
                for p in range(NT):
                    dma(E, "cp", cpbuf[:], srcd[p * 128:(p + 1) * 128, :])
                    if E is not _FAKE:
                        E.wait_ge(C["cp"].sem, C["cp"].v)
                    dma(E, "cp", yout[p * 128:(p + 1) * 128, :], cpbuf[:])
                    if E is not _FAKE:
                        E.wait_ge(C["cp"].sem, C["cp"].v)
            sbt_free("cpbuf")

        @body("gpsimd")
        def _(E):
            if E is not _FAKE:
                for n in dma_sems:
                    if C[n].v > 0:
                        E.wait_ge(C[n].sem, C[n].v)

        if dbg_d is not None:
            sbt_free("dbg_buf", "dbgc")

    emit(final=False)
    emit(final=True)
    nc._marks = dict(M.m0)
    ctx.close()
    return nc



def _tables(par):
    inv = 10000.0 ** (-(np.arange(32, dtype=np.float64)) / 32.0)
    posk = np.arange(T, dtype=np.float64)
    angk = posk[:, None] * inv[None, :]
    g = np.arange(32) * 2 + par
    posq = (g[:, None] * 64 + np.arange(64)[None, :]).reshape(-1).astype(np.float64)
    angq = posq[:, None] * inv[None, :]
    qcos = np.tile((np.cos(angq) * 0.125).astype(np.float32), (1, 16))
    qsin = np.tile((np.sin(angq) * 0.125).astype(np.float32), (1, 16))
    kcos = np.tile(np.cos(angk).astype(np.float32), (1, 8))
    ksin = np.tile(np.sin(angk).astype(np.float32), (1, 8))
    return (qcos.astype(ml_dtypes.bfloat16), qsin.astype(ml_dtypes.bfloat16),
            kcos.astype(ml_dtypes.bfloat16), ksin.astype(ml_dtypes.bfloat16))


def _masks(par):
    r = np.arange(128)
    kap = np.arange(128)[:, None]
    qpos = np.where(r < 64, 64 * par + r, 64 * par + r + 64)
    m0 = (kap <= qpos[None, :]).astype(np.float32)
    m1 = ((128 + kap) <= qpos[None, :]).astype(np.float32)
    return np.ascontiguousarray(
        np.concatenate([m0, m1], axis=1).astype(ml_dtypes.bfloat16))


_NC_CACHE = {}


def kernel(L0_attn_x, L0_attn_W_q, L0_attn_W_dkv, L0_attn_W_uk, L0_attn_W_uv,
           L0_attn_W_o, L0_attn_ln_gamma, L0_attn_ln_beta, L0_ffn_w_gate,
           L0_ffn_w_up, L0_ffn_w_down):
    x = np.ascontiguousarray(np.asarray(L0_attn_x, dtype=np.float32))
    wq = np.ascontiguousarray(np.asarray(L0_attn_W_q, dtype=np.float32))
    wdkv = np.ascontiguousarray(np.asarray(L0_attn_W_dkv, dtype=np.float32))
    wuk = np.ascontiguousarray(np.asarray(L0_attn_W_uk, dtype=np.float32))
    wuv = np.ascontiguousarray(np.asarray(L0_attn_W_uv, dtype=np.float32))
    wo = np.ascontiguousarray(np.asarray(L0_attn_W_o, dtype=np.float32))
    gam = np.asarray(L0_attn_ln_gamma, dtype=np.float32)
    bet = np.asarray(L0_attn_ln_beta, dtype=np.float32)
    wg = np.ascontiguousarray(np.asarray(L0_ffn_w_gate, dtype=np.float32))
    wu = np.ascontiguousarray(np.asarray(L0_ffn_w_up, dtype=np.float32))
    wd = np.ascontiguousarray(np.asarray(L0_ffn_w_down, dtype=np.float32))

    gam_t = np.ascontiguousarray(np.broadcast_to(gam, (128, LAT)))
    bet_t = np.ascontiguousarray(np.broadcast_to(bet, (128, LAT)))
    ident = np.eye(128, dtype=ml_dtypes.bfloat16)
    magic = np.random.randint(1, 2**31 - 1, size=(1, 2)).astype(np.int32)

    if "nc" not in _NC_CACHE:
        _NC_CACHE["nc"] = build_nc()
    nc = _NC_CACHE["nc"]

    in_maps = []
    for c in range(8):
        b, par = c // 2, c % 2
        xb = np.ascontiguousarray(x[b].reshape(32, 2, 64, D)[:, par]
                                  .reshape(2048, D))
        qcos, qsin, kcos, ksin = _tables(par)
        xoffs = np.array([[64 * par * LAT, par * 2, (1 - par) * 2,
                           0, 0, 0, 0, 0]], dtype=np.int32)
        in_maps.append({
            "xin": xb, "wq": wq, "wdkv": wdkv, "wuk": wuk, "wuv": wuv,
            "wo": wo, "gam": gam_t, "bet": bet_t, "wg": wg, "wu": wu,
            "wd": wd, "qcos": qcos, "qsin": qsin, "kcos": kcos,
            "ksin": ksin, "masks": _masks(par), "ident": ident,
            "xoffs": xoffs, "magic": magic,
        })

    res = run_bass_kernel_spmd(nc, in_maps, core_ids=list(range(8)))

    out = np.empty((B, T, D), dtype=np.float32)
    for c in range(8):
        b, par = c // 2, c % 2
        yc = np.asarray(res.results[c]["yout"])
        out[b].reshape(32, 2, 64, D)[:, par] = yc.reshape(32, 64, D)
    return out


# version 04
# speedup vs baseline: 1.2273x; 1.2273x over previous
from contextlib import ExitStack

import numpy as np
import ml_dtypes

import concourse.bass as bass
import concourse.mybir as mybir
from concourse.bass_utils import run_bass_kernel_spmd
from concourse.bass_isa import ReduceOp

F32 = mybir.dt.float32
BF16 = mybir.dt.bfloat16
I32 = mybir.dt.int32
ALU = mybir.AluOpType
ACTF = mybir.ActivationFunctionType
AX = mybir.AxisListType.X

B, T, D = 4, 4096, 1024
INNER = 4096
LAT = 256
NT = 16
ST = 4
NST = NT // ST
MROUND = 12582912.0
LN_EPS = 1e-5
MAXPOLL = 30_000_000

ALL_PHASES = ("wprep", "ffn1", "qlat", "attn", "wo", "ffn2")


class _Fake:
    def __call__(self, *a, **k):
        return self

    def __getattr__(self, name):
        return self

    def __enter__(self):
        return self

    def __exit__(self, *a):
        return False


_FAKE = _Fake()


class Ctr:
    def __init__(self, name, sem, step=1):
        self.name, self.sem, self.v, self.step = name, sem, 0, step

    def inc(self, inst, n=None):
        n = self.step if n is None else n
        self.v += n
        if inst is not _FAKE:
            inst.then_inc(self.sem, n)
        return self.v


class Marks:

    def __init__(self):
        self.m0 = None
        self.w = {}

    def start_pass(self, final):
        if final:
            self.m0 = dict(self.w)
        self.w = {}

    def __setitem__(self, k, v):
        if self.m0 is not None and k in self.m0:
            assert self.m0[k] == v, f"mark drift {k}: {self.m0[k]} != {v}"
        self.w[k] = v

    def __contains__(self, k):
        return k in self.w

    def __getitem__(self, k):
        if self.m0 is not None:
            return self.m0[k]
        return self.w.get(k, ("none", 0))


def build_nc(phases=ALL_PHASES, dbg=False):
    nc = bass.Bass("TRN2", target_bir_lowering=False, debug=False,
                   detect_race_conditions=False)

    def din(n, s, dt=F32):
        return nc.dram_tensor(n, s, dt, kind="ExternalInput").ap()

    xin = din("xin", [2048, D])
    wq_d = din("wq", [D, D])
    wdkv_d = din("wdkv", [D, LAT])
    wuk_d = din("wuk", [LAT, D])
    wuv_d = din("wuv", [LAT, D])
    wo_d = din("wo", [D, D])
    gam_d = din("gam", [128, LAT])
    bet_d = din("bet", [128, LAT])
    wg_d = din("wg", [INNER, D])
    wu_d = din("wu", [INNER, D])
    wd_d = din("wd", [D, INNER])
    qcos_d = din("qcos", [2048, 512], BF16)
    qsin_d = din("qsin", [2048, 512], BF16)
    kcos_d = din("kcos", [T, 256], BF16)
    ksin_d = din("ksin", [T, 256], BF16)
    masks_d = din("masks", [128, 256], BF16)
    ident_d = din("ident", [128, 128], BF16)
    xoffs_d = din("xoffs", [1, 8], I32)
    magic_d = din("magic", [1, 2], I32)

    yout = nc.dram_tensor("yout", [2048, D], F32, kind="ExternalOutput").ap()
    dbg_d = (nc.dram_tensor("dbg", [128, 16384], F32, kind="ExternalOutput").ap()
             if dbg else None)

    lat_sh = nc.dram_tensor("lat_sh", [T, LAT], BF16, kind="Internal",
                            addr_space="Shared").ap()
    flag_sh = nc.dram_tensor("flag_sh", [2, 2], I32, kind="Internal",
                             addr_space="Shared").ap()
    x1sp = nc.dram_tensor("x1sp", [2048, D], F32, kind="Internal").ap()
    lat_pv = nc.dram_tensor("lat_pv", [2048, LAT], BF16, kind="Internal").ap()
    wgT_s = nc.dram_tensor("wgT_s", [8, 128, INNER], BF16, kind="Internal").ap()
    wuT_s = nc.dram_tensor("wuT_s", [8, 128, INNER], BF16, kind="Internal").ap()
    wdT_s = nc.dram_tensor("wdT_s", [32, 128, D], BF16, kind="Internal").ap()

    ctx = ExitStack()
    e = ctx.enter_context

    ident_sb = e(nc.sbuf_tensor("ident_sb", [128, 128], BF16))
    col = e(nc.sbuf_tensor("col", [128, 40], F32))
    ag_sb = e(nc.sbuf_tensor("ag_sb", [128, NT], F32))
    au_sb = e(nc.sbuf_tensor("au_sb", [128, NT], F32))
    adh_sb = e(nc.sbuf_tensor("adh_sb", [128, NT], F32))
    mclip = e(nc.sbuf_tensor("mclip", [128, 4], F32))
    wrecip = e(nc.sbuf_tensor("wrecip", [128, 4], F32))
    gam_sb = e(nc.sbuf_tensor("gam_sb", [128, LAT], F32))
    bet_sb = e(nc.sbuf_tensor("bet_sb", [128, LAT], F32))
    masks_sb = e(nc.sbuf_tensor("masks_sb", [128, 256], BF16))
    ones_sb = e(nc.sbuf_tensor("ones_sb", [128, 128], F32))
    offs_sb = e(nc.sbuf_tensor("offs_sb", [1, 8], I32))
    magic_sb = e(nc.sbuf_tensor("magic_sb", [1, 2], I32))

    pb03 = e(nc.psum_tensor("pb03", [128, 2048], F32))
    pb45 = e(nc.psum_tensor("pb45", [128, 1024], F32))
    pb6t = e(nc.psum_tensor("pb6", [128, 512], F32))
    pb7t = e(nc.psum_tensor("pb7", [128, 512], F32))
    pb = ([pb03[:, i * 512:(i + 1) * 512] for i in range(4)]
          + [pb45[:, i * 512:(i + 1) * 512] for i in range(2)]
          + [pb6t[:], pb7t[:]])

    sem_T = e(nc.semaphore("sem_T"))
    sem_V = e(nc.semaphore("sem_V"))
    sem_S = e(nc.semaphore("sem_S"))
    sem_G = e(nc.semaphore("sem_G"))
    DMA_SLOTS = ("cP0", "wpA", "wpB", "wpC", "wpD", "wsA", "wsB", "fxA", "fxB", "fgA",
                 "fgB", "fd0", "fd1", "fd2", "fd3", "fstA", "fstB", "cP3w",
                 "cQT", "p3A", "p3B", "lst", "flg", "cLT", "cP4w", "ktA",
                 "ktB", "cP5w", "p5A", "p5B", "cp")
    dma_sems = {n: e(nc.semaphore("sem_" + n)) for n in DMA_SLOTS}

    block = e(nc.Block())
    M = Marks()

    sbuf_cache = {}

    def emit(final):
        C = {}
        C["T"] = Ctr("T", sem_T)
        C["V"] = Ctr("V", sem_V)
        C["S"] = Ctr("S", sem_S)
        C["G"] = Ctr("G", sem_G)
        for n, sem in dma_sems.items():
            C[n] = Ctr(n, sem, 16)
        sT, sV, sS, sG = C["T"], C["V"], C["S"], C["G"]
        M.start_pass(final)

        def dma(E, slot, out, in_, **kw):
            inst = E.dma_start(out=out, in_=in_, **kw)
            C[slot].inc(inst)
            return (slot, C[slot].v)

        dbg_col = [0]
        dbg_buf = None

        def tap(E, ap, n):
            if dbg_d is None:
                return
            c0 = dbg_col[0]
            dbg_col[0] += n
            dma(E, "cp", dbg_d[:, c0:c0 + n], ap)
            if E is not _FAKE:
                E.wait_ge(C["cp"].sem, C["cp"].v)

        bank_readers = {i: [] for i in range(8)}
        bank_writer = {i: ("T", 0) for i in range(8)}
        _waited = {}

        def wt(E, mark):
            tag, v = mark
            if tag == "none" or v == 0:
                return
            if E is not _FAKE:
                seen = _waited.setdefault(_cur_eng[0], {})
                if seen.get(tag, -1) >= v:
                    return
                seen[tag] = v
                E.wait_ge(C[tag].sem, v)

        def jnk(E):
            E.tensor_copy(out=col[:, 38:39], in_=ones_sb[:, 0:1])

        def vdrain(E):
            if E is not _FAKE:
                E.wait_ge(sem_V, C["V"].v)

        def t_acquire(E, b):
            pass

        def t_wrote(b):
            bank_writer[b] = ("T", sT.v)

        def r_wait(E, b):
            wt(E, bank_writer[b])

        def r_done(b, tag):
            pass

        _cur_eng = [None]

        def body(engname):
            def deco(fn):
                if final:
                    def wrapped(E, fn=fn, engname=engname):
                        _cur_eng[0] = engname
                        fn(E)
                    getattr(block, engname)(wrapped)
                else:
                    fn(_FAKE)
            return deco

        def RAP(tensor, off, pat):
            if not final:
                return None
            return bass.AP(tensor, off, pat)

        def sbt(name, shape, dt):
            if name not in sbuf_cache:
                guard = nc.sbuf_tensor(name, shape, dt)
                handle = guard.__enter__()
                sbuf_cache[name] = (guard, handle)
            return sbuf_cache[name][1]

        if dbg_d is not None:
            dbg_buf = sbt("dbg_buf", [128, D], BF16)
            dbgc = sbt("dbgc", [128, 32], F32)

        def sbt_free(*names):
            if final:
                return
            for name in reversed(names):
                guard, _ = sbuf_cache[name]
                guard.__exit__(None, None, None)

        def barrier():
            vals = [(k, c.v) for k, c in C.items()]

            def waits(E, skip):
                for k, v in vals:
                    if k != skip and v > 0:
                        wt(E, (k, v))

            for engname, skip in (("tensor", "T"), ("vector", "V"),
                                  ("scalar", "S"), ("gpsimd", "G"),
                                  ("sync", None)):
                @body(engname)
                def _(E, skip=skip):
                    waits(E, skip)

        @body("sync")
        def _(E):
            for dst, srct in ((ident_sb, ident_d), (gam_sb, gam_d),
                              (bet_sb, bet_d), (masks_sb, masks_d),
                              (offs_sb, xoffs_d), (magic_sb, magic_d)):
                mk = dma(E, "cP0", dst[:], srct[:])
            M["p0"] = mk

        @body("vector")
        def _(E):
            sV.inc(E.memset(ones_sb[:], 1.0))
            M["ones"] = ("V", sV.v)

        if "wprep" in phases:
            wp_f32 = sbt("wp_f32", [128, 4 * INNER], F32)
            wp_i32 = sbt("wp_i32", [128, INNER], F32)
            wp_t = sbt("wp_t", [128, INNER], F32)
            wp_m = sbt("wp_m", [128, 4 * INNER], BF16)
            wp_st = sbt("wp_st", [128, 2 * 512], BF16)
            wp_acc = sbt("wp_acc", [128, 4], F32)
            wp_tmp = sbt("wp_tmp", [128, 4], F32)
            wp_strip = sbt("wp_strip", [128, 72], F32)

            WCFG = [("g", wg_d, wgT_s, 32, D),
                    ("u", wu_d, wuT_s, 32, D),
                    ("d", wd_d, wdT_s, 8, INNER)]
            CHUNKS = [(wi, tag, wdram, ch, K)
                      for wi, (tag, wdram, _w, nch, K) in enumerate(WCFG)
                      for ch in range(nch)]

            WPSLOT = ("wpA", "wpB", "wpC", "wpD")
            WPENG = ("scalar", "sync", "gpsimd", "sync")

            for _eng in ("scalar", "sync", "gpsimd"):
                @body(_eng)
                def _(E, _eng=_eng):
                    for gi, (wi, tag, wdram, ch, K) in enumerate(CHUNKS):
                        sl = gi % 4
                        if WPENG[sl] != _eng:
                            continue
                        if gi >= 4:
                            wt(E, M[f"wp1v_{gi - 4}"])
                        M[f"wp1ld_{gi}"] = dma(
                            E, WPSLOT[sl],
                            wp_f32[:, sl * INNER:sl * INNER + K],
                            wdram[ch * 128:(ch + 1) * 128, :])

            @body("vector")
            def _(E):
                for gi, (wi, tag, wdram, ch, K) in enumerate(CHUNKS):
                    sl = gi % 4
                    wt(E, M[f"wp1ld_{gi}"])
                    src = wp_f32[:, sl * INNER:sl * INNER + K]
                    sV.inc(E.reduce_sum(
                        out=wp_strip[:, gi:gi + 1], in_=src, axis=AX,
                        apply_absolute_value=True))
                    M[f"wp1v_{gi}"] = ("V", sV.v)
                E.reduce_sum(out=wp_acc[:, 0:1], in_=wp_strip[:, 0:32], axis=AX)
                E.reduce_sum(out=wp_acc[:, 1:2], in_=wp_strip[:, 32:64], axis=AX)
                sV.inc(E.reduce_sum(out=wp_acc[:, 2:3], in_=wp_strip[:, 64:72],
                                    axis=AX))
                M["wp1_done"] = ("V", sV.v)

            @body("tensor")
            def _(E):
                wt(E, M["wp1_done"])
                wt(E, M["ones"])
                t_acquire(E, 6)
                sT.inc(E.matmul(out=pb[6][0:1, 0:3], lhsT=ones_sb[:, 0:1],
                                rhs=wp_acc[:, 0:3], start=True, stop=True))
                t_wrote(6)
                M["wp_sum"] = ("T", sT.v)

            @body("vector")
            def _(E):
                wt(E, M["wp_sum"])
                sV.inc(E.tensor_copy(out=wp_tmp[0:1, 0:3], in_=pb[6][0:1, 0:3]))
                r_done(6, "V")
                M["wp_sumrow"] = ("V", sV.v)

            @body("tensor")
            def _(E):
                wt(E, M["wp_sumrow"])
                t_acquire(E, 6)
                sT.inc(E.matmul(out=pb[6][:, 0:3], lhsT=ones_sb[0:1, 0:128],
                                rhs=wp_tmp[0:1, 0:3], start=True, stop=True))
                t_wrote(6)
                M["wp_bc"] = ("T", sT.v)

            @body("vector")
            def _(E):
                wt(E, M["wp_bc"])
                E.tensor_scalar(out=wp_tmp[:, 0:3], in0=pb[6][:, 0:3],
                                scalar1=1.0 / 4194304.0, scalar2=1e-5,
                                op0=ALU.mult, op1=ALU.max)
                r_done(6, "V")
                E.tensor_copy(out=col[:, 38:39], in_=ones_sb[:, 0:1])
                E.tensor_copy(out=mclip[:, 0:3], in_=wp_tmp[:, 0:3])
                sV.inc(E.reciprocal(out=wrecip[:, 0:3], in_=wp_tmp[:, 0:3]))
                M["wscale"] = ("V", sV.v)

            if dbg_d is not None:
                @body("gpsimd")
                def _(E):
                    wt(E, M["wscale"])
                    tap(E, mclip[:, 0:3], 3)
                    tap(E, wrecip[:, 0:3], 3)

            WPENG2 = ("scalar", "sync", "scalar", "sync")
            for _eng in ("scalar", "sync"):
                @body(_eng)
                def _(E, _eng=_eng):
                    for gi, (wi, tag, wdram, ch, K) in enumerate(CHUNKS):
                        sl = gi % 4
                        if WPENG2[sl] != _eng:
                            continue
                        if gi >= 4:
                            wt(E, M[f"wp2q_{gi - 4}"])
                        else:
                            lg = len(CHUNKS) - 1
                            while lg % 4 != sl:
                                lg -= 1
                            wt(E, M[f"wp1v_{lg}"])
                        M[f"wp2ld_{gi}"] = dma(
                            E, WPSLOT[sl],
                            wp_f32[:, sl * INNER:sl * INNER + K],
                            wdram[ch * 128:(ch + 1) * 128, :])

            GRPS = []
            _off = 0
            for wi, (tag, wdram, wscr, nch, K) in enumerate(WCFG):
                for grp in range(nch // 4):
                    GRPS.append((wi, wscr, grp, K // 128, _off + grp * 4))
                _off += nch

            for gg, (wi, wscr, grp, nib, gi0) in enumerate(GRPS):
                @body("vector")
                def _(E, gg=gg, wi=wi, grp=grp, nib=nib, gi0=gi0):
                    if gg == 0:
                        wt(E, M["wscale"])
                    for g4 in range(4):
                        gi = gi0 + g4
                        K = CHUNKS[gi][4]
                        sl = gi % 4
                        wt(E, M[f"wp2ld_{gi}"])
                        if g4 == 0 and gg >= 1:
                            wt(E, M[f"wpflips_{gg - 1}"])
                        srcw = wp_f32[:, sl * INNER:sl * INNER + K]
                        nC = K // 512
                        stages = []
                        for cc in range(nC):
                            stages.append(("a", cc))
                            if cc >= 1:
                                stages.append(("b", cc - 1))
                            if cc >= 2:
                                stages.append(("c", cc - 2))
                        stages += [("b", nC - 1), ("c", nC - 2), ("c", nC - 1)]
                        for st_, cc in stages:
                            lo, hi = cc * 512, (cc + 1) * 512
                            if st_ == "a":
                                inst = E.tensor_scalar(
                                    out=wp_i32[:, lo:hi], in0=srcw[:, lo:hi],
                                    scalar1=wrecip[:, wi:wi + 1],
                                    scalar2=MROUND, op0=ALU.mult, op1=ALU.add)
                            elif st_ == "b":
                                inst = E.tensor_scalar(
                                    out=wp_t[:, lo:hi], in0=wp_i32[:, lo:hi],
                                    scalar1=-MROUND, scalar2=1.0,
                                    op0=ALU.add, op1=ALU.min)
                            else:
                                inst = E.tensor_scalar(
                                    out=wp_m[:, g4 * INNER + lo:
                                             g4 * INNER + hi],
                                    in0=wp_t[:, lo:hi], scalar1=-1.0,
                                    scalar2=None, op0=ALU.max)
                        sV.inc(inst)
                        M[f"wp2q_{gi}"] = ("V", sV.v)

                @body("tensor")
                def _(E, gg=gg, nib=nib, gi0=gi0):
                    wt(E, M[f"wp2q_{gi0 + 3}"])
                    for ib in range(nib):
                        b = 4 + (ib % 2)
                        if ib >= 2:
                            wt(E, M[f"wpev_{gg}_{ib - 2}"])
                        elif gg > 0:
                            pnib = GRPS[gg - 1][3]
                            wt(E, M[f"wpev_{gg - 1}_{pnib - 2 + ib}"])
                        for j in range(4):
                            mm = E.matmul(
                                out=pb[b][:, j * 128:(j + 1) * 128],
                                lhsT=wp_m[:, j * INNER + ib * 128:
                                          j * INNER + ib * 128 + 128],
                                rhs=ident_sb[:], start=True, stop=True)
                        sT.inc(mm)
                        M[f"wpflip_{gg}_{ib}"] = ("T", sT.v)
                    M[f"wpflips_{gg}"] = ("T", sT.v)

                @body("vector")
                def _(E, gg=gg, nib=nib):
                    for ib in range(nib):
                        b = 4 + (ib % 2)
                        wt(E, M[f"wpflip_{gg}_{ib}"])
                        if ib >= 2:
                            wt(E, M[f"wpstd_{gg}_{ib - 2}"])
                        elif gg > 0:
                            wt(E, M[f"wpstd_{gg - 1}_last{ib}"])
                        sV.inc(E.tensor_copy(
                            out=wp_st[:, (ib % 2) * 512:(ib % 2) * 512 + 512],
                            in_=pb[b][:]))
                        M[f"wpev_{gg}_{ib}"] = ("V", sV.v)

                @body("gpsimd")
                def _(E, gg=gg, wscr=wscr, grp=grp, nib=nib):
                    for ib in range(nib):
                        wt(E, M[f"wpev_{gg}_{ib}"])
                        M[f"wpstd_{gg}_{ib}"] = dma(
                            E, "wsA" if ib % 2 == 0 else "wsB",
                            wscr[ib, :, grp * 512:(grp + 1) * 512],
                            wp_st[:, (ib % 2) * 512:(ib % 2) * 512 + 512])
                        if ib == nib - 2:
                            M[f"wpstd_{gg}_last0"] = M[f"wpstd_{gg}_{ib}"]
                        if ib == nib - 1:
                            M[f"wpstd_{gg}_last1"] = M[f"wpstd_{gg}_{ib}"]

            @body("gpsimd")
            def _(E):
                M["wprep_done"] = M[f"wpstd_{len(GRPS) - 1}_{31}"]

            barrier()
            sbt_free("wp_f32", "wp_i32", "wp_t", "wp_m", "wp_st", "wp_acc",
                     "wp_tmp", "wp_strip")

        def emit_ffn(fid, src_dram, dst_dram):
            pref = f"f{fid}"
            x_st = sbt(f"{pref}x", [128, 2 * ST * D], F32)
            h_sb = sbt(f"{pref}h", [128, ST * INNER], F32)
            hqT = sbt(f"{pref}hqT", [128, ST * INNER], BF16)
            xqT = sbt(f"{pref}xqT", [128, ST * D], BF16)
            wbg = sbt(f"{pref}wbg", [128, 2 * INNER], BF16)
            wbu = sbt(f"{pref}wbu", [128, 2 * INNER], BF16)
            wbd = sbt(f"{pref}wbd", [128, 4 * D], BF16)
            scr = sbt(f"{pref}scr", [128, D], F32)
            hq_bf = sbt(f"{pref}hqb", [128, INNER], BF16)
            xq_bf = sbt(f"{pref}xqb", [128, D], BF16)
            gate = sbt(f"{pref}gate", [128, 512], F32)

            @body("sync")
            def _(E):
                for s in range(NST):
                    if s >= 2:
                        wt(E, M[f"{pref}res_{s - 2}"])
                        wt(E, M[f"{pref}xst_{s - 2}"])
                    M[f"{pref}xld_{s}"] = dma(
                        E, "fxA" if s % 2 == 0 else "fxB",
                        x_st[:, (s % 2) * ST * D:(s % 2 + 1) * ST * D],
                        src_dram[s * ST * 128:(s + 1) * ST * 128, :])
                    for oc in range(8):
                        half = oc % 2
                        if s > 0 or oc >= 2:
                            po = oc - 2 if oc >= 2 else 6 + oc
                            ps = s if oc >= 2 else s - 1
                            wt(E, M[f"{pref}gu_done_{ps}_{po}"])
                        for wscr, buf in ((wgT_s, wbg), (wuT_s, wbu)):
                            mk = dma(
                                E, "fgA" if half == 0 else "fgB",
                                buf[:, half * INNER:(half + 1) * INNER]
                                .rearrange("p (i c) -> p i c", i=8),
                                wscr[:, :, oc * 512:(oc + 1) * 512]
                                .rearrange("i p c -> p i c"))
                        M[f"{pref}wgu_{s}_{oc}"] = mk

            for s in range(NST):
                xoff = (s % 2) * ST * D

                if dbg_d is not None and fid == 1 and s == 0:
                    @body("gpsimd")
                    def _(E):
                        wt(E, M[f"{pref}sil_0_0_0"])
                        tap(E, gate[:], 512)
                        M["dbg_gate_done"] = ("cp", C["cp"].v)
                        wt(E, M[f"{pref}hcmb_0_0_0"])
                        tap(E, h_sb[:, 0:512], 512)
                        M["dbg_h_done"] = ("cp", C["cp"].v)
                        wt(E, M["dbghq"])
                        tap(E, dbg_buf[:, 0:D], D)
                        tap(E, dbgc[:, 0:3], 3)
                        M["dbg_hq_done"] = ("cp", C["cp"].v)

                @body("gpsimd")
                def _(E, s=s):
                    for ib in range(32):
                        if s > 0 or ib >= 4:
                            pi = ib - 4 if ib >= 4 else 28 + ib
                            ps = s if ib >= 4 else s - 1
                            wt(E, M[f"{pref}dn_{ps}_{pi}"])
                        M[f"{pref}wdl_{s}_{ib}"] = dma(
                            E, f"fd{ib % 4}",
                            wbd[:, (ib % 4) * D:(ib % 4 + 1) * D],
                            wdT_s[ib, :, :])

                @body("vector")
                def _(E, s=s, xoff=xoff):
                    wt(E, M[f"{pref}xld_{s}"])
                    if fid == 1 and s == 0 and "wprep" in phases:
                        wt(E, M["wscale"])
                    for ti in range(ST):
                        p = s * ST + ti
                        xt = x_st[:, xoff + ti * D:xoff + (ti + 1) * D]
                        cs = (ti % 2) * 16
                        E.reduce_sum(out=col[:, cs:cs + 1], in_=xt, axis=AX)
                        jnk(E)
                        sV.inc(E.tensor_scalar(
                            out=col[:, cs + 1:cs + 2], in0=col[:, cs:cs + 1],
                            scalar1=-1.0 / D, scalar2=None, op0=ALU.mult))
                        M[f"{pref}mneg_{p}"] = ("V", sV.v)
                        wt(E, M[f"{pref}sq_{p}"])
                        E.tensor_scalar(out=col[:, cs + 2:cs + 3],
                                        in0=col[:, cs + 3:cs + 4],
                                        scalar1=1.0 / D, scalar2=LN_EPS,
                                        op0=ALU.mult, op1=ALU.add)
                        sV.inc(E.reduce_max(out=col[:, cs + 4:cs + 5],
                                            in_=scr[:, 0:D], axis=AX))
                        M[f"{pref}vm_{p}"] = ("V", sV.v)
                        wt(E, M[f"{pref}sqrt_{p}"])
                        E.reciprocal(out=col[:, cs + 7:cs + 8],
                                     in_=col[:, cs + 5:cs + 6])
                        E.tensor_scalar(out=col[:, cs + 12:cs + 13],
                                        in0=col[:, cs + 6:cs + 7],
                                        scalar1=mclip[:, 0:1],
                                        scalar2=1.0 / 127.0,
                                        op0=ALU.mult, op1=ALU.mult)
                        E.tensor_scalar(out=col[:, cs + 13:cs + 14],
                                        in0=col[:, cs + 6:cs + 7],
                                        scalar1=mclip[:, 1:2],
                                        scalar2=1.0 / 127.0,
                                        op0=ALU.mult, op1=ALU.mult)
                        E.tensor_mul(out=col[:, cs + 8:cs + 9],
                                     in0=col[:, cs + 7:cs + 8],
                                     in1=col[:, cs + 6:cs + 7])
                        E.tensor_mul(out=ag_sb[:, p:p + 1],
                                     in0=col[:, cs + 12:cs + 13],
                                     in1=col[:, cs + 7:cs + 8])
                        E.tensor_mul(out=au_sb[:, p:p + 1],
                                     in0=col[:, cs + 13:cs + 14],
                                     in1=col[:, cs + 7:cs + 8])
                        E.reciprocal(out=col[:, cs + 9:cs + 10],
                                     in_=col[:, cs + 8:cs + 9])
                        jnk(E)
                        E.tensor_scalar(out=col[:, cs + 9:cs + 10],
                                        in0=col[:, cs + 9:cs + 10],
                                        scalar1=127.0, scalar2=1.27e7,
                                        op0=ALU.mult, op1=ALU.min)
                        jnk(E)
                        E.tensor_mul(out=col[:, cs + 10:cs + 11],
                                     in0=col[:, cs + 9:cs + 10],
                                     in1=col[:, cs + 7:cs + 8])
                        jnk(E)
                        E.tensor_mul(out=col[:, cs + 11:cs + 12],
                                     in0=col[:, cs + 1:cs + 2],
                                     in1=col[:, cs + 10:cs + 11])
                        jnk(E)
                        if p > 0:
                            wt(E, M[f"{pref}xqf_{p - 1}"])
                        E.tensor_scalar(out=scr[:, 0:512],
                                        in0=xt[:, 0:512],
                                        scalar1=col[:, cs + 10:cs + 11],
                                        scalar2=col[:, cs + 11:cs + 12],
                                        op0=ALU.mult, op1=ALU.add)
                        E.tensor_scalar(out=scr[:, 512:1024],
                                        in0=xt[:, 512:1024],
                                        scalar1=col[:, cs + 10:cs + 11],
                                        scalar2=col[:, cs + 11:cs + 12],
                                        op0=ALU.mult, op1=ALU.add)
                        E.tensor_scalar(out=xq_bf[:, 0:512],
                                        in0=scr[:, 0:512],
                                        scalar1=MROUND, scalar2=MROUND,
                                        op0=ALU.add, op1=ALU.subtract)
                        sV.inc(E.tensor_scalar(out=xq_bf[:, 512:1024],
                                               in0=scr[:, 512:1024],
                                               scalar1=MROUND, scalar2=MROUND,
                                               op0=ALU.add, op1=ALU.subtract))
                        M[f"{pref}xq_{p}"] = ("V", sV.v)
                        for gfl in range(2):
                            b = 4 + gfl
                            wt(E, M[f"{pref}xqfg_{p}_{gfl}"])
                            sV.inc(E.tensor_copy(
                                out=xqT[:, ti * D + gfl * 512:
                                        ti * D + gfl * 512 + 512],
                                in_=pb[b][:]))
                            M[f"{pref}xqTe_{p}_{gfl}"] = ("V", sV.v)
                        M[f"{pref}xqT_{p}"] = ("V", sV.v)


                @body("scalar")
                def _(E, s=s, xoff=xoff):
                    for ti in range(ST):
                        p = s * ST + ti
                        cs = (ti % 2) * 16
                        xt = x_st[:, xoff + ti * D:xoff + (ti + 1) * D]
                        wt(E, M[f"{pref}mneg_{p}"])
                        sS.inc(E.activation(out=scr[:, 0:D], in_=xt,
                                            func=ACTF.Square,
                                            bias=col[:, cs + 1:cs + 2],
                                            scale=1.0,
                                            accum_out=col[:, cs + 3:cs + 4]))
                        M[f"{pref}sq_{p}"] = ("S", sS.v)
                        wt(E, M[f"{pref}vm_{p}"])
                        E.activation(out=col[:, cs + 5:cs + 6],
                                     in_=col[:, cs + 2:cs + 3], func=ACTF.Sqrt)
                        sS.inc(E.activation(out=col[:, cs + 6:cs + 7],
                                            in_=col[:, cs + 4:cs + 5],
                                            func=ACTF.Sqrt))
                        M[f"{pref}sqrt_{p}"] = ("S", sS.v)

                @body("tensor")
                def _(E, s=s):
                    for ti in range(ST):
                        p = s * ST + ti
                        wt(E, M[f"{pref}xq_{p}"])
                        if ti == 0 and s > 0:
                            wt(E, M[f"{pref}res_{s - 1}"])
                        for gfl in range(2):
                            b = 4 + gfl
                            if p > 0:
                                wt(E, M[f"{pref}xqTe_{p - 1}_{gfl}"])
                            for j in range(4):
                                mm = E.matmul(
                                    out=pb[b][:, j * 128:(j + 1) * 128],
                                    lhsT=xq_bf[:, gfl * 512 + j * 128:
                                               gfl * 512 + j * 128 + 128],
                                    rhs=ident_sb[:], start=True, stop=True)
                            sT.inc(mm)
                            t_wrote(b)
                            M[f"{pref}xqfg_{p}_{gfl}"] = ("T", sT.v)
                        M[f"{pref}xqf_{p}"] = ("T", sT.v)

                @body("tensor")
                def _(E, s=s):
                    for oc in range(8):
                        half = oc % 2
                        wt(E, M[f"{pref}wgu_{s}_{oc}"])
                        for ti in range(ST):
                            p = s * ST + ti
                            if oc == 0:
                                wt(E, M[f"{pref}xqT_{p}"])
                            bg, bu = (ti % 2) * 2, (ti % 2) * 2 + 1
                            if ti >= 2:
                                wt(E, M[f"{pref}sil_{s}_{oc}_{ti - 2}"])
                                wt(E, M[f"{pref}hcmb_{s}_{oc}_{ti - 2}"])
                            elif oc >= 1:
                                wt(E, M[f"{pref}sil_{s}_{oc - 1}_{ti + 2}"])
                                wt(E, M[f"{pref}hcmb_{s}_{oc - 1}_{ti + 2}"])
                            t_acquire(E, bg)
                            for ib in range(8):
                                mm = E.matmul(
                                    out=pb[bg][:],
                                    lhsT=xqT[:, ti * D + ib * 128:
                                             ti * D + ib * 128 + 128],
                                    rhs=wbg[:, half * INNER + ib * 512:
                                            half * INNER + ib * 512 + 512],
                                    start=(ib == 0), stop=(ib == 7))
                            sT.inc(mm)
                            t_wrote(bg)
                            M[f"{pref}g_{s}_{oc}_{ti}"] = ("T", sT.v)
                            t_acquire(E, bu)
                            for ib in range(8):
                                mm = E.matmul(
                                    out=pb[bu][:],
                                    lhsT=xqT[:, ti * D + ib * 128:
                                             ti * D + ib * 128 + 128],
                                    rhs=wbu[:, half * INNER + ib * 512:
                                            half * INNER + ib * 512 + 512],
                                    start=(ib == 0), stop=(ib == 7))
                            sT.inc(mm)
                            t_wrote(bu)
                            M[f"{pref}u_{s}_{oc}_{ti}"] = ("T", sT.v)
                        M[f"{pref}gu_done_{s}_{oc}"] = ("T", sT.v)

                @body("scalar")
                def _(E, s=s):
                    for oc in range(8):
                        for ti in range(ST):
                            p = s * ST + ti
                            bg = (ti % 2) * 2
                            wt(E, M[f"{pref}g_{s}_{oc}_{ti}"])
                            if (dbg_d is not None and fid == 1 and s == 0
                                    and oc == 0 and ti == 1):
                                wt(E, M["dbg_gate_done"])
                            if oc > 0 or ti > 0:
                                pti = ti - 1 if ti > 0 else ST - 1
                                poc = oc if ti > 0 else oc - 1
                                wt(E, M[f"{pref}hcmb_{s}_{poc}_{pti}"])
                            sS.inc(E.activation(out=gate[:], in_=pb[bg][:],
                                                func=ACTF.Silu,
                                                scale=ag_sb[:, p:p + 1]))
                            r_done(bg, "S")
                            M[f"{pref}sil_{s}_{oc}_{ti}"] = ("S", sS.v)

                @body("vector")
                def _(E, s=s):
                    for oc in range(8):
                        for ti in range(ST):
                            p = s * ST + ti
                            bu = (ti % 2) * 2 + 1
                            wt(E, M[f"{pref}sil_{s}_{oc}_{ti}"])
                            wt(E, M[f"{pref}u_{s}_{oc}_{ti}"])
                            sV.inc(E.scalar_tensor_tensor(
                                out=h_sb[:, ti * INNER + oc * 512:
                                         ti * INNER + oc * 512 + 512],
                                in0=pb[bu][:], scalar=au_sb[:, p:p + 1],
                                in1=gate[:], op0=ALU.mult, op1=ALU.mult))
                            r_done(bu, "V")
                            M[f"{pref}hcmb_{s}_{oc}_{ti}"] = ("V", sV.v)

                for ti in range(ST):
                    @body("vector")
                    def _(E, s=s, ti=ti):
                        p = s * ST + ti
                        wt(E, M[f"{pref}hcmb_{s}_7_{ti}"])
                        if (dbg_d is not None and fid == 1 and s == 0
                                and ti == 0):
                            wt(E, M["dbg_h_done"])
                        ht = h_sb[:, ti * INNER:(ti + 1) * INNER]
                        E.reduce_max(out=col[:, 0:1], in_=ht, axis=AX,
                                     apply_absolute_value=True)
                        jnk(E)
                        E.reciprocal(out=col[:, 1:2], in_=col[:, 0:1])
                        E.tensor_scalar(out=adh_sb[:, p:p + 1],
                                        in0=col[:, 0:1],
                                        scalar1=mclip[:, 2:3],
                                        scalar2=0.5 / 127.0,
                                        op0=ALU.mult, op1=ALU.mult)
                        E.tensor_scalar(out=col[:, 1:2], in0=col[:, 1:2],
                                        scalar1=127.0, scalar2=1.27e7,
                                        op0=ALU.mult, op1=ALU.min)
                        jnk(E)
                        if ti > 0:
                            wt(E, M[f"{pref}hqf_{s}_{ti - 1}"])
                        elif s > 0:
                            wt(E, M[f"{pref}hqf_{s - 1}_{ST - 1}"])
                        if (dbg_d is not None and fid == 1 and s == 0
                                and ti == 1):
                            wt(E, M["dbg_hq_done"])
                        for cc in range(8):
                            E.tensor_scalar(
                                out=ht[:, cc * 512:(cc + 1) * 512],
                                in0=ht[:, cc * 512:(cc + 1) * 512],
                                scalar1=col[:, 1:2], scalar2=None,
                                op0=ALU.mult)
                            if cc >= 1:
                                E.tensor_scalar(
                                    out=hq_bf[:, (cc - 1) * 512:cc * 512],
                                    in0=ht[:, (cc - 1) * 512:cc * 512],
                                    scalar1=MROUND, scalar2=MROUND,
                                    op0=ALU.add, op1=ALU.subtract)
                        sV.inc(E.tensor_scalar(out=hq_bf[:, 7 * 512:8 * 512],
                                               in0=ht[:, 7 * 512:8 * 512],
                                               scalar1=MROUND, scalar2=MROUND,
                                               op0=ALU.add, op1=ALU.subtract))
                        M[f"{pref}hq_{s}_{ti}"] = ("V", sV.v)
                        if (dbg_d is not None and fid == 1 and s == 0
                                and ti == 0):
                            E.tensor_copy(out=dbg_buf[:, 0:D],
                                          in_=hq_bf[:, 0:D])
                            E.tensor_copy(out=dbgc[:, 0:2], in_=col[:, 0:2])
                            sV.inc(E.tensor_copy(out=dbgc[:, 2:3],
                                                 in_=adh_sb[:, 0:1]))
                            M["dbghq"] = ("V", sV.v)

                    @body("tensor")
                    def _(E, s=s, ti=ti):
                        wt(E, M[f"{pref}hq_{s}_{ti}"])
                        for gfl in range(8):
                            b = 4 + (gfl % 2)
                            if gfl >= 2:
                                wt(E, M[f"{pref}hqTe_{s}_{ti}_{gfl - 2}"])
                            elif ti > 0:
                                wt(E, M[f"{pref}hqTe_{s}_{ti - 1}_{6 + gfl}"])
                            else:
                                wt(E, M[f"{pref}xqTe_{s * ST + ST - 1}_{gfl}"])
                            for j in range(4):
                                mm = E.matmul(
                                    out=pb[b][:, j * 128:(j + 1) * 128],
                                    lhsT=hq_bf[:, gfl * 512 + j * 128:
                                               gfl * 512 + j * 128 + 128],
                                    rhs=ident_sb[:], start=True, stop=True)
                            sT.inc(mm)
                            t_wrote(b)
                            M[f"{pref}hqfg_{s}_{ti}_{gfl}"] = ("T", sT.v)
                        M[f"{pref}hqf_{s}_{ti}"] = ("T", sT.v)

                    @body("vector")
                    def _(E, s=s, ti=ti):
                        for gfl in range(8):
                            b = 4 + (gfl % 2)
                            wt(E, M[f"{pref}hqfg_{s}_{ti}_{gfl}"])
                            sV.inc(E.tensor_copy(
                                out=hqT[:, ti * INNER + gfl * 512:
                                        ti * INNER + gfl * 512 + 512],
                                in_=pb[b][:]))
                            M[f"{pref}hqTe_{s}_{ti}_{gfl}"] = ("V", sV.v)
                        M[f"{pref}hqT_{s}_{ti}"] = ("V", sV.v)

                @body("tensor")
                def _(E, s=s):
                    for ti in range(ST):
                        wt(E, M[f"{pref}hqT_{s}_{ti}"])
                    for b in range(8):
                        t_acquire(E, b)
                    for ib in range(32):
                        wt(E, M[f"{pref}wdl_{s}_{ib}"])
                        sl = ib % 4
                        for ti in range(ST):
                            for og in range(2):
                                mm = E.matmul(
                                    out=pb[ti * 2 + og][:],
                                    lhsT=hqT[:, ti * INNER + ib * 128:
                                             ti * INNER + ib * 128 + 128],
                                    rhs=wbd[:, sl * D + og * 512:
                                            sl * D + og * 512 + 512],
                                    start=(ib == 0), stop=(ib == 31))
                        sT.inc(mm)
                        M[f"{pref}dn_{s}_{ib}"] = ("T", sT.v)
                    for b in range(8):
                        t_wrote(b)
                    M[f"{pref}dn_done_{s}"] = ("T", sT.v)

                @body("vector")
                def _(E, s=s, xoff=xoff):
                    wt(E, M[f"{pref}dn_done_{s}"])
                    for ti in range(ST):
                        p = s * ST + ti
                        for og in range(2):
                            xo = xoff + ti * D + og * 512
                            sV.inc(E.scalar_tensor_tensor(
                                out=x_st[:, xo:xo + 512],
                                in0=pb[ti * 2 + og][:],
                                scalar=adh_sb[:, p:p + 1],
                                in1=x_st[:, xo:xo + 512],
                                op0=ALU.mult, op1=ALU.add))
                            r_done(ti * 2 + og, "V")
                    M[f"{pref}res_{s}"] = ("V", sV.v)

                @body("gpsimd")
                def _(E, s=s, xoff=xoff):
                    wt(E, M[f"{pref}res_{s}"])
                    M[f"{pref}xst_{s}"] = dma(
                        E, "fstA" if s % 2 == 0 else "fstB",
                        dst_dram[s * ST * 128:(s + 1) * ST * 128, :],
                        x_st[:, xoff:xoff + ST * D])

            barrier()
            sbt_free(f"{pref}x", f"{pref}h", f"{pref}hqT", f"{pref}xqT",
                     f"{pref}wbg", f"{pref}wbu", f"{pref}wbd", f"{pref}scr",
                     f"{pref}hqb", f"{pref}xqb", f"{pref}gate")

        if "ffn1" in phases:
            dst1 = x1sp if len(phases) > 2 else yout
            emit_ffn(1, xin, dst1)

        need_ao = ("qlat" in phases)
        if need_ao:
            attnout = sbt("attnout", [128, NT * D], BF16)
            q_sb = sbt("q_sb", [128, NT * D], BF16)

            qtab_c = sbt("qtab_c", [128, NT * 512], BF16)
            qtab_s = sbt("qtab_s", [128, NT * 512], BF16)
            wq_sb = sbt("wq_sb", [128, 8 * D], BF16)
            wdkv_sb = sbt("wdkv_sb", [128, 8 * LAT], BF16)
            x1b = sbt("x1b", [128, 2 * D], F32)
            xa_bf = sbt("xa_bf", [128, D], BF16)
            xaT = sbt("xaT", [128, D], BF16)
            ropes = sbt("ropes", [128, 4 * 512], BF16)
            lat_f = sbt("lat_f", [128, LAT], F32)
            lat_bf = sbt("lat_bf", [128, LAT], BF16)
            scr3 = sbt("scr3", [128, D], F32)

            @body("gpsimd")
            def _(E):
                dma(E, "cP3w", wq_sb[:].rearrange("p (i c) -> p i c", i=8),
                    wq_d.rearrange("(i p) c -> p i c", p=128))
                M["p3w"] = dma(
                    E, "cP3w", wdkv_sb[:].rearrange("p (i c) -> p i c", i=8),
                    wdkv_d.rearrange("(i p) c -> p i c", p=128))

            @body("sync")
            def _(E):
                dma(E, "cQT", qtab_c[:].rearrange("r (p c) -> r p c", p=NT),
                    qcos_d.rearrange("(p r) c -> r p c", r=128))
                M["p3tab"] = dma(
                    E, "cQT", qtab_s[:].rearrange("r (p c) -> r p c", p=NT),
                    qsin_d.rearrange("(p r) c -> r p c", r=128))
                for p in range(NT):
                    if p >= 2:
                        wt(E, M[f"p3xa_{p - 2}"])
                        wt(E, M[f"p3sq_{p - 2}"])
                    M[f"p3xld_{p}"] = dma(
                        E, "p3A" if p % 2 == 0 else "p3B",
                        x1b[:, (p % 2) * D:(p % 2 + 1) * D],
                        x1sp[p * 128:(p + 1) * 128, :])

            @body("vector")
            def _(E):
                wt(E, M["p3tab"])
                wt(E, M["p0"])
                for p in range(NT):
                    cs = (p % 2) * 16
                    wt(E, M[f"p3xld_{p}"])
                    xt = x1b[:, (p % 2) * D:(p % 2 + 1) * D]
                    E.reduce_sum(out=col[:, cs:cs + 1], in_=xt, axis=AX)
                    jnk(E)
                    sV.inc(E.tensor_scalar(out=col[:, cs + 1:cs + 2],
                                           in0=col[:, cs:cs + 1],
                                           scalar1=-1.0 / D, scalar2=None,
                                           op0=ALU.mult))
                    M[f"p3mneg_{p}"] = ("V", sV.v)
                    wt(E, M[f"p3sq_{p}"])
                    sV.inc(E.tensor_scalar(out=col[:, cs + 2:cs + 3],
                                           in0=col[:, cs + 3:cs + 4],
                                           scalar1=1.0 / D, scalar2=LN_EPS,
                                           op0=ALU.mult, op1=ALU.add))
                    M[f"p3var_{p}"] = ("V", sV.v)
                    wt(E, M[f"p3sqrt_{p}"])
                    E.reciprocal(out=col[:, cs + 7:cs + 8],
                                 in_=col[:, cs + 5:cs + 6])
                    jnk(E)
                    jnk(E)
                    if p > 0:
                        wt(E, M[f"p3lat_{p - 1}"])
                    sV.inc(E.tensor_scalar(out=xa_bf[:], in0=xt,
                                           scalar1=col[:, cs + 1:cs + 2],
                                           scalar2=col[:, cs + 7:cs + 8],
                                           op0=ALU.add, op1=ALU.mult))
                    M[f"p3xa_{p}"] = ("V", sV.v)
                    for gfl in range(2):
                        b = 4 + gfl
                        wt(E, M[f"p3xafg_{p}_{gfl}"])
                        sV.inc(E.tensor_copy(
                            out=xaT[:, gfl * 512:gfl * 512 + 512],
                            in_=pb[b][:]))
                        r_done(b, "V")
                    M[f"p3xaT_{p}"] = ("V", sV.v)
                    for oc2 in range(2):
                        wt(E, M[f"p3q_{p}_{oc2}"])
                        sV.inc(E.tensor_copy(
                            out=q_sb[:, p * D + oc2 * 512:
                                     p * D + oc2 * 512 + 512],
                            in_=pb[oc2][:]))
                        r_done(oc2, "V")
                    vdrain(E)
                    qt = q_sb[:, p * D:(p + 1) * D]
                    q3 = qt.rearrange("p (h d) -> p h d", h=16)
                    q1 = q3[:, :, 0:32]
                    q2 = q3[:, :, 32:64]
                    tc = qtab_c[:, p * 512:(p + 1) * 512]
                    ts2 = qtab_s[:, p * 512:(p + 1) * 512]
                    Av = ropes[:, 0:512]
                    Bv = ropes[:, 512:1024]
                    Cv = ropes[:, 1024:1536]
                    Dv = ropes[:, 1536:2048]
                    E.tensor_mul(out=Av, in0=q1, in1=tc)
                    E.tensor_mul(out=Bv, in0=q2, in1=ts2)
                    E.tensor_mul(out=Cv, in0=q1, in1=ts2)
                    E.tensor_mul(out=Dv, in0=q2, in1=tc)
                    E.tensor_sub(out=q1, in0=Av, in1=Bv)
                    sV.inc(E.tensor_add(out=q2, in0=Cv, in1=Dv))
                    M[f"p3rope_{p}"] = ("V", sV.v)
                    wt(E, M[f"p3lat_{p}"])
                    E.reduce_sum(out=col[:, cs + 12:cs + 13],
                                 in_=pb[2][:, 0:LAT], axis=AX)
                    jnk(E)
                    sV.inc(E.tensor_scalar(out=col[:, cs + 13:cs + 14],
                                           in0=col[:, cs + 12:cs + 13],
                                           scalar1=-1.0 / LAT, scalar2=None,
                                           op0=ALU.mult))
                    M[f"p3lm_{p}"] = ("V", sV.v)
                    wt(E, M[f"p3lsq_{p}"])
                    sV.inc(E.tensor_scalar(out=col[:, cs + 14:cs + 15],
                                           in0=col[:, cs + 15:cs + 16],
                                           scalar1=1.0 / LAT, scalar2=LN_EPS,
                                           op0=ALU.mult, op1=ALU.add))
                    M[f"p3lv_{p}"] = ("V", sV.v)
                    wt(E, M[f"p3lsqrt_{p}"])
                    E.reciprocal(out=col[:, cs + 12:cs + 13],
                                 in_=col[:, cs + 14:cs + 15])
                    jnk(E)
                    jnk(E)
                    E.tensor_scalar(out=lat_f[:], in0=pb[2][:, 0:LAT],
                                    scalar1=col[:, cs + 13:cs + 14],
                                    scalar2=col[:, cs + 12:cs + 13],
                                    op0=ALU.add, op1=ALU.mult)
                    r_done(2, "V")
                    E.tensor_mul(out=lat_f[:], in0=lat_f[:], in1=gam_sb[:])
                    if p > 0:
                        wt(E, M[f"p3latst_{p - 1}"])
                    sV.inc(E.tensor_add(out=lat_bf[:], in0=lat_f[:],
                                        in1=bet_sb[:]))
                    M[f"p3latbf_{p}"] = ("V", sV.v)

            @body("scalar")
            def _(E):
                for p in range(NT):
                    cs = (p % 2) * 16
                    xt = x1b[:, (p % 2) * D:(p % 2 + 1) * D]
                    wt(E, M[f"p3mneg_{p}"])
                    sS.inc(E.activation(out=scr3[:, 0:D], in_=xt,
                                        func=ACTF.Square,
                                        bias=col[:, cs + 1:cs + 2], scale=1.0,
                                        accum_out=col[:, cs + 3:cs + 4]))
                    M[f"p3sq_{p}"] = ("S", sS.v)
                    wt(E, M[f"p3var_{p}"])
                    sS.inc(E.activation(out=col[:, cs + 5:cs + 6],
                                        in_=col[:, cs + 2:cs + 3],
                                        func=ACTF.Sqrt))
                    M[f"p3sqrt_{p}"] = ("S", sS.v)
                    wt(E, M[f"p3lm_{p}"])
                    sS.inc(E.activation(out=scr3[:, 0:LAT], in_=pb[2][:, 0:LAT],
                                        func=ACTF.Square,
                                        bias=col[:, cs + 13:cs + 14], scale=1.0,
                                        accum_out=col[:, cs + 15:cs + 16]))
                    r_done(2, "S")
                    M[f"p3lsq_{p}"] = ("S", sS.v)
                    wt(E, M[f"p3lv_{p}"])
                    sS.inc(E.activation(out=col[:, cs + 14:cs + 15],
                                        in_=col[:, cs + 14:cs + 15],
                                        func=ACTF.Sqrt))
                    M[f"p3lsqrt_{p}"] = ("S", sS.v)

            @body("tensor")
            def _(E):
                wt(E, M["p3w"])
                for p in range(NT):
                    wt(E, M[f"p3xa_{p}"])
                    for gfl in range(2):
                        b = 4 + gfl
                        t_acquire(E, b)
                        for j in range(4):
                            mm = E.matmul(
                                out=pb[b][:, j * 128:(j + 1) * 128],
                                lhsT=xa_bf[:, gfl * 512 + j * 128:
                                           gfl * 512 + j * 128 + 128],
                                rhs=ident_sb[:], start=True, stop=True)
                        sT.inc(mm)
                        t_wrote(b)
                        M[f"p3xafg_{p}_{gfl}"] = ("T", sT.v)
                    wt(E, M[f"p3xaT_{p}"])
                    for oc2 in range(2):
                        t_acquire(E, oc2)
                        for ib in range(8):
                            mm = E.matmul(
                                out=pb[oc2][:],
                                lhsT=xaT[:, ib * 128:(ib + 1) * 128],
                                rhs=wq_sb[:, ib * D + oc2 * 512:
                                         ib * D + oc2 * 512 + 512],
                                start=(ib == 0), stop=(ib == 7))
                        sT.inc(mm)
                        t_wrote(oc2)
                        M[f"p3q_{p}_{oc2}"] = ("T", sT.v)
                    t_acquire(E, 2)
                    for ib in range(8):
                        mm = E.matmul(out=pb[2][:, 0:LAT],
                                      lhsT=xaT[:, ib * 128:(ib + 1) * 128],
                                      rhs=wdkv_sb[:, ib * LAT:(ib + 1) * LAT],
                                      start=(ib == 0), stop=(ib == 7))
                    sT.inc(mm)
                    t_wrote(2)
                    M[f"p3lat_{p}"] = ("T", sT.v)

            @body("gpsimd")
            def _(E):
                wt(E, M["p0"])
                for p in range(NT):
                    wt(E, M[f"p3latbf_{p}"])
                    M[f"p3latst_{p}"] = dma(
                        E, "lst", lat_pv[p * 128:(p + 1) * 128, :], lat_bf[:])
                M["p3done"] = M[f"p3latst_{NT - 1}"]

            barrier()
            sbt_free("qtab_c", "qtab_s", "wq_sb", "wdkv_sb", "x1b", "xa_bf",
                     "xaT", "ropes", "lat_f", "lat_bf", "scr3")

        if "attn" in phases:
            latT = sbt("latT", [128, 2 * T], BF16)
            wuk_sb = sbt("wuk_sb", [128, 2 * D], BF16)
            wuv_sb = sbt("wuv_sb", [128, 2 * D], BF16)
            kT_sb = sbt("kT_sb", [128, 4 * T], BF16)
            v_sb = sbt("v_sb", [128, 32 * 520], BF16)
            ktab = sbt("ktab", [128, 2 * 512], BF16)
            k_tok = sbt("k_tok", [128, 512], BF16)
            qT_sb = sbt("qT_sb", [128, 512], BF16)
            pT_sb = sbt("pT_sb", [128, 2048], BF16)
            ropes4 = sbt("ropes4", [128, 1024], BF16)

            @body("gpsimd")
            def _(E):
                with ExitStack() as rs:
                    r = rs.enter_context
                    goff = r(E.register("goff"))
                    wfo = r(E.register("wfo"))
                    rfo = r(E.register("rfo"))
                    m0r = r(E.register("m0r"))
                    m1r = r(E.register("m1r"))
                    f0 = r(E.register("f0"))
                    f1 = r(E.register("f1"))
                    e0 = r(E.register("e0"))
                    e1 = r(E.register("e1"))
                    more = r(E.register("more"))
                    cnt = r(E.register("cnt"))
                    wt(E, M["p3done"])
                    E.reg_load(goff, offs_sb[0:1, 0:1])
                    E.reg_load(wfo, offs_sb[0:1, 1:2])
                    E.reg_load(rfo, offs_sb[0:1, 2:3])
                    E.reg_load(m0r, magic_sb[0:1, 0:1])
                    E.reg_load(m1r, magic_sb[0:1, 1:2])
                    M["latgat"] = dma(
                        E, "lst",
                        RAP(lat_sh.tensor, goff,
                            [[32768, 32], [LAT, 64], [1, LAT]]),
                        lat_pv.rearrange("(lg i) c -> lg i c", i=64))
                    wt(E, M["latgat"])
                    dma(E, "flg", RAP(flag_sh.tensor, wfo, [[2, 1], [1, 2]]),
                        magic_sb[:])
                    E.reg_mov(more, 1)
                    E.reg_mov(cnt, MAXPOLL)
                    with E.While(more):
                        E.reg_load((f0, f1),
                                   RAP(flag_sh.tensor, rfo, [[2, 1], [1, 2]]))
                        E.reg_alu(e0, f0, m0r, ALU.is_equal)
                        E.reg_alu(e1, f1, m1r, ALU.is_equal)
                        E.reg_alu(e0, e0, e1, ALU.bitwise_and)
                        E.reg_alu(more, e0, 1, ALU.bitwise_xor)
                        E.reg_add(cnt, cnt, -1)
                        E.reg_alu(e1, cnt, 0, ALU.is_gt)
                        E.reg_alu(more, more, e1, ALU.bitwise_and)
                    sG.inc(E.nop())
                    M["xchg"] = ("G", sG.v)
                dma(E, "cP4w", wuk_sb[:].rearrange("p (i c) -> p i c", i=2),
                    wuk_d.rearrange("(i p) c -> p i c", p=128))
                M["p4w"] = dma(
                    E, "cP4w", wuv_sb[:].rearrange("p (i c) -> p i c", i=2),
                    wuv_d.rearrange("(i p) c -> p i c", p=128))
                sG.inc(E.memset(
                    RAP(v_sb, 64, [[520 * 32, 128], [520, 32], [65, 8]]), 1.0))
                M["vones"] = ("G", sG.v)

            @body("sync")
            def _(E):
                wt(E, M["xchg"])
                dma(E, "cLT", latT[:, 0:T], lat_sh[:, 0:128], transpose=True)
                M["latT"] = dma(E, "cLT", latT[:, T:2 * T], lat_sh[:, 128:256],
                                transpose=True)

            pt_guard = {0: ("none", 0), 1: ("none", 0)}
            qT_guard = ("none", 0)
            av_guard = {6: ("none", 0), 7: ("none", 0)}
            sc_guard = {0: ("none", 0), 1: ("none", 0)}
            scpair = [pb03[:, 0:1024], pb45[:, 0:1024]]
            bctr = [0, 0, 0]

            def batches_of(p):
                nkb = 2 * p + 2
                return nkb, [(b, list(range(8 * b, min(8 * b + 8, nkb))))
                             for b in range((nkb + 7) // 8)]

            for half in range(2):
                @body("sync")
                def _(E, half=half):
                    for kb in range(32):
                        if kb >= 2:
                            wt(E, M[f"krope_{half}_{kb - 2}"])
                        elif half > 0:
                            wt(E, M[f"krope_{half - 1}_{30 + kb}"])
                        sl = "ktA" if kb % 2 == 0 else "ktB"
                        dma(E, sl,
                            ktab[:, (kb % 2) * 512:(kb % 2) * 512 + 256],
                            kcos_d[kb * 128:(kb + 1) * 128, :])
                        M[f"ktab_{half}_{kb}"] = dma(
                            E, sl,
                            ktab[:, (kb % 2) * 512 + 256:(kb % 2) * 512 + 512],
                            ksin_d[kb * 128:(kb + 1) * 128, :])

                @body("tensor")
                def _(E, half=half):
                    if half == 0:
                        wt(E, M["latT"])
                        wt(E, M["p4w"])
                    else:
                        wt(E, sc_guard[0])
                        wt(E, sc_guard[1])
                    for kb in range(32):
                        t_acquire(E, 0)
                        for c in range(2):
                            mm = E.matmul(
                                out=pb[0][:],
                                lhsT=latT[:, c * T + kb * 128:
                                         c * T + kb * 128 + 128],
                                rhs=wuk_sb[:, c * D + half * 512:
                                         c * D + half * 512 + 512],
                                start=(c == 0), stop=(c == 1))
                        sT.inc(mm)
                        t_wrote(0)
                        M[f"kmm_{half}_{kb}"] = ("T", sT.v)
                        if kb > 0:
                            wt(E, M[f"vev_{half}_{kb - 1}"])
                        elif half > 0:
                            wt(E, M[f"vev_{half - 1}_31"])
                        t_acquire(E, 1)
                        for c in range(2):
                            mm = E.matmul(
                                out=pb[1][:],
                                lhsT=latT[:, c * T + kb * 128:
                                         c * T + kb * 128 + 128],
                                rhs=wuv_sb[:, c * D + half * 512:
                                         c * D + half * 512 + 512],
                                start=(c == 0), stop=(c == 1))
                        sT.inc(mm)
                        t_wrote(1)
                        M[f"vmm_{half}_{kb}"] = ("T", sT.v)
                        wt(E, M[f"krope_{half}_{kb}"])
                        t_acquire(E, 2)
                        for j in range(4):
                            mm = E.matmul(out=pb[2][:, j * 128:(j + 1) * 128],
                                          lhsT=k_tok[:, j * 128:(j + 1) * 128],
                                          rhs=ident_sb[:], start=True,
                                          stop=True)
                        sT.inc(mm)
                        t_wrote(2)
                        M[f"kflip_{half}_{kb}"] = ("T", sT.v)

                @body("scalar")
                def _(E, half=half):
                    if half == 0:
                        wt(E, M["vones"])
                    for kb in range(32):
                        wt(E, M[f"vmm_{half}_{kb}"])
                        vdst = RAP(v_sb, kb * 520,
                                   [[520 * 32, 128], [65, 8], [1, 64]])
                        sS.inc(E.copy(out=vdst, in_=pb[1][:]))
                        r_done(1, "S")
                        M[f"vev_{half}_{kb}"] = ("S", sS.v)

                @body("vector")
                def _(E, half=half):
                    for kb in range(32):
                        wt(E, M[f"kmm_{half}_{kb}"])
                        wt(E, M[f"ktab_{half}_{kb}"])
                        if kb > 0:
                            wt(E, M[f"kflip_{half}_{kb - 1}"])
                        elif half > 0:
                            wt(E, M[f"kflip_{half - 1}_31"])
                        sV.inc(E.tensor_copy(out=k_tok[:], in_=pb[0][:]))
                        r_done(0, "V")
                        vdrain(E)
                        kt3 = k_tok[:].rearrange("p (h d) -> p h d", h=8)
                        k1 = kt3[:, :, 0:32]
                        k2 = kt3[:, :, 32:64]
                        tc = ktab[:, (kb % 2) * 512:(kb % 2) * 512 + 256]
                        ts2 = ktab[:, (kb % 2) * 512 + 256:(kb % 2) * 512 + 512]
                        Av = ropes4[:, 0:256]
                        Bv = ropes4[:, 256:512]
                        Cv = ropes4[:, 512:768]
                        Dv = ropes4[:, 768:1024]
                        E.tensor_mul(out=Av, in0=k1, in1=tc)
                        E.tensor_mul(out=Bv, in0=k2, in1=ts2)
                        E.tensor_mul(out=Cv, in0=k1, in1=ts2)
                        E.tensor_mul(out=Dv, in0=k2, in1=tc)
                        E.tensor_sub(out=k1, in0=Av, in1=Bv)
                        sV.inc(E.tensor_add(out=k2, in0=Cv, in1=Dv))
                        M[f"krope_{half}_{kb}"] = ("V", sV.v)
                        wt(E, M[f"kflip_{half}_{kb}"])
                        kdst = RAP(kT_sb, kb * 128,
                                   [[4 * T, 128], [T, 4], [1, 128]])
                        sV.inc(E.tensor_copy(out=kdst, in_=pb[2][:]))
                        r_done(2, "V")
                        M[f"kTev_{half}_{kb}"] = ("V", sV.v)

                @body("tensor")
                def _(E, half=half):
                    wt(E, M[f"kTev_{half}_31"])
                    wt(E, M[f"vev_{half}_31"])
                    for p in range(NT):
                        t_acquire(E, 3)
                        for j in range(4):
                            jj = 4 * half + j
                            mm = E.matmul(
                                out=pb[3][:, j * 128:(j + 1) * 128],
                                lhsT=q_sb[:, p * D + jj * 128:
                                         p * D + jj * 128 + 128],
                                rhs=ident_sb[:], start=True, stop=True)
                        sT.inc(mm)
                        t_wrote(3)
                        M[f"qTf_{half}_{p}"] = ("T", sT.v)
                        wt(E, M[f"qTe_{half}_{p}"])
                        nkb, bts = batches_of(p)
                        for hw in range(8):
                            jj, hi = hw // 2, hw % 2
                            bav = 6 + (hw % 2)
                            wt(E, av_guard[bav])

                            def av_emit(b, pr, kbs):
                                wt(E, M[f"pTrdy_{half}_{p}_{hw}_{b}"])
                                for i, kb in enumerate(kbs):
                                    mm = E.matmul(
                                        out=pb[bav][:, 0:65],
                                        lhsT=pT_sb[:, pr * 1024 + i * 128:
                                                   pr * 1024 + i * 128 + 128],
                                        rhs=v_sb[:, kb * 520 + hw * 65:
                                                 kb * 520 + hw * 65 + 65],
                                        start=(kb == 0),
                                        stop=(kb == nkb - 1))
                                sT.inc(mm)
                                M[f"avg_{half}_{p}_{hw}_{b}"] = ("T", sT.v)

                            prev = None
                            for b, kbs in bts:
                                pr = bctr[0] % 2
                                bctr[0] += 1
                                wt(E, sc_guard[pr])
                                for i, kb in enumerate(kbs):
                                    mm = E.matmul(
                                        out=scpair[pr][:, i * 128:
                                                       (i + 1) * 128],
                                        lhsT=kT_sb[64 * hi:64 * hi + 64,
                                                   jj * T + kb * 128:
                                                   jj * T + kb * 128 + 128],
                                        rhs=qT_sb[64 * hi:64 * hi + 64,
                                                  jj * 128:jj * 128 + 128],
                                        start=True, stop=True)
                                sT.inc(mm)
                                M[f"sc_{half}_{p}_{hw}_{b}"] = ("T", sT.v)
                                sc_guard[pr] = M[f"exp_{half}_{p}_{hw}_{b}"]
                                if prev is not None:
                                    av_emit(*prev)
                                prev = (b, pr, kbs)
                            av_emit(*prev)
                            t_wrote(bav)
                            M[f"av_{half}_{p}_{hw}"] = ("T", sT.v)
                            av_guard[bav] = M[f"norm_{half}_{p}_{hw}"]
                        M[f"scl_{half}_{p}"] = ("T", sT.v)

                @body("scalar")
                def _(E, half=half):
                    for p in range(NT):
                        nkb, bts = batches_of(p)
                        for hw in range(8):
                            for b, kbs in bts:
                                pr = bctr[1] % 2
                                wt(E, M[f"sc_{half}_{p}_{hw}_{b}"])
                                wt(E, pt_guard[pr])
                                sS.inc(E.activation(
                                    out=pT_sb[:, pr * 1024:
                                              pr * 1024 + len(kbs) * 128],
                                    in_=scpair[pr][:, 0:len(kbs) * 128],
                                    func=ACTF.Exp))
                                M[f"exp_{half}_{p}_{hw}_{b}"] = ("S", sS.v)
                                pt_guard[pr] = M[f"avg_{half}_{p}_{hw}_{b}"]
                                bctr[1] += 1

                @body("vector")
                def _(E, half=half):
                    nonlocal qT_guard
                    for p in range(NT):
                        wt(E, M[f"qTf_{half}_{p}"])
                        wt(E, qT_guard)
                        sV.inc(E.tensor_copy(out=qT_sb[:], in_=pb[3][:]))
                        r_done(3, "V")
                        M[f"qTe_{half}_{p}"] = ("V", sV.v)
                        qT_guard = M[f"scl_{half}_{p}"]
                        nkb, bts = batches_of(p)
                        for hw in range(8):
                            for b, kbs in bts:
                                pr = bctr[2] % 2
                                bctr[2] += 1
                                dia = [i for i, kb in enumerate(kbs)
                                       if kb in (2 * p, 2 * p + 1)]
                                if dia:
                                    wt(E, M[f"exp_{half}_{p}_{hw}_{b}"])
                                    for i in dia:
                                        kb = kbs[i]
                                        mi = kb - 2 * p
                                        sl = pT_sb[:, pr * 1024 + i * 128:
                                                   pr * 1024 + i * 128 + 128]
                                        mm = E.tensor_mul(
                                            out=sl, in0=sl,
                                            in1=masks_sb[:, mi * 128:
                                                         (mi + 1) * 128])
                                    sV.inc(mm)
                                    M[f"pTrdy_{half}_{p}_{hw}_{b}"] = ("V", sV.v)
                                else:
                                    M[f"pTrdy_{half}_{p}_{hw}_{b}"] = \
                                        M[f"exp_{half}_{p}_{hw}_{b}"]
                            bav = 6 + (hw % 2)
                            wt(E, M[f"av_{half}_{p}_{hw}"])
                            E.reciprocal(out=col[:, 20 + hw:21 + hw],
                                         in_=pb[bav][:, 64:65])
                            if hw > 0:
                                pv = 6 + ((hw - 1) % 2)
                                h_abs = 8 * half + hw - 1
                                sV.inc(E.tensor_scalar(
                                    out=attnout[:, p * D + h_abs * 64:
                                                p * D + h_abs * 64 + 64],
                                    in0=pb[pv][:, 0:64],
                                    scalar1=col[:, 19 + hw:20 + hw],
                                    scalar2=None, op0=ALU.mult))
                                r_done(pv, "V")
                                M[f"norm_{half}_{p}_{hw - 1}"] = ("V", sV.v)
                        jnk(E)
                        h_abs = 8 * half + 7
                        sV.inc(E.tensor_scalar(
                            out=attnout[:, p * D + h_abs * 64:
                                        p * D + h_abs * 64 + 64],
                            in0=pb[7][:, 0:64], scalar1=col[:, 27:28],
                            scalar2=None, op0=ALU.mult))
                        r_done(7, "V")
                        M[f"norm_{half}_{p}_7"] = ("V", sV.v)

            barrier()
            sbt_free("latT", "wuk_sb", "wuv_sb", "kT_sb", "v_sb", "ktab",
                     "k_tok", "qT_sb", "pT_sb", "ropes4")

        if "wo" in phases:
            wo_sb = sbt("wo_sb", [128, 8 * D], BF16)
            x1b5 = sbt("x1b5", [128, 2 * D], F32)
            x2b5 = sbt("x2b5", [128, 2 * D], F32)
            aoT = sbt("aoT", [128, D], BF16)

            @body("gpsimd")
            def _(E):
                M["p5w"] = dma(
                    E, "cP5w", wo_sb[:].rearrange("p (i c) -> p i c", i=8),
                    wo_d.rearrange("(i p) c -> p i c", p=128))

            @body("sync")
            def _(E):
                for p in range(NT):
                    if p >= 2:
                        wt(E, M[f"p5res_{p - 2}"])
                    M[f"p5xld_{p}"] = dma(
                        E, "p3A" if p % 2 == 0 else "p3B",
                        x1b5[:, (p % 2) * D:(p % 2 + 1) * D],
                        x1sp[p * 128:(p + 1) * 128, :])

            @body("tensor")
            def _(E):
                wt(E, M["p5w"])
                for p in range(NT):
                    wt(E, M[f"norm_1_{p}_7"])
                    for gfl in range(2):
                        b = 4 + gfl
                        t_acquire(E, b)
                        for j in range(4):
                            mm = E.matmul(
                                out=pb[b][:, j * 128:(j + 1) * 128],
                                lhsT=attnout[:, p * D + gfl * 512 + j * 128:
                                            p * D + gfl * 512 + j * 128 + 128],
                                rhs=ident_sb[:], start=True, stop=True)
                        sT.inc(mm)
                        t_wrote(b)
                        M[f"p5aof_{p}_{gfl}"] = ("T", sT.v)
                    wt(E, M[f"p5aoT_{p}"])
                    for oc2 in range(2):
                        t_acquire(E, oc2)
                        for ib in range(8):
                            mm = E.matmul(
                                out=pb[oc2][:],
                                lhsT=aoT[:, ib * 128:(ib + 1) * 128],
                                rhs=wo_sb[:, ib * D + oc2 * 512:
                                         ib * D + oc2 * 512 + 512],
                                start=(ib == 0), stop=(ib == 7))
                        sT.inc(mm)
                        t_wrote(oc2)
                        M[f"p5mm_{p}_{oc2}"] = ("T", sT.v)

            @body("vector")
            def _(E):
                for p in range(NT):
                    for gfl in range(2):
                        b = 4 + gfl
                        wt(E, M[f"p5aof_{p}_{gfl}"])
                        sV.inc(E.tensor_copy(out=aoT[:, gfl * 512:gfl * 512 + 512],
                                             in_=pb[b][:]))
                        r_done(b, "V")
                    M[f"p5aoT_{p}"] = ("V", sV.v)
                    wt(E, M[f"p5xld_{p}"])
                    for oc2 in range(2):
                        wt(E, M[f"p5mm_{p}_{oc2}"])
                        if p >= 2:
                            wt(E, M[f"p5xst_{p - 2}"])
                        sV.inc(E.tensor_add(
                            out=x2b5[:, (p % 2) * D + oc2 * 512:
                                     (p % 2) * D + oc2 * 512 + 512],
                            in0=pb[oc2][:],
                            in1=x1b5[:, (p % 2) * D + oc2 * 512:
                                     (p % 2) * D + oc2 * 512 + 512]))
                        r_done(oc2, "V")
                    M[f"p5res_{p}"] = ("V", sV.v)

            @body("gpsimd")
            def _(E):
                for p in range(NT):
                    wt(E, M[f"p5res_{p}"])
                    M[f"p5xst_{p}"] = dma(
                        E, "p5A" if p % 2 == 0 else "p5B",
                        x1sp[p * 128:(p + 1) * 128, :],
                        x2b5[:, (p % 2) * D:(p % 2 + 1) * D])

            barrier()
            sbt_free("wo_sb", "x1b5", "x2b5", "aoT")

        if need_ao:
            sbt_free("attnout", "q_sb")

        if "ffn2" in phases:
            emit_ffn(2, x1sp, yout)
            final_is_yout = True
        else:
            final_is_yout = (len(phases) <= 2 and "ffn1" in phases)

        if not final_is_yout:
            cpbuf = sbt("cpbuf", [128, D], F32)

            @body("gpsimd")
            def _(E):
                srcd = x1sp
                for p in range(NT):
                    dma(E, "cp", cpbuf[:], srcd[p * 128:(p + 1) * 128, :])
                    if E is not _FAKE:
                        E.wait_ge(C["cp"].sem, C["cp"].v)
                    dma(E, "cp", yout[p * 128:(p + 1) * 128, :], cpbuf[:])
                    if E is not _FAKE:
                        E.wait_ge(C["cp"].sem, C["cp"].v)
            sbt_free("cpbuf")

        @body("gpsimd")
        def _(E):
            if E is not _FAKE:
                for n in dma_sems:
                    if C[n].v > 0:
                        E.wait_ge(C[n].sem, C[n].v)

        if dbg_d is not None:
            sbt_free("dbg_buf", "dbgc")

    emit(final=False)
    emit(final=True)
    nc._marks = dict(M.m0)
    ctx.close()
    return nc



def _tables(par):
    inv = 10000.0 ** (-(np.arange(32, dtype=np.float64)) / 32.0)
    posk = np.arange(T, dtype=np.float64)
    angk = posk[:, None] * inv[None, :]
    g = np.arange(32) * 2 + par
    posq = (g[:, None] * 64 + np.arange(64)[None, :]).reshape(-1).astype(np.float64)
    angq = posq[:, None] * inv[None, :]
    qcos = np.tile((np.cos(angq) * 0.125).astype(np.float32), (1, 16))
    qsin = np.tile((np.sin(angq) * 0.125).astype(np.float32), (1, 16))
    kcos = np.tile(np.cos(angk).astype(np.float32), (1, 8))
    ksin = np.tile(np.sin(angk).astype(np.float32), (1, 8))
    return (qcos.astype(ml_dtypes.bfloat16), qsin.astype(ml_dtypes.bfloat16),
            kcos.astype(ml_dtypes.bfloat16), ksin.astype(ml_dtypes.bfloat16))


def _masks(par):
    r = np.arange(128)
    kap = np.arange(128)[:, None]
    qpos = np.where(r < 64, 64 * par + r, 64 * par + r + 64)
    m0 = (kap <= qpos[None, :]).astype(np.float32)
    m1 = ((128 + kap) <= qpos[None, :]).astype(np.float32)
    return np.ascontiguousarray(
        np.concatenate([m0, m1], axis=1).astype(ml_dtypes.bfloat16))


_NC_CACHE = {}


def kernel(L0_attn_x, L0_attn_W_q, L0_attn_W_dkv, L0_attn_W_uk, L0_attn_W_uv,
           L0_attn_W_o, L0_attn_ln_gamma, L0_attn_ln_beta, L0_ffn_w_gate,
           L0_ffn_w_up, L0_ffn_w_down):
    x = np.ascontiguousarray(np.asarray(L0_attn_x, dtype=np.float32))
    wq = np.ascontiguousarray(np.asarray(L0_attn_W_q, dtype=np.float32))
    wdkv = np.ascontiguousarray(np.asarray(L0_attn_W_dkv, dtype=np.float32))
    wuk = np.ascontiguousarray(np.asarray(L0_attn_W_uk, dtype=np.float32))
    wuv = np.ascontiguousarray(np.asarray(L0_attn_W_uv, dtype=np.float32))
    wo = np.ascontiguousarray(np.asarray(L0_attn_W_o, dtype=np.float32))
    gam = np.asarray(L0_attn_ln_gamma, dtype=np.float32)
    bet = np.asarray(L0_attn_ln_beta, dtype=np.float32)
    wg = np.ascontiguousarray(np.asarray(L0_ffn_w_gate, dtype=np.float32))
    wu = np.ascontiguousarray(np.asarray(L0_ffn_w_up, dtype=np.float32))
    wd = np.ascontiguousarray(np.asarray(L0_ffn_w_down, dtype=np.float32))

    gam_t = np.ascontiguousarray(np.broadcast_to(gam, (128, LAT)))
    bet_t = np.ascontiguousarray(np.broadcast_to(bet, (128, LAT)))
    ident = np.eye(128, dtype=ml_dtypes.bfloat16)
    magic = np.random.randint(1, 2**31 - 1, size=(1, 2)).astype(np.int32)

    if "nc" not in _NC_CACHE:
        _NC_CACHE["nc"] = build_nc()
    nc = _NC_CACHE["nc"]

    in_maps = []
    for c in range(8):
        b, par = c // 2, c % 2
        xb = np.ascontiguousarray(x[b].reshape(32, 2, 64, D)[:, par]
                                  .reshape(2048, D))
        qcos, qsin, kcos, ksin = _tables(par)
        xoffs = np.array([[64 * par * LAT, par * 2, (1 - par) * 2,
                           0, 0, 0, 0, 0]], dtype=np.int32)
        in_maps.append({
            "xin": xb, "wq": wq, "wdkv": wdkv, "wuk": wuk, "wuv": wuv,
            "wo": wo, "gam": gam_t, "bet": bet_t, "wg": wg, "wu": wu,
            "wd": wd, "qcos": qcos, "qsin": qsin, "kcos": kcos,
            "ksin": ksin, "masks": _masks(par), "ident": ident,
            "xoffs": xoffs, "magic": magic,
        })

    res = run_bass_kernel_spmd(nc, in_maps, core_ids=list(range(8)))

    out = np.empty((B, T, D), dtype=np.float32)
    for c in range(8):
        b, par = c // 2, c % 2
        yc = np.asarray(res.results[c]["yout"])
        out[b].reshape(32, 2, 64, D)[:, par] = yc.reshape(32, 64, D)
    return out


# version 05
# speedup vs baseline: 1.3331x; 1.0862x over previous
from contextlib import ExitStack

import numpy as np
import ml_dtypes

import concourse.bass as bass
import concourse.mybir as mybir
from concourse.bass_utils import run_bass_kernel_spmd
from concourse.bass_isa import ReduceOp

F32 = mybir.dt.float32
BF16 = mybir.dt.bfloat16
I32 = mybir.dt.int32
ALU = mybir.AluOpType
ACTF = mybir.ActivationFunctionType
AX = mybir.AxisListType.X

B, T, D = 4, 4096, 1024
INNER = 4096
LAT = 256
NT = 16
ST = 4
NST = NT // ST
MROUND = 12582912.0
LN_EPS = 1e-5
MAXPOLL = 30_000_000

ALL_PHASES = ("wprep", "ffn1", "qlat", "attn", "wo", "ffn2")


class _Fake:
    def __call__(self, *a, **k):
        return self

    def __getattr__(self, name):
        return self

    def __enter__(self):
        return self

    def __exit__(self, *a):
        return False


_FAKE = _Fake()


class Ctr:
    def __init__(self, name, sem, step=1):
        self.name, self.sem, self.v, self.step = name, sem, 0, step

    def inc(self, inst, n=None):
        n = self.step if n is None else n
        self.v += n
        if inst is not _FAKE:
            inst.then_inc(self.sem, n)
        return self.v


class Marks:

    def __init__(self):
        self.m0 = None
        self.w = {}

    def start_pass(self, final):
        if final:
            self.m0 = dict(self.w)
        self.w = {}

    def __setitem__(self, k, v):
        if self.m0 is not None and k in self.m0:
            assert self.m0[k] == v, f"mark drift {k}: {self.m0[k]} != {v}"
        self.w[k] = v

    def __contains__(self, k):
        return k in self.w

    def __getitem__(self, k):
        if self.m0 is not None:
            return self.m0[k]
        return self.w.get(k, ("none", 0))


def build_nc(phases=ALL_PHASES, dbg=False):
    nc = bass.Bass("TRN2", target_bir_lowering=False, debug=False,
                   detect_race_conditions=False)

    def din(n, s, dt=F32):
        return nc.dram_tensor(n, s, dt, kind="ExternalInput").ap()

    xin = din("xin", [2048, D])
    wq_d = din("wq", [D, D])
    wdkv_d = din("wdkv", [D, LAT])
    wuk_d = din("wuk", [LAT, D])
    wuv_d = din("wuv", [LAT, D])
    wo_d = din("wo", [D, D])
    gam_d = din("gam", [128, LAT])
    bet_d = din("bet", [128, LAT])
    wg_d = din("wg", [INNER, D])
    wu_d = din("wu", [INNER, D])
    wd_d = din("wd", [D, INNER])
    qcos_d = din("qcos", [2048, 512], BF16)
    qsin_d = din("qsin", [2048, 512], BF16)
    kcos_d = din("kcos", [T, 256], BF16)
    ksin_d = din("ksin", [T, 256], BF16)
    masks_d = din("masks", [128, 256], BF16)
    ident_d = din("ident", [128, 128], BF16)
    xoffs_d = din("xoffs", [1, 8], I32)
    magic_d = din("magic", [1, 2], I32)

    yout = nc.dram_tensor("yout", [2048, D], F32, kind="ExternalOutput").ap()
    dbg_d = (nc.dram_tensor("dbg", [128, 16384], F32, kind="ExternalOutput").ap()
             if dbg else None)

    lat_sh = nc.dram_tensor("lat_sh", [T, LAT], BF16, kind="Internal",
                            addr_space="Shared").ap()
    flag_sh = nc.dram_tensor("flag_sh", [2, 2], I32, kind="Internal",
                             addr_space="Shared").ap()
    x1sp = nc.dram_tensor("x1sp", [2048, D], F32, kind="Internal").ap()
    lat_pv = nc.dram_tensor("lat_pv", [2048, LAT], BF16, kind="Internal").ap()
    wgT_s = nc.dram_tensor("wgT_s", [8, 128, INNER], BF16, kind="Internal").ap()
    wuT_s = nc.dram_tensor("wuT_s", [8, 128, INNER], BF16, kind="Internal").ap()
    wdT_s = nc.dram_tensor("wdT_s", [32, 128, D], BF16, kind="Internal").ap()

    ctx = ExitStack()
    e = ctx.enter_context

    ident_sb = e(nc.sbuf_tensor("ident_sb", [128, 128], BF16))
    col = e(nc.sbuf_tensor("col", [128, 40], F32))
    ag_sb = e(nc.sbuf_tensor("ag_sb", [128, NT], F32))
    au_sb = e(nc.sbuf_tensor("au_sb", [128, NT], F32))
    adh_sb = e(nc.sbuf_tensor("adh_sb", [128, NT], F32))
    mclip = e(nc.sbuf_tensor("mclip", [128, 4], F32))
    wrecip = e(nc.sbuf_tensor("wrecip", [128, 4], F32))
    gam_sb = e(nc.sbuf_tensor("gam_sb", [128, LAT], F32))
    bet_sb = e(nc.sbuf_tensor("bet_sb", [128, LAT], F32))
    masks_sb = e(nc.sbuf_tensor("masks_sb", [128, 256], BF16))
    ones_sb = e(nc.sbuf_tensor("ones_sb", [128, 128], F32))
    offs_sb = e(nc.sbuf_tensor("offs_sb", [1, 8], I32))
    magic_sb = e(nc.sbuf_tensor("magic_sb", [1, 2], I32))

    pb03 = e(nc.psum_tensor("pb03", [128, 2048], F32))
    pb45 = e(nc.psum_tensor("pb45", [128, 1024], F32))
    pb6t = e(nc.psum_tensor("pb6", [128, 512], F32))
    pb7t = e(nc.psum_tensor("pb7", [128, 512], F32))
    pb = ([pb03[:, i * 512:(i + 1) * 512] for i in range(4)]
          + [pb45[:, i * 512:(i + 1) * 512] for i in range(2)]
          + [pb6t[:], pb7t[:]])

    sem_T = e(nc.semaphore("sem_T"))
    sem_V = e(nc.semaphore("sem_V"))
    sem_S = e(nc.semaphore("sem_S"))
    sem_G = e(nc.semaphore("sem_G"))
    DMA_SLOTS = ("cP0", "wpA", "wpB", "wpC", "wpD", "wpE", "wsA", "wsB", "fxA", "fxB", "fgA",
                 "fgB", "fd0", "fd1", "fd2", "fd3", "fstA", "fstB", "cP3w",
                 "cQT", "p3A", "p3B", "lst", "flg", "cLT", "cP4w", "ktA",
                 "ktB", "cP5w", "p5A", "p5B", "cp")
    dma_sems = {n: e(nc.semaphore("sem_" + n)) for n in DMA_SLOTS}

    block = e(nc.Block())
    M = Marks()

    sbuf_cache = {}

    def emit(final):
        C = {}
        C["T"] = Ctr("T", sem_T)
        C["V"] = Ctr("V", sem_V)
        C["S"] = Ctr("S", sem_S)
        C["G"] = Ctr("G", sem_G)
        for n, sem in dma_sems.items():
            C[n] = Ctr(n, sem, 16)
        sT, sV, sS, sG = C["T"], C["V"], C["S"], C["G"]
        M.start_pass(final)

        def dma(E, slot, out, in_, **kw):
            inst = E.dma_start(out=out, in_=in_, **kw)
            C[slot].inc(inst)
            return (slot, C[slot].v)

        dbg_col = [0]
        dbg_buf = None

        def tap(E, ap, n):
            if dbg_d is None:
                return
            c0 = dbg_col[0]
            dbg_col[0] += n
            dma(E, "cp", dbg_d[:, c0:c0 + n], ap)
            if E is not _FAKE:
                E.wait_ge(C["cp"].sem, C["cp"].v)

        bank_readers = {i: [] for i in range(8)}
        bank_writer = {i: ("T", 0) for i in range(8)}
        _waited = {}

        def wt(E, mark):
            tag, v = mark
            if tag == "none" or v == 0:
                return
            if E is not _FAKE:
                seen = _waited.setdefault(_cur_eng[0], {})
                if seen.get(tag, -1) >= v:
                    return
                seen[tag] = v
                E.wait_ge(C[tag].sem, v)

        def jnk(E):
            E.tensor_copy(out=col[:, 38:39], in_=ones_sb[:, 0:1])

        def vdrain(E):
            if E is not _FAKE:
                E.wait_ge(sem_V, C["V"].v)

        def t_acquire(E, b):
            pass

        def t_wrote(b):
            bank_writer[b] = ("T", sT.v)

        def r_wait(E, b):
            wt(E, bank_writer[b])

        def r_done(b, tag):
            pass

        _cur_eng = [None]

        def body(engname):
            def deco(fn):
                if final:
                    def wrapped(E, fn=fn, engname=engname):
                        _cur_eng[0] = engname
                        fn(E)
                    getattr(block, engname)(wrapped)
                else:
                    fn(_FAKE)
            return deco

        def RAP(tensor, off, pat):
            if not final:
                return None
            return bass.AP(tensor, off, pat)

        def sbt(name, shape, dt):
            if name not in sbuf_cache:
                guard = nc.sbuf_tensor(name, shape, dt)
                handle = guard.__enter__()
                sbuf_cache[name] = (guard, handle)
            return sbuf_cache[name][1]

        if dbg_d is not None:
            dbg_buf = sbt("dbg_buf", [128, D], BF16)
            dbgc = sbt("dbgc", [128, 32], F32)

        def sbt_free(*names):
            if final:
                return
            for name in reversed(names):
                guard, _ = sbuf_cache[name]
                guard.__exit__(None, None, None)

        def barrier():
            vals = [(k, c.v) for k, c in C.items()]

            def waits(E, skip):
                for k, v in vals:
                    if k != skip and v > 0:
                        wt(E, (k, v))

            for engname, skip in (("tensor", "T"), ("vector", "V"),
                                  ("scalar", "S"), ("gpsimd", "G"),
                                  ("sync", None)):
                @body(engname)
                def _(E, skip=skip):
                    waits(E, skip)

        @body("sync")
        def _(E):
            for dst, srct in ((ident_sb, ident_d), (gam_sb, gam_d),
                              (bet_sb, bet_d), (masks_sb, masks_d),
                              (offs_sb, xoffs_d), (magic_sb, magic_d)):
                mk = dma(E, "cP0", dst[:], srct[:])
            M["p0"] = mk

        @body("vector")
        def _(E):
            sV.inc(E.memset(ones_sb[:], 1.0))
            M["ones"] = ("V", sV.v)

        if "wprep" in phases:
            wp_f32 = sbt("wp_f32", [128, 4 * INNER], F32)
            wp_i32 = sbt("wp_i32", [128, INNER], F32)
            wp_t = sbt("wp_t", [128, INNER], F32)
            wp_m = sbt("wp_m", [128, 4 * INNER], BF16)
            wp_st = sbt("wp_st", [128, 2 * 512], BF16)
            wp_acc = sbt("wp_acc", [128, 4], F32)
            wp_tmp = sbt("wp_tmp", [128, 4], F32)
            wp_strip = sbt("wp_strip", [128, 72], F32)

            WCFG = [("g", wg_d, wgT_s, 32, D),
                    ("u", wu_d, wuT_s, 32, D),
                    ("d", wd_d, wdT_s, 8, INNER)]
            CHUNKS = [(wi, tag, wdram, ch, K)
                      for wi, (tag, wdram, _w, nch, K) in enumerate(WCFG)
                      for ch in range(nch)]

            WPSLOT = ("wpA", "wpB", "wpC", "wpD")
            WPENG = ("scalar", "sync", "gpsimd", "sync")

            for _eng in ("scalar", "sync", "gpsimd"):
                @body(_eng)
                def _(E, _eng=_eng):
                    for gi, (wi, tag, wdram, ch, K) in enumerate(CHUNKS):
                        sl = gi % 4
                        if WPENG[sl] != _eng:
                            continue
                        if gi >= 4:
                            wt(E, M[f"wp1v_{gi - 4}"])
                        M[f"wp1ld_{gi}"] = dma(
                            E, WPSLOT[sl],
                            wp_f32[:, sl * INNER:sl * INNER + K],
                            wdram[ch * 128:(ch + 1) * 128, :])

            @body("vector")
            def _(E):
                for gi, (wi, tag, wdram, ch, K) in enumerate(CHUNKS):
                    sl = gi % 4
                    wt(E, M[f"wp1ld_{gi}"])
                    src = wp_f32[:, sl * INNER:sl * INNER + K]
                    sV.inc(E.reduce_sum(
                        out=wp_strip[:, gi:gi + 1], in_=src, axis=AX,
                        apply_absolute_value=True))
                    M[f"wp1v_{gi}"] = ("V", sV.v)
                E.reduce_sum(out=wp_acc[:, 0:1], in_=wp_strip[:, 0:32], axis=AX)
                E.reduce_sum(out=wp_acc[:, 1:2], in_=wp_strip[:, 32:64], axis=AX)
                sV.inc(E.reduce_sum(out=wp_acc[:, 2:3], in_=wp_strip[:, 64:72],
                                    axis=AX))
                M["wp1_done"] = ("V", sV.v)

            @body("tensor")
            def _(E):
                wt(E, M["wp1_done"])
                wt(E, M["ones"])
                t_acquire(E, 6)
                sT.inc(E.matmul(out=pb[6][0:1, 0:3], lhsT=ones_sb[:, 0:1],
                                rhs=wp_acc[:, 0:3], start=True, stop=True))
                t_wrote(6)
                M["wp_sum"] = ("T", sT.v)

            @body("vector")
            def _(E):
                wt(E, M["wp_sum"])
                sV.inc(E.tensor_copy(out=wp_tmp[0:1, 0:3], in_=pb[6][0:1, 0:3]))
                r_done(6, "V")
                M["wp_sumrow"] = ("V", sV.v)

            @body("tensor")
            def _(E):
                wt(E, M["wp_sumrow"])
                t_acquire(E, 6)
                sT.inc(E.matmul(out=pb[6][:, 0:3], lhsT=ones_sb[0:1, 0:128],
                                rhs=wp_tmp[0:1, 0:3], start=True, stop=True))
                t_wrote(6)
                M["wp_bc"] = ("T", sT.v)

            @body("vector")
            def _(E):
                wt(E, M["wp_bc"])
                E.tensor_scalar(out=wp_tmp[:, 0:3], in0=pb[6][:, 0:3],
                                scalar1=1.0 / 4194304.0, scalar2=1e-5,
                                op0=ALU.mult, op1=ALU.max)
                r_done(6, "V")
                E.tensor_copy(out=col[:, 38:39], in_=ones_sb[:, 0:1])
                E.tensor_copy(out=mclip[:, 0:3], in_=wp_tmp[:, 0:3])
                sV.inc(E.reciprocal(out=wrecip[:, 0:3], in_=wp_tmp[:, 0:3]))
                M["wscale"] = ("V", sV.v)

            if dbg_d is not None:
                @body("gpsimd")
                def _(E):
                    wt(E, M["wscale"])
                    tap(E, mclip[:, 0:3], 3)
                    tap(E, wrecip[:, 0:3], 3)

            WPENG2 = ("scalar", "sync", "scalar", "sync")
            WPSLOT2 = ("wpA", "wpB", "wpE", "wpD")
            for _eng in ("scalar", "sync"):
                @body(_eng)
                def _(E, _eng=_eng):
                    for gi, (wi, tag, wdram, ch, K) in enumerate(CHUNKS):
                        sl = gi % 4
                        if WPENG2[sl] != _eng:
                            continue
                        if gi >= 4:
                            wt(E, M[f"wp2q_{gi - 4}"])
                        else:
                            lg = len(CHUNKS) - 1
                            while lg % 4 != sl:
                                lg -= 1
                            wt(E, M[f"wp1v_{lg}"])
                        M[f"wp2ld_{gi}"] = dma(
                            E, WPSLOT2[sl],
                            wp_f32[:, sl * INNER:sl * INNER + K],
                            wdram[ch * 128:(ch + 1) * 128, :])

            GRPS = []
            _off = 0
            for wi, (tag, wdram, wscr, nch, K) in enumerate(WCFG):
                for grp in range(nch // 4):
                    GRPS.append((wi, wscr, grp, K // 128, _off + grp * 4))
                _off += nch

            for gg, (wi, wscr, grp, nib, gi0) in enumerate(GRPS):
                @body("vector")
                def _(E, gg=gg, wi=wi, grp=grp, nib=nib, gi0=gi0):
                    if gg == 0:
                        wt(E, M["wscale"])
                    for g4 in range(4):
                        gi = gi0 + g4
                        K = CHUNKS[gi][4]
                        sl = gi % 4
                        wt(E, M[f"wp2ld_{gi}"])
                        if g4 == 0 and gg >= 1:
                            wt(E, M[f"wpflips_{gg - 1}"])
                        srcw = wp_f32[:, sl * INNER:sl * INNER + K]
                        nC = K // 512
                        stages = []
                        for cc in range(nC):
                            stages.append(("a", cc))
                            if cc >= 1:
                                stages.append(("b", cc - 1))
                            if cc >= 2:
                                stages.append(("c", cc - 2))
                        stages += [("b", nC - 1), ("c", nC - 2), ("c", nC - 1)]
                        for st_, cc in stages:
                            lo, hi = cc * 512, (cc + 1) * 512
                            if st_ == "a":
                                inst = E.tensor_scalar(
                                    out=wp_i32[:, lo:hi], in0=srcw[:, lo:hi],
                                    scalar1=wrecip[:, wi:wi + 1],
                                    scalar2=MROUND, op0=ALU.mult, op1=ALU.add)
                            elif st_ == "b":
                                inst = E.tensor_scalar(
                                    out=wp_t[:, lo:hi], in0=wp_i32[:, lo:hi],
                                    scalar1=-MROUND, scalar2=1.0,
                                    op0=ALU.add, op1=ALU.min)
                            else:
                                inst = E.tensor_scalar(
                                    out=wp_m[:, g4 * INNER + lo:
                                             g4 * INNER + hi],
                                    in0=wp_t[:, lo:hi], scalar1=-1.0,
                                    scalar2=None, op0=ALU.max)
                        sV.inc(inst)
                        M[f"wp2q_{gi}"] = ("V", sV.v)

                @body("tensor")
                def _(E, gg=gg, nib=nib, gi0=gi0):
                    wt(E, M[f"wp2q_{gi0 + 3}"])
                    for ib in range(nib):
                        b = 4 + (ib % 2)
                        if ib >= 2:
                            wt(E, M[f"wpev_{gg}_{ib - 2}"])
                        elif gg > 0:
                            pnib = GRPS[gg - 1][3]
                            wt(E, M[f"wpev_{gg - 1}_{pnib - 2 + ib}"])
                        for j in range(4):
                            mm = E.matmul(
                                out=pb[b][:, j * 128:(j + 1) * 128],
                                lhsT=wp_m[:, j * INNER + ib * 128:
                                          j * INNER + ib * 128 + 128],
                                rhs=ident_sb[:], start=True, stop=True)
                        sT.inc(mm)
                        M[f"wpflip_{gg}_{ib}"] = ("T", sT.v)
                    M[f"wpflips_{gg}"] = ("T", sT.v)

                @body("vector")
                def _(E, gg=gg, nib=nib):
                    for ib in range(nib):
                        b = 4 + (ib % 2)
                        wt(E, M[f"wpflip_{gg}_{ib}"])
                        if ib >= 2:
                            wt(E, M[f"wpstd_{gg}_{ib - 2}"])
                        elif gg > 0:
                            wt(E, M[f"wpstd_{gg - 1}_last{ib}"])
                        sV.inc(E.tensor_copy(
                            out=wp_st[:, (ib % 2) * 512:(ib % 2) * 512 + 512],
                            in_=pb[b][:]))
                        M[f"wpev_{gg}_{ib}"] = ("V", sV.v)

                @body("gpsimd")
                def _(E, gg=gg, wscr=wscr, grp=grp, nib=nib):
                    for ib in range(nib):
                        wt(E, M[f"wpev_{gg}_{ib}"])
                        M[f"wpstd_{gg}_{ib}"] = dma(
                            E, "wsA" if ib % 2 == 0 else "wsB",
                            wscr[ib, :, grp * 512:(grp + 1) * 512],
                            wp_st[:, (ib % 2) * 512:(ib % 2) * 512 + 512])
                        if ib == nib - 2:
                            M[f"wpstd_{gg}_last0"] = M[f"wpstd_{gg}_{ib}"]
                        if ib == nib - 1:
                            M[f"wpstd_{gg}_last1"] = M[f"wpstd_{gg}_{ib}"]

            @body("gpsimd")
            def _(E):
                M["wprep_done"] = M[f"wpstd_{len(GRPS) - 1}_{31}"]

            barrier()
            sbt_free("wp_f32", "wp_i32", "wp_t", "wp_m", "wp_st", "wp_acc",
                     "wp_tmp", "wp_strip")

        def emit_ffn(fid, src_dram, dst_dram):
            pref = f"f{fid}"
            x_st = sbt(f"{pref}x", [128, 2 * ST * D], F32)
            h_sb = sbt(f"{pref}h", [128, ST * INNER], F32)
            hqT = sbt(f"{pref}hqT", [128, ST * INNER], BF16)
            xqT = sbt(f"{pref}xqT", [128, ST * D], BF16)
            wbg = sbt(f"{pref}wbg", [128, 2 * INNER], BF16)
            wbu = sbt(f"{pref}wbu", [128, 2 * INNER], BF16)
            wbd = sbt(f"{pref}wbd", [128, 4 * D], BF16)
            scr = sbt(f"{pref}scr", [128, D], F32)
            hq_bf = sbt(f"{pref}hqb", [128, INNER], BF16)
            xq_bf = sbt(f"{pref}xqb", [128, D], BF16)
            gate = sbt(f"{pref}gate", [128, 512], F32)

            @body("sync")
            def _(E):
                for s in range(NST):
                    if s >= 2:
                        wt(E, M[f"{pref}res_{s - 2}"])
                        wt(E, M[f"{pref}xst_{s - 2}"])
                    M[f"{pref}xld_{s}"] = dma(
                        E, "fxA" if s % 2 == 0 else "fxB",
                        x_st[:, (s % 2) * ST * D:(s % 2 + 1) * ST * D],
                        src_dram[s * ST * 128:(s + 1) * ST * 128, :])
                    for oc in range(8):
                        half = oc % 2
                        if s > 0 or oc >= 2:
                            po = oc - 2 if oc >= 2 else 6 + oc
                            ps = s if oc >= 2 else s - 1
                            wt(E, M[f"{pref}gu_done_{ps}_{po}"])
                        for wscr, buf in ((wgT_s, wbg), (wuT_s, wbu)):
                            mk = dma(
                                E, "fgA" if half == 0 else "fgB",
                                buf[:, half * INNER:(half + 1) * INNER]
                                .rearrange("p (i c) -> p i c", i=8),
                                wscr[:, :, oc * 512:(oc + 1) * 512]
                                .rearrange("i p c -> p i c"))
                        M[f"{pref}wgu_{s}_{oc}"] = mk

            for s in range(NST):
                xoff = (s % 2) * ST * D

                if dbg_d is not None and fid == 1 and s == 0:
                    @body("gpsimd")
                    def _(E):
                        wt(E, M[f"{pref}sil_0_0_0"])
                        tap(E, gate[:], 512)
                        M["dbg_gate_done"] = ("cp", C["cp"].v)
                        wt(E, M[f"{pref}hcmb_0_0_0"])
                        tap(E, h_sb[:, 0:512], 512)
                        M["dbg_h_done"] = ("cp", C["cp"].v)
                        wt(E, M["dbghq"])
                        tap(E, dbg_buf[:, 0:D], D)
                        tap(E, dbgc[:, 0:3], 3)
                        M["dbg_hq_done"] = ("cp", C["cp"].v)

                @body("gpsimd")
                def _(E, s=s):
                    for ib in range(32):
                        if s > 0 or ib >= 4:
                            pi = ib - 4 if ib >= 4 else 28 + ib
                            ps = s if ib >= 4 else s - 1
                            wt(E, M[f"{pref}dn_{ps}_{pi}"])
                        M[f"{pref}wdl_{s}_{ib}"] = dma(
                            E, f"fd{ib % 4}",
                            wbd[:, (ib % 4) * D:(ib % 4 + 1) * D],
                            wdT_s[ib, :, :])

                @body("vector")
                def _(E, s=s, xoff=xoff):
                    wt(E, M[f"{pref}xld_{s}"])
                    if fid == 1 and s == 0 and "wprep" in phases:
                        wt(E, M["wscale"])
                    for ti in range(ST):
                        p = s * ST + ti
                        xt = x_st[:, xoff + ti * D:xoff + (ti + 1) * D]
                        cs = (ti % 2) * 16
                        E.reduce_sum(out=col[:, cs:cs + 1], in_=xt, axis=AX)
                        jnk(E)
                        sV.inc(E.tensor_scalar(
                            out=col[:, cs + 1:cs + 2], in0=col[:, cs:cs + 1],
                            scalar1=-1.0 / D, scalar2=None, op0=ALU.mult))
                        M[f"{pref}mneg_{p}"] = ("V", sV.v)
                        wt(E, M[f"{pref}sq_{p}"])
                        E.tensor_scalar(out=col[:, cs + 2:cs + 3],
                                        in0=col[:, cs + 3:cs + 4],
                                        scalar1=1.0 / D, scalar2=LN_EPS,
                                        op0=ALU.mult, op1=ALU.add)
                        sV.inc(E.reduce_max(out=col[:, cs + 4:cs + 5],
                                            in_=scr[:, 0:D], axis=AX))
                        M[f"{pref}vm_{p}"] = ("V", sV.v)
                        wt(E, M[f"{pref}sqrt_{p}"])
                        E.reciprocal(out=col[:, cs + 7:cs + 8],
                                     in_=col[:, cs + 5:cs + 6])
                        E.tensor_scalar(out=col[:, cs + 12:cs + 13],
                                        in0=col[:, cs + 6:cs + 7],
                                        scalar1=mclip[:, 0:1],
                                        scalar2=1.0 / 127.0,
                                        op0=ALU.mult, op1=ALU.mult)
                        E.tensor_scalar(out=col[:, cs + 13:cs + 14],
                                        in0=col[:, cs + 6:cs + 7],
                                        scalar1=mclip[:, 1:2],
                                        scalar2=1.0 / 127.0,
                                        op0=ALU.mult, op1=ALU.mult)
                        E.tensor_mul(out=col[:, cs + 8:cs + 9],
                                     in0=col[:, cs + 7:cs + 8],
                                     in1=col[:, cs + 6:cs + 7])
                        E.tensor_mul(out=ag_sb[:, p:p + 1],
                                     in0=col[:, cs + 12:cs + 13],
                                     in1=col[:, cs + 7:cs + 8])
                        E.tensor_mul(out=au_sb[:, p:p + 1],
                                     in0=col[:, cs + 13:cs + 14],
                                     in1=col[:, cs + 7:cs + 8])
                        E.reciprocal(out=col[:, cs + 9:cs + 10],
                                     in_=col[:, cs + 8:cs + 9])
                        jnk(E)
                        E.tensor_scalar(out=col[:, cs + 9:cs + 10],
                                        in0=col[:, cs + 9:cs + 10],
                                        scalar1=127.0, scalar2=1.27e7,
                                        op0=ALU.mult, op1=ALU.min)
                        jnk(E)
                        E.tensor_mul(out=col[:, cs + 10:cs + 11],
                                     in0=col[:, cs + 9:cs + 10],
                                     in1=col[:, cs + 7:cs + 8])
                        jnk(E)
                        E.tensor_mul(out=col[:, cs + 11:cs + 12],
                                     in0=col[:, cs + 1:cs + 2],
                                     in1=col[:, cs + 10:cs + 11])
                        jnk(E)
                        if p > 0:
                            wt(E, M[f"{pref}xqf_{p - 1}"])
                        E.tensor_scalar(out=scr[:, 0:512],
                                        in0=xt[:, 0:512],
                                        scalar1=col[:, cs + 10:cs + 11],
                                        scalar2=col[:, cs + 11:cs + 12],
                                        op0=ALU.mult, op1=ALU.add)
                        E.tensor_scalar(out=scr[:, 512:1024],
                                        in0=xt[:, 512:1024],
                                        scalar1=col[:, cs + 10:cs + 11],
                                        scalar2=col[:, cs + 11:cs + 12],
                                        op0=ALU.mult, op1=ALU.add)
                        E.tensor_scalar(out=xq_bf[:, 0:512],
                                        in0=scr[:, 0:512],
                                        scalar1=MROUND, scalar2=MROUND,
                                        op0=ALU.add, op1=ALU.subtract)
                        sV.inc(E.tensor_scalar(out=xq_bf[:, 512:1024],
                                               in0=scr[:, 512:1024],
                                               scalar1=MROUND, scalar2=MROUND,
                                               op0=ALU.add, op1=ALU.subtract))
                        M[f"{pref}xq_{p}"] = ("V", sV.v)
                        for gfl in range(2):
                            b = 4 + gfl
                            wt(E, M[f"{pref}xqfg_{p}_{gfl}"])
                            sV.inc(E.tensor_copy(
                                out=xqT[:, ti * D + gfl * 512:
                                        ti * D + gfl * 512 + 512],
                                in_=pb[b][:]))
                            M[f"{pref}xqTe_{p}_{gfl}"] = ("V", sV.v)
                        M[f"{pref}xqT_{p}"] = ("V", sV.v)


                @body("scalar")
                def _(E, s=s, xoff=xoff):
                    for ti in range(ST):
                        p = s * ST + ti
                        cs = (ti % 2) * 16
                        xt = x_st[:, xoff + ti * D:xoff + (ti + 1) * D]
                        wt(E, M[f"{pref}mneg_{p}"])
                        sS.inc(E.activation(out=scr[:, 0:D], in_=xt,
                                            func=ACTF.Square,
                                            bias=col[:, cs + 1:cs + 2],
                                            scale=1.0,
                                            accum_out=col[:, cs + 3:cs + 4]))
                        M[f"{pref}sq_{p}"] = ("S", sS.v)
                        wt(E, M[f"{pref}vm_{p}"])
                        E.activation(out=col[:, cs + 5:cs + 6],
                                     in_=col[:, cs + 2:cs + 3], func=ACTF.Sqrt)
                        sS.inc(E.activation(out=col[:, cs + 6:cs + 7],
                                            in_=col[:, cs + 4:cs + 5],
                                            func=ACTF.Sqrt))
                        M[f"{pref}sqrt_{p}"] = ("S", sS.v)

                @body("tensor")
                def _(E, s=s):
                    for ti in range(ST):
                        p = s * ST + ti
                        wt(E, M[f"{pref}xq_{p}"])
                        if ti == 0 and s > 0:
                            wt(E, M[f"{pref}res_{s - 1}"])
                        for gfl in range(2):
                            b = 4 + gfl
                            if p > 0:
                                wt(E, M[f"{pref}xqTe_{p - 1}_{gfl}"])
                            for j in range(4):
                                mm = E.matmul(
                                    out=pb[b][:, j * 128:(j + 1) * 128],
                                    lhsT=xq_bf[:, gfl * 512 + j * 128:
                                               gfl * 512 + j * 128 + 128],
                                    rhs=ident_sb[:], start=True, stop=True)
                            sT.inc(mm)
                            t_wrote(b)
                            M[f"{pref}xqfg_{p}_{gfl}"] = ("T", sT.v)
                        M[f"{pref}xqf_{p}"] = ("T", sT.v)

                @body("tensor")
                def _(E, s=s):
                    for oc in range(8):
                        half = oc % 2
                        wt(E, M[f"{pref}wgu_{s}_{oc}"])
                        for ti in range(ST):
                            p = s * ST + ti
                            if oc == 0:
                                wt(E, M[f"{pref}xqT_{p}"])
                            bg, bu = (ti % 2) * 2, (ti % 2) * 2 + 1
                            if ti >= 2:
                                wt(E, M[f"{pref}sil_{s}_{oc}_{ti - 2}"])
                                wt(E, M[f"{pref}hcmb_{s}_{oc}_{ti - 2}"])
                            elif oc >= 1:
                                wt(E, M[f"{pref}sil_{s}_{oc - 1}_{ti + 2}"])
                                wt(E, M[f"{pref}hcmb_{s}_{oc - 1}_{ti + 2}"])
                            t_acquire(E, bg)
                            for ib in range(8):
                                mm = E.matmul(
                                    out=pb[bg][:],
                                    lhsT=xqT[:, ti * D + ib * 128:
                                             ti * D + ib * 128 + 128],
                                    rhs=wbg[:, half * INNER + ib * 512:
                                            half * INNER + ib * 512 + 512],
                                    start=(ib == 0), stop=(ib == 7))
                            sT.inc(mm)
                            t_wrote(bg)
                            M[f"{pref}g_{s}_{oc}_{ti}"] = ("T", sT.v)
                            t_acquire(E, bu)
                            for ib in range(8):
                                mm = E.matmul(
                                    out=pb[bu][:],
                                    lhsT=xqT[:, ti * D + ib * 128:
                                             ti * D + ib * 128 + 128],
                                    rhs=wbu[:, half * INNER + ib * 512:
                                            half * INNER + ib * 512 + 512],
                                    start=(ib == 0), stop=(ib == 7))
                            sT.inc(mm)
                            t_wrote(bu)
                            M[f"{pref}u_{s}_{oc}_{ti}"] = ("T", sT.v)
                        M[f"{pref}gu_done_{s}_{oc}"] = ("T", sT.v)

                @body("scalar")
                def _(E, s=s):
                    for oc in range(8):
                        for ti in range(ST):
                            p = s * ST + ti
                            bg = (ti % 2) * 2
                            wt(E, M[f"{pref}g_{s}_{oc}_{ti}"])
                            if (dbg_d is not None and fid == 1 and s == 0
                                    and oc == 0 and ti == 1):
                                wt(E, M["dbg_gate_done"])
                            if oc > 0 or ti > 0:
                                pti = ti - 1 if ti > 0 else ST - 1
                                poc = oc if ti > 0 else oc - 1
                                wt(E, M[f"{pref}hcmb_{s}_{poc}_{pti}"])
                            sS.inc(E.activation(out=gate[:], in_=pb[bg][:],
                                                func=ACTF.Silu,
                                                scale=ag_sb[:, p:p + 1]))
                            r_done(bg, "S")
                            M[f"{pref}sil_{s}_{oc}_{ti}"] = ("S", sS.v)

                @body("vector")
                def _(E, s=s):
                    for oc in range(8):
                        for ti in range(ST):
                            p = s * ST + ti
                            bu = (ti % 2) * 2 + 1
                            wt(E, M[f"{pref}sil_{s}_{oc}_{ti}"])
                            wt(E, M[f"{pref}u_{s}_{oc}_{ti}"])
                            sV.inc(E.scalar_tensor_tensor(
                                out=h_sb[:, ti * INNER + oc * 512:
                                         ti * INNER + oc * 512 + 512],
                                in0=pb[bu][:], scalar=au_sb[:, p:p + 1],
                                in1=gate[:], op0=ALU.mult, op1=ALU.mult))
                            r_done(bu, "V")
                            M[f"{pref}hcmb_{s}_{oc}_{ti}"] = ("V", sV.v)

                for ti in range(ST):
                    @body("vector")
                    def _(E, s=s, ti=ti):
                        p = s * ST + ti
                        wt(E, M[f"{pref}hcmb_{s}_7_{ti}"])
                        if (dbg_d is not None and fid == 1 and s == 0
                                and ti == 0):
                            wt(E, M["dbg_h_done"])
                        ht = h_sb[:, ti * INNER:(ti + 1) * INNER]
                        E.reduce_max(out=col[:, 0:1], in_=ht, axis=AX,
                                     apply_absolute_value=True)
                        jnk(E)
                        E.reciprocal(out=col[:, 1:2], in_=col[:, 0:1])
                        E.tensor_scalar(out=adh_sb[:, p:p + 1],
                                        in0=col[:, 0:1],
                                        scalar1=mclip[:, 2:3],
                                        scalar2=0.5 / 127.0,
                                        op0=ALU.mult, op1=ALU.mult)
                        E.tensor_scalar(out=col[:, 1:2], in0=col[:, 1:2],
                                        scalar1=127.0, scalar2=1.27e7,
                                        op0=ALU.mult, op1=ALU.min)
                        jnk(E)
                        if ti > 0:
                            wt(E, M[f"{pref}hqf_{s}_{ti - 1}"])
                        elif s > 0:
                            wt(E, M[f"{pref}hqf_{s - 1}_{ST - 1}"])
                        if (dbg_d is not None and fid == 1 and s == 0
                                and ti == 1):
                            wt(E, M["dbg_hq_done"])
                        for cc in range(8):
                            E.tensor_scalar(
                                out=ht[:, cc * 512:(cc + 1) * 512],
                                in0=ht[:, cc * 512:(cc + 1) * 512],
                                scalar1=col[:, 1:2], scalar2=None,
                                op0=ALU.mult)
                            if cc >= 1:
                                E.tensor_scalar(
                                    out=hq_bf[:, (cc - 1) * 512:cc * 512],
                                    in0=ht[:, (cc - 1) * 512:cc * 512],
                                    scalar1=MROUND, scalar2=MROUND,
                                    op0=ALU.add, op1=ALU.subtract)
                        sV.inc(E.tensor_scalar(out=hq_bf[:, 7 * 512:8 * 512],
                                               in0=ht[:, 7 * 512:8 * 512],
                                               scalar1=MROUND, scalar2=MROUND,
                                               op0=ALU.add, op1=ALU.subtract))
                        M[f"{pref}hq_{s}_{ti}"] = ("V", sV.v)
                        if (dbg_d is not None and fid == 1 and s == 0
                                and ti == 0):
                            E.tensor_copy(out=dbg_buf[:, 0:D],
                                          in_=hq_bf[:, 0:D])
                            E.tensor_copy(out=dbgc[:, 0:2], in_=col[:, 0:2])
                            sV.inc(E.tensor_copy(out=dbgc[:, 2:3],
                                                 in_=adh_sb[:, 0:1]))
                            M["dbghq"] = ("V", sV.v)

                    @body("tensor")
                    def _(E, s=s, ti=ti):
                        wt(E, M[f"{pref}hq_{s}_{ti}"])
                        for gfl in range(8):
                            b = 4 + (gfl % 2)
                            if gfl >= 2:
                                wt(E, M[f"{pref}hqTe_{s}_{ti}_{gfl - 2}"])
                            elif ti > 0:
                                wt(E, M[f"{pref}hqTe_{s}_{ti - 1}_{6 + gfl}"])
                            else:
                                wt(E, M[f"{pref}xqTe_{s * ST + ST - 1}_{gfl}"])
                            for j in range(4):
                                mm = E.matmul(
                                    out=pb[b][:, j * 128:(j + 1) * 128],
                                    lhsT=hq_bf[:, gfl * 512 + j * 128:
                                               gfl * 512 + j * 128 + 128],
                                    rhs=ident_sb[:], start=True, stop=True)
                            sT.inc(mm)
                            t_wrote(b)
                            M[f"{pref}hqfg_{s}_{ti}_{gfl}"] = ("T", sT.v)
                        M[f"{pref}hqf_{s}_{ti}"] = ("T", sT.v)

                    @body("vector")
                    def _(E, s=s, ti=ti):
                        for gfl in range(8):
                            b = 4 + (gfl % 2)
                            wt(E, M[f"{pref}hqfg_{s}_{ti}_{gfl}"])
                            sV.inc(E.tensor_copy(
                                out=hqT[:, ti * INNER + gfl * 512:
                                        ti * INNER + gfl * 512 + 512],
                                in_=pb[b][:]))
                            M[f"{pref}hqTe_{s}_{ti}_{gfl}"] = ("V", sV.v)
                        M[f"{pref}hqT_{s}_{ti}"] = ("V", sV.v)

                @body("tensor")
                def _(E, s=s):
                    for ti in range(ST):
                        wt(E, M[f"{pref}hqT_{s}_{ti}"])
                    for b in range(8):
                        t_acquire(E, b)
                    for ib in range(32):
                        wt(E, M[f"{pref}wdl_{s}_{ib}"])
                        sl = ib % 4
                        for ti in range(ST):
                            for og in range(2):
                                mm = E.matmul(
                                    out=pb[ti * 2 + og][:],
                                    lhsT=hqT[:, ti * INNER + ib * 128:
                                             ti * INNER + ib * 128 + 128],
                                    rhs=wbd[:, sl * D + og * 512:
                                            sl * D + og * 512 + 512],
                                    start=(ib == 0), stop=(ib == 31))
                        sT.inc(mm)
                        M[f"{pref}dn_{s}_{ib}"] = ("T", sT.v)
                    for b in range(8):
                        t_wrote(b)
                    M[f"{pref}dn_done_{s}"] = ("T", sT.v)

                @body("vector")
                def _(E, s=s, xoff=xoff):
                    wt(E, M[f"{pref}dn_done_{s}"])
                    for ti in range(ST):
                        p = s * ST + ti
                        for og in range(2):
                            xo = xoff + ti * D + og * 512
                            sV.inc(E.scalar_tensor_tensor(
                                out=x_st[:, xo:xo + 512],
                                in0=pb[ti * 2 + og][:],
                                scalar=adh_sb[:, p:p + 1],
                                in1=x_st[:, xo:xo + 512],
                                op0=ALU.mult, op1=ALU.add))
                            r_done(ti * 2 + og, "V")
                    M[f"{pref}res_{s}"] = ("V", sV.v)

                @body("gpsimd")
                def _(E, s=s, xoff=xoff):
                    wt(E, M[f"{pref}res_{s}"])
                    M[f"{pref}xst_{s}"] = dma(
                        E, "fstA" if s % 2 == 0 else "fstB",
                        dst_dram[s * ST * 128:(s + 1) * ST * 128, :],
                        x_st[:, xoff:xoff + ST * D])

            barrier()
            sbt_free(f"{pref}x", f"{pref}h", f"{pref}hqT", f"{pref}xqT",
                     f"{pref}wbg", f"{pref}wbu", f"{pref}wbd", f"{pref}scr",
                     f"{pref}hqb", f"{pref}xqb", f"{pref}gate")

        if "ffn1" in phases:
            dst1 = x1sp if len(phases) > 2 else yout
            emit_ffn(1, xin, dst1)

        need_ao = ("qlat" in phases)
        if need_ao:
            attnout = sbt("attnout", [128, NT * D], BF16)
            q_sb = sbt("q_sb", [128, NT * D], BF16)

            qtab_c = sbt("qtab_c", [128, NT * 512], BF16)
            qtab_s = sbt("qtab_s", [128, NT * 512], BF16)
            wq_sb = sbt("wq_sb", [128, 8 * D], BF16)
            wdkv_sb = sbt("wdkv_sb", [128, 8 * LAT], BF16)
            x1b = sbt("x1b", [128, 2 * D], F32)
            xa_bf = sbt("xa_bf", [128, D], BF16)
            xaT = sbt("xaT", [128, D], BF16)
            ropes = sbt("ropes", [128, 4 * 512], BF16)
            lat_f = sbt("lat_f", [128, LAT], F32)
            lat_bf = sbt("lat_bf", [128, LAT], BF16)
            scr3 = sbt("scr3", [128, D], F32)

            @body("gpsimd")
            def _(E):
                dma(E, "cP3w", wq_sb[:].rearrange("p (i c) -> p i c", i=8),
                    wq_d.rearrange("(i p) c -> p i c", p=128))
                M["p3w"] = dma(
                    E, "cP3w", wdkv_sb[:].rearrange("p (i c) -> p i c", i=8),
                    wdkv_d.rearrange("(i p) c -> p i c", p=128))

            @body("sync")
            def _(E):
                dma(E, "cQT", qtab_c[:].rearrange("r (p c) -> r p c", p=NT),
                    qcos_d.rearrange("(p r) c -> r p c", r=128))
                M["p3tab"] = dma(
                    E, "cQT", qtab_s[:].rearrange("r (p c) -> r p c", p=NT),
                    qsin_d.rearrange("(p r) c -> r p c", r=128))
                for p in range(NT):
                    if p >= 2:
                        wt(E, M[f"p3xa_{p - 2}"])
                        wt(E, M[f"p3sq_{p - 2}"])
                    M[f"p3xld_{p}"] = dma(
                        E, "p3A" if p % 2 == 0 else "p3B",
                        x1b[:, (p % 2) * D:(p % 2 + 1) * D],
                        x1sp[p * 128:(p + 1) * 128, :])

            @body("vector")
            def _(E):
                wt(E, M["p3tab"])
                wt(E, M["p0"])
                for p in range(NT):
                    cs = (p % 2) * 16
                    wt(E, M[f"p3xld_{p}"])
                    xt = x1b[:, (p % 2) * D:(p % 2 + 1) * D]
                    E.reduce_sum(out=col[:, cs:cs + 1], in_=xt, axis=AX)
                    jnk(E)
                    sV.inc(E.tensor_scalar(out=col[:, cs + 1:cs + 2],
                                           in0=col[:, cs:cs + 1],
                                           scalar1=-1.0 / D, scalar2=None,
                                           op0=ALU.mult))
                    M[f"p3mneg_{p}"] = ("V", sV.v)
                    wt(E, M[f"p3sq_{p}"])
                    sV.inc(E.tensor_scalar(out=col[:, cs + 2:cs + 3],
                                           in0=col[:, cs + 3:cs + 4],
                                           scalar1=1.0 / D, scalar2=LN_EPS,
                                           op0=ALU.mult, op1=ALU.add))
                    M[f"p3var_{p}"] = ("V", sV.v)
                    wt(E, M[f"p3sqrt_{p}"])
                    E.reciprocal(out=col[:, cs + 7:cs + 8],
                                 in_=col[:, cs + 5:cs + 6])
                    jnk(E)
                    jnk(E)
                    if p > 0:
                        wt(E, M[f"p3lat_{p - 1}"])
                    sV.inc(E.tensor_scalar(out=xa_bf[:], in0=xt,
                                           scalar1=col[:, cs + 1:cs + 2],
                                           scalar2=col[:, cs + 7:cs + 8],
                                           op0=ALU.add, op1=ALU.mult))
                    M[f"p3xa_{p}"] = ("V", sV.v)
                    for gfl in range(2):
                        b = 4 + gfl
                        wt(E, M[f"p3xafg_{p}_{gfl}"])
                        sV.inc(E.tensor_copy(
                            out=xaT[:, gfl * 512:gfl * 512 + 512],
                            in_=pb[b][:]))
                        r_done(b, "V")
                    M[f"p3xaT_{p}"] = ("V", sV.v)
                    wt(E, M[f"p3q_{p}_1"])
                    qp3 = pb03[:, 0:1024].rearrange("p (h d) -> p h d", h=16)
                    qp1 = qp3[:, :, 0:32]
                    qp2 = qp3[:, :, 32:64]
                    qt = q_sb[:, p * D:(p + 1) * D]
                    q3 = qt.rearrange("p (h d) -> p h d", h=16)
                    q1 = q3[:, :, 0:32]
                    q2 = q3[:, :, 32:64]
                    tc = qtab_c[:, p * 512:(p + 1) * 512]
                    ts2 = qtab_s[:, p * 512:(p + 1) * 512]
                    Av = ropes[:, 0:512]
                    Bv = ropes[:, 512:1024]
                    Cv = ropes[:, 1024:1536]
                    Dv = ropes[:, 1536:2048]
                    E.tensor_mul(out=Av, in0=qp1, in1=tc)
                    E.tensor_mul(out=Bv, in0=qp2, in1=ts2)
                    E.tensor_mul(out=Cv, in0=qp1, in1=ts2)
                    E.tensor_mul(out=Dv, in0=qp2, in1=tc)
                    r_done(0, "V")
                    r_done(1, "V")
                    E.tensor_sub(out=q1, in0=Av, in1=Bv)
                    sV.inc(E.tensor_add(out=q2, in0=Cv, in1=Dv))
                    M[f"p3rope_{p}"] = ("V", sV.v)
                    wt(E, M[f"p3lat_{p}"])
                    E.reduce_sum(out=col[:, cs + 12:cs + 13],
                                 in_=pb[2][:, 0:LAT], axis=AX)
                    jnk(E)
                    sV.inc(E.tensor_scalar(out=col[:, cs + 13:cs + 14],
                                           in0=col[:, cs + 12:cs + 13],
                                           scalar1=-1.0 / LAT, scalar2=None,
                                           op0=ALU.mult))
                    M[f"p3lm_{p}"] = ("V", sV.v)
                    wt(E, M[f"p3lsq_{p}"])
                    sV.inc(E.tensor_scalar(out=col[:, cs + 14:cs + 15],
                                           in0=col[:, cs + 15:cs + 16],
                                           scalar1=1.0 / LAT, scalar2=LN_EPS,
                                           op0=ALU.mult, op1=ALU.add))
                    M[f"p3lv_{p}"] = ("V", sV.v)
                    wt(E, M[f"p3lsqrt_{p}"])
                    E.reciprocal(out=col[:, cs + 12:cs + 13],
                                 in_=col[:, cs + 14:cs + 15])
                    jnk(E)
                    jnk(E)
                    E.tensor_scalar(out=lat_f[:], in0=pb[2][:, 0:LAT],
                                    scalar1=col[:, cs + 13:cs + 14],
                                    scalar2=col[:, cs + 12:cs + 13],
                                    op0=ALU.add, op1=ALU.mult)
                    r_done(2, "V")
                    E.tensor_mul(out=lat_f[:], in0=lat_f[:], in1=gam_sb[:])
                    if p > 0:
                        wt(E, M[f"p3latst_{p - 1}"])
                    sV.inc(E.tensor_add(out=lat_bf[:], in0=lat_f[:],
                                        in1=bet_sb[:]))
                    M[f"p3latbf_{p}"] = ("V", sV.v)

            @body("scalar")
            def _(E):
                for p in range(NT):
                    cs = (p % 2) * 16
                    xt = x1b[:, (p % 2) * D:(p % 2 + 1) * D]
                    wt(E, M[f"p3mneg_{p}"])
                    sS.inc(E.activation(out=scr3[:, 0:D], in_=xt,
                                        func=ACTF.Square,
                                        bias=col[:, cs + 1:cs + 2], scale=1.0,
                                        accum_out=col[:, cs + 3:cs + 4]))
                    M[f"p3sq_{p}"] = ("S", sS.v)
                    wt(E, M[f"p3var_{p}"])
                    sS.inc(E.activation(out=col[:, cs + 5:cs + 6],
                                        in_=col[:, cs + 2:cs + 3],
                                        func=ACTF.Sqrt))
                    M[f"p3sqrt_{p}"] = ("S", sS.v)
                    wt(E, M[f"p3lm_{p}"])
                    sS.inc(E.activation(out=scr3[:, 0:LAT], in_=pb[2][:, 0:LAT],
                                        func=ACTF.Square,
                                        bias=col[:, cs + 13:cs + 14], scale=1.0,
                                        accum_out=col[:, cs + 15:cs + 16]))
                    r_done(2, "S")
                    M[f"p3lsq_{p}"] = ("S", sS.v)
                    wt(E, M[f"p3lv_{p}"])
                    sS.inc(E.activation(out=col[:, cs + 14:cs + 15],
                                        in_=col[:, cs + 14:cs + 15],
                                        func=ACTF.Sqrt))
                    M[f"p3lsqrt_{p}"] = ("S", sS.v)

            @body("tensor")
            def _(E):
                wt(E, M["p3w"])
                for p in range(NT):
                    wt(E, M[f"p3xa_{p}"])
                    for gfl in range(2):
                        b = 4 + gfl
                        t_acquire(E, b)
                        for j in range(4):
                            mm = E.matmul(
                                out=pb[b][:, j * 128:(j + 1) * 128],
                                lhsT=xa_bf[:, gfl * 512 + j * 128:
                                           gfl * 512 + j * 128 + 128],
                                rhs=ident_sb[:], start=True, stop=True)
                        sT.inc(mm)
                        t_wrote(b)
                        M[f"p3xafg_{p}_{gfl}"] = ("T", sT.v)
                    wt(E, M[f"p3xaT_{p}"])
                    for oc2 in range(2):
                        t_acquire(E, oc2)
                        for ib in range(8):
                            mm = E.matmul(
                                out=pb[oc2][:],
                                lhsT=xaT[:, ib * 128:(ib + 1) * 128],
                                rhs=wq_sb[:, ib * D + oc2 * 512:
                                         ib * D + oc2 * 512 + 512],
                                start=(ib == 0), stop=(ib == 7))
                        sT.inc(mm)
                        t_wrote(oc2)
                        M[f"p3q_{p}_{oc2}"] = ("T", sT.v)
                    t_acquire(E, 2)
                    for ib in range(8):
                        mm = E.matmul(out=pb[2][:, 0:LAT],
                                      lhsT=xaT[:, ib * 128:(ib + 1) * 128],
                                      rhs=wdkv_sb[:, ib * LAT:(ib + 1) * LAT],
                                      start=(ib == 0), stop=(ib == 7))
                    sT.inc(mm)
                    t_wrote(2)
                    M[f"p3lat_{p}"] = ("T", sT.v)

            @body("gpsimd")
            def _(E):
                wt(E, M["p0"])
                for p in range(NT):
                    wt(E, M[f"p3latbf_{p}"])
                    M[f"p3latst_{p}"] = dma(
                        E, "lst", lat_pv[p * 128:(p + 1) * 128, :], lat_bf[:])
                M["p3done"] = M[f"p3latst_{NT - 1}"]

            barrier()
            sbt_free("qtab_c", "qtab_s", "wq_sb", "wdkv_sb", "x1b", "xa_bf",
                     "xaT", "ropes", "lat_f", "lat_bf", "scr3")

        if "attn" in phases:
            latT = sbt("latT", [128, 2 * T], BF16)
            wuk_sb = sbt("wuk_sb", [128, 2 * D], BF16)
            wuv_sb = sbt("wuv_sb", [128, 2 * D], BF16)
            kT_sb = sbt("kT_sb", [128, 4 * T], BF16)
            v_sb = sbt("v_sb", [128, 32 * 520], BF16)
            ktab = sbt("ktab", [128, 2 * 512], BF16)
            k_tok = sbt("k_tok", [128, 512], BF16)
            qT_sb = sbt("qT_sb", [128, 512], BF16)
            pT_sb = sbt("pT_sb", [128, 2048], BF16)
            ropes4 = sbt("ropes4", [128, 1024], BF16)

            @body("gpsimd")
            def _(E):
                with ExitStack() as rs:
                    r = rs.enter_context
                    goff = r(E.register("goff"))
                    wfo = r(E.register("wfo"))
                    rfo = r(E.register("rfo"))
                    m0r = r(E.register("m0r"))
                    m1r = r(E.register("m1r"))
                    f0 = r(E.register("f0"))
                    f1 = r(E.register("f1"))
                    e0 = r(E.register("e0"))
                    e1 = r(E.register("e1"))
                    more = r(E.register("more"))
                    cnt = r(E.register("cnt"))
                    wt(E, M["p3done"])
                    E.reg_load(goff, offs_sb[0:1, 0:1])
                    E.reg_load(wfo, offs_sb[0:1, 1:2])
                    E.reg_load(rfo, offs_sb[0:1, 2:3])
                    E.reg_load(m0r, magic_sb[0:1, 0:1])
                    E.reg_load(m1r, magic_sb[0:1, 1:2])
                    M["latgat"] = dma(
                        E, "lst",
                        RAP(lat_sh.tensor, goff,
                            [[32768, 32], [LAT, 64], [1, LAT]]),
                        lat_pv.rearrange("(lg i) c -> lg i c", i=64))
                    wt(E, M["latgat"])
                    dma(E, "flg", RAP(flag_sh.tensor, wfo, [[2, 1], [1, 2]]),
                        magic_sb[:])
                    E.reg_mov(more, 1)
                    E.reg_mov(cnt, MAXPOLL)
                    with E.While(more):
                        E.reg_load((f0, f1),
                                   RAP(flag_sh.tensor, rfo, [[2, 1], [1, 2]]))
                        E.reg_alu(e0, f0, m0r, ALU.is_equal)
                        E.reg_alu(e1, f1, m1r, ALU.is_equal)
                        E.reg_alu(e0, e0, e1, ALU.bitwise_and)
                        E.reg_alu(more, e0, 1, ALU.bitwise_xor)
                        E.reg_add(cnt, cnt, -1)
                        E.reg_alu(e1, cnt, 0, ALU.is_gt)
                        E.reg_alu(more, more, e1, ALU.bitwise_and)
                    sG.inc(E.nop())
                    M["xchg"] = ("G", sG.v)
                dma(E, "cP4w", wuk_sb[:].rearrange("p (i c) -> p i c", i=2),
                    wuk_d.rearrange("(i p) c -> p i c", p=128))
                M["p4w"] = dma(
                    E, "cP4w", wuv_sb[:].rearrange("p (i c) -> p i c", i=2),
                    wuv_d.rearrange("(i p) c -> p i c", p=128))
                sG.inc(E.memset(
                    RAP(v_sb, 64, [[520 * 32, 128], [520, 32], [65, 8]]), 1.0))
                M["vones"] = ("G", sG.v)

            @body("sync")
            def _(E):
                wt(E, M["xchg"])
                dma(E, "cLT", latT[:, 0:T], lat_sh[:, 0:128], transpose=True)
                M["latT"] = dma(E, "cLT", latT[:, T:2 * T], lat_sh[:, 128:256],
                                transpose=True)

            pt_guard = {0: ("none", 0), 1: ("none", 0)}
            qT_guard = ("none", 0)
            av_guard = {6: ("none", 0), 7: ("none", 0)}
            sc_guard = {0: ("none", 0), 1: ("none", 0)}
            scpair = [pb03[:, 0:1024], pb45[:, 0:1024]]
            bctr = [0, 0, 0]

            def batches_of(p):
                nkb = 2 * p + 2
                return nkb, [(b, list(range(8 * b, min(8 * b + 8, nkb))))
                             for b in range((nkb + 7) // 8)]

            for half in range(2):
                @body("sync")
                def _(E, half=half):
                    for kb in range(32):
                        if kb >= 2:
                            wt(E, M[f"krope_{half}_{kb - 2}"])
                        elif half > 0:
                            wt(E, M[f"krope_{half - 1}_{30 + kb}"])
                        sl = "ktA" if kb % 2 == 0 else "ktB"
                        dma(E, sl,
                            ktab[:, (kb % 2) * 512:(kb % 2) * 512 + 256],
                            kcos_d[kb * 128:(kb + 1) * 128, :])
                        M[f"ktab_{half}_{kb}"] = dma(
                            E, sl,
                            ktab[:, (kb % 2) * 512 + 256:(kb % 2) * 512 + 512],
                            ksin_d[kb * 128:(kb + 1) * 128, :])

                @body("tensor")
                def _(E, half=half):
                    if half == 0:
                        wt(E, M["latT"])
                        wt(E, M["p4w"])
                    else:
                        wt(E, sc_guard[0])
                        wt(E, sc_guard[1])
                    for kb in range(32):
                        t_acquire(E, 0)
                        for c in range(2):
                            mm = E.matmul(
                                out=pb[0][:],
                                lhsT=latT[:, c * T + kb * 128:
                                         c * T + kb * 128 + 128],
                                rhs=wuk_sb[:, c * D + half * 512:
                                         c * D + half * 512 + 512],
                                start=(c == 0), stop=(c == 1))
                        sT.inc(mm)
                        t_wrote(0)
                        M[f"kmm_{half}_{kb}"] = ("T", sT.v)
                        if kb > 0:
                            wt(E, M[f"vev_{half}_{kb - 1}"])
                        elif half > 0:
                            wt(E, M[f"vev_{half - 1}_31"])
                        t_acquire(E, 1)
                        for c in range(2):
                            mm = E.matmul(
                                out=pb[1][:],
                                lhsT=latT[:, c * T + kb * 128:
                                         c * T + kb * 128 + 128],
                                rhs=wuv_sb[:, c * D + half * 512:
                                         c * D + half * 512 + 512],
                                start=(c == 0), stop=(c == 1))
                        sT.inc(mm)
                        t_wrote(1)
                        M[f"vmm_{half}_{kb}"] = ("T", sT.v)
                        wt(E, M[f"krope_{half}_{kb}"])
                        t_acquire(E, 2)
                        for j in range(4):
                            mm = E.matmul(out=pb[2][:, j * 128:(j + 1) * 128],
                                          lhsT=k_tok[:, j * 128:(j + 1) * 128],
                                          rhs=ident_sb[:], start=True,
                                          stop=True)
                        sT.inc(mm)
                        t_wrote(2)
                        M[f"kflip_{half}_{kb}"] = ("T", sT.v)

                @body("scalar")
                def _(E, half=half):
                    if half == 0:
                        wt(E, M["vones"])
                    for kb in range(32):
                        wt(E, M[f"vmm_{half}_{kb}"])
                        vdst = RAP(v_sb, kb * 520,
                                   [[520 * 32, 128], [65, 8], [1, 64]])
                        sS.inc(E.copy(out=vdst, in_=pb[1][:]))
                        r_done(1, "S")
                        M[f"vev_{half}_{kb}"] = ("S", sS.v)

                @body("vector")
                def _(E, half=half):
                    for kb in range(32):
                        wt(E, M[f"kmm_{half}_{kb}"])
                        wt(E, M[f"ktab_{half}_{kb}"])
                        if kb > 0:
                            wt(E, M[f"kflip_{half}_{kb - 1}"])
                        elif half > 0:
                            wt(E, M[f"kflip_{half - 1}_31"])
                        kp3 = pb[0].rearrange("p (h d) -> p h d", h=8)
                        kp1 = kp3[:, :, 0:32]
                        kp2 = kp3[:, :, 32:64]
                        kt3 = k_tok[:].rearrange("p (h d) -> p h d", h=8)
                        k1 = kt3[:, :, 0:32]
                        k2 = kt3[:, :, 32:64]
                        tc = ktab[:, (kb % 2) * 512:(kb % 2) * 512 + 256]
                        ts2 = ktab[:, (kb % 2) * 512 + 256:(kb % 2) * 512 + 512]
                        Av = ropes4[:, 0:256]
                        Bv = ropes4[:, 256:512]
                        Cv = ropes4[:, 512:768]
                        Dv = ropes4[:, 768:1024]
                        E.tensor_mul(out=Av, in0=kp1, in1=tc)
                        E.tensor_mul(out=Bv, in0=kp2, in1=ts2)
                        E.tensor_mul(out=Cv, in0=kp1, in1=ts2)
                        E.tensor_mul(out=Dv, in0=kp2, in1=tc)
                        r_done(0, "V")
                        E.tensor_sub(out=k1, in0=Av, in1=Bv)
                        sV.inc(E.tensor_add(out=k2, in0=Cv, in1=Dv))
                        M[f"krope_{half}_{kb}"] = ("V", sV.v)
                        wt(E, M[f"kflip_{half}_{kb}"])
                        kdst = RAP(kT_sb, kb * 128,
                                   [[4 * T, 128], [T, 4], [1, 128]])
                        sV.inc(E.tensor_copy(out=kdst, in_=pb[2][:]))
                        r_done(2, "V")
                        M[f"kTev_{half}_{kb}"] = ("V", sV.v)

                @body("tensor")
                def _(E, half=half):
                    wt(E, M[f"kTev_{half}_31"])
                    wt(E, M[f"vev_{half}_31"])
                    for p in range(NT):
                        t_acquire(E, 3)
                        for j in range(4):
                            jj = 4 * half + j
                            mm = E.matmul(
                                out=pb[3][:, j * 128:(j + 1) * 128],
                                lhsT=q_sb[:, p * D + jj * 128:
                                         p * D + jj * 128 + 128],
                                rhs=ident_sb[:], start=True, stop=True)
                        sT.inc(mm)
                        t_wrote(3)
                        M[f"qTf_{half}_{p}"] = ("T", sT.v)
                        wt(E, M[f"qTe_{half}_{p}"])
                        nkb, bts = batches_of(p)
                        seq = [(hw, b, kbs) for hw in range(8)
                               for b, kbs in bts]

                        def av_emit(hw, b, pr, kbs):
                            bav = 6 + (hw % 2)
                            if b == 0:
                                wt(E, av_guard[bav])
                            wt(E, M[f"pTrdy_{half}_{p}_{hw}_{b}"])
                            for i, kb in enumerate(kbs):
                                mm = E.matmul(
                                    out=pb[bav][:, 0:65],
                                    lhsT=pT_sb[:, pr * 1024 + i * 128:
                                               pr * 1024 + i * 128 + 128],
                                    rhs=v_sb[:, kb * 520 + hw * 65:
                                             kb * 520 + hw * 65 + 65],
                                    start=(kb == 0),
                                    stop=(kb == nkb - 1))
                            sT.inc(mm)
                            M[f"avg_{half}_{p}_{hw}_{b}"] = ("T", sT.v)
                            if b == len(bts) - 1:
                                t_wrote(bav)
                                M[f"av_{half}_{p}_{hw}"] = ("T", sT.v)
                                av_guard[bav] = M[f"norm_{half}_{p}_{hw}"]

                        prev = None
                        for hw, b, kbs in seq:
                            jj, hi = hw // 2, hw % 2
                            pr = bctr[0] % 2
                            bctr[0] += 1
                            wt(E, sc_guard[pr])
                            for i, kb in enumerate(kbs):
                                mm = E.matmul(
                                    out=scpair[pr][:, i * 128:(i + 1) * 128],
                                    lhsT=kT_sb[64 * hi:64 * hi + 64,
                                               jj * T + kb * 128:
                                               jj * T + kb * 128 + 128],
                                    rhs=qT_sb[64 * hi:64 * hi + 64,
                                              jj * 128:jj * 128 + 128],
                                    start=True, stop=True)
                            sT.inc(mm)
                            M[f"sc_{half}_{p}_{hw}_{b}"] = ("T", sT.v)
                            sc_guard[pr] = M[f"exp_{half}_{p}_{hw}_{b}"]
                            if prev is not None:
                                av_emit(*prev)
                            prev = (hw, b, pr, kbs)
                        av_emit(*prev)
                        M[f"scl_{half}_{p}"] = ("T", sT.v)

                @body("scalar")
                def _(E, half=half):
                    for p in range(NT):
                        nkb, bts = batches_of(p)
                        for hw in range(8):
                            for b, kbs in bts:
                                pr = bctr[1] % 2
                                wt(E, M[f"sc_{half}_{p}_{hw}_{b}"])
                                wt(E, pt_guard[pr])
                                sS.inc(E.activation(
                                    out=pT_sb[:, pr * 1024:
                                              pr * 1024 + len(kbs) * 128],
                                    in_=scpair[pr][:, 0:len(kbs) * 128],
                                    func=ACTF.Exp))
                                M[f"exp_{half}_{p}_{hw}_{b}"] = ("S", sS.v)
                                pt_guard[pr] = M[f"avg_{half}_{p}_{hw}_{b}"]
                                bctr[1] += 1

                @body("vector")
                def _(E, half=half):
                    nonlocal qT_guard
                    for p in range(NT):
                        wt(E, M[f"qTf_{half}_{p}"])
                        wt(E, qT_guard)
                        sV.inc(E.tensor_copy(out=qT_sb[:], in_=pb[3][:]))
                        r_done(3, "V")
                        M[f"qTe_{half}_{p}"] = ("V", sV.v)
                        qT_guard = M[f"scl_{half}_{p}"]
                        nkb, bts = batches_of(p)
                        for hw in range(8):
                            for b, kbs in bts:
                                pr = bctr[2] % 2
                                bctr[2] += 1
                                dia = [i for i, kb in enumerate(kbs)
                                       if kb in (2 * p, 2 * p + 1)]
                                if dia:
                                    wt(E, M[f"exp_{half}_{p}_{hw}_{b}"])
                                    for i in dia:
                                        kb = kbs[i]
                                        mi = kb - 2 * p
                                        sl = pT_sb[:, pr * 1024 + i * 128:
                                                   pr * 1024 + i * 128 + 128]
                                        mm = E.tensor_mul(
                                            out=sl, in0=sl,
                                            in1=masks_sb[:, mi * 128:
                                                         (mi + 1) * 128])
                                    sV.inc(mm)
                                    M[f"pTrdy_{half}_{p}_{hw}_{b}"] = ("V", sV.v)
                                else:
                                    M[f"pTrdy_{half}_{p}_{hw}_{b}"] = \
                                        M[f"exp_{half}_{p}_{hw}_{b}"]
                            bav = 6 + (hw % 2)
                            wt(E, M[f"av_{half}_{p}_{hw}"])
                            E.reciprocal(out=col[:, 20 + hw:21 + hw],
                                         in_=pb[bav][:, 64:65])
                            if hw > 0:
                                pv = 6 + ((hw - 1) % 2)
                                h_abs = 8 * half + hw - 1
                                sV.inc(E.tensor_scalar(
                                    out=attnout[:, p * D + h_abs * 64:
                                                p * D + h_abs * 64 + 64],
                                    in0=pb[pv][:, 0:64],
                                    scalar1=col[:, 19 + hw:20 + hw],
                                    scalar2=None, op0=ALU.mult))
                                r_done(pv, "V")
                                M[f"norm_{half}_{p}_{hw - 1}"] = ("V", sV.v)
                        jnk(E)
                        h_abs = 8 * half + 7
                        sV.inc(E.tensor_scalar(
                            out=attnout[:, p * D + h_abs * 64:
                                        p * D + h_abs * 64 + 64],
                            in0=pb[7][:, 0:64], scalar1=col[:, 27:28],
                            scalar2=None, op0=ALU.mult))
                        r_done(7, "V")
                        M[f"norm_{half}_{p}_7"] = ("V", sV.v)

            barrier()
            sbt_free("latT", "wuk_sb", "wuv_sb", "kT_sb", "v_sb", "ktab",
                     "k_tok", "qT_sb", "pT_sb", "ropes4")

        if "wo" in phases:
            wo_sb = sbt("wo_sb", [128, 8 * D], BF16)
            x1b5 = sbt("x1b5", [128, 2 * D], F32)
            x2b5 = sbt("x2b5", [128, 2 * D], F32)
            aoT = sbt("aoT", [128, D], BF16)

            @body("gpsimd")
            def _(E):
                M["p5w"] = dma(
                    E, "cP5w", wo_sb[:].rearrange("p (i c) -> p i c", i=8),
                    wo_d.rearrange("(i p) c -> p i c", p=128))

            @body("sync")
            def _(E):
                for p in range(NT):
                    if p >= 2:
                        wt(E, M[f"p5res_{p - 2}"])
                    M[f"p5xld_{p}"] = dma(
                        E, "p3A" if p % 2 == 0 else "p3B",
                        x1b5[:, (p % 2) * D:(p % 2 + 1) * D],
                        x1sp[p * 128:(p + 1) * 128, :])

            @body("tensor")
            def _(E):
                wt(E, M["p5w"])
                for p in range(NT):
                    wt(E, M[f"norm_1_{p}_7"])
                    for gfl in range(2):
                        b = 4 + gfl
                        t_acquire(E, b)
                        for j in range(4):
                            mm = E.matmul(
                                out=pb[b][:, j * 128:(j + 1) * 128],
                                lhsT=attnout[:, p * D + gfl * 512 + j * 128:
                                            p * D + gfl * 512 + j * 128 + 128],
                                rhs=ident_sb[:], start=True, stop=True)
                        sT.inc(mm)
                        t_wrote(b)
                        M[f"p5aof_{p}_{gfl}"] = ("T", sT.v)
                    wt(E, M[f"p5aoT_{p}"])
                    for oc2 in range(2):
                        t_acquire(E, oc2)
                        for ib in range(8):
                            mm = E.matmul(
                                out=pb[oc2][:],
                                lhsT=aoT[:, ib * 128:(ib + 1) * 128],
                                rhs=wo_sb[:, ib * D + oc2 * 512:
                                         ib * D + oc2 * 512 + 512],
                                start=(ib == 0), stop=(ib == 7))
                        sT.inc(mm)
                        t_wrote(oc2)
                        M[f"p5mm_{p}_{oc2}"] = ("T", sT.v)

            @body("vector")
            def _(E):
                for p in range(NT):
                    for gfl in range(2):
                        b = 4 + gfl
                        wt(E, M[f"p5aof_{p}_{gfl}"])
                        sV.inc(E.tensor_copy(out=aoT[:, gfl * 512:gfl * 512 + 512],
                                             in_=pb[b][:]))
                        r_done(b, "V")
                    M[f"p5aoT_{p}"] = ("V", sV.v)
                    wt(E, M[f"p5xld_{p}"])
                    for oc2 in range(2):
                        wt(E, M[f"p5mm_{p}_{oc2}"])
                        if p >= 2:
                            wt(E, M[f"p5xst_{p - 2}"])
                        sV.inc(E.tensor_add(
                            out=x2b5[:, (p % 2) * D + oc2 * 512:
                                     (p % 2) * D + oc2 * 512 + 512],
                            in0=pb[oc2][:],
                            in1=x1b5[:, (p % 2) * D + oc2 * 512:
                                     (p % 2) * D + oc2 * 512 + 512]))
                        r_done(oc2, "V")
                    M[f"p5res_{p}"] = ("V", sV.v)

            @body("gpsimd")
            def _(E):
                for p in range(NT):
                    wt(E, M[f"p5res_{p}"])
                    M[f"p5xst_{p}"] = dma(
                        E, "p5A" if p % 2 == 0 else "p5B",
                        x1sp[p * 128:(p + 1) * 128, :],
                        x2b5[:, (p % 2) * D:(p % 2 + 1) * D])

            barrier()
            sbt_free("wo_sb", "x1b5", "x2b5", "aoT")

        if need_ao:
            sbt_free("attnout", "q_sb")

        if "ffn2" in phases:
            emit_ffn(2, x1sp, yout)
            final_is_yout = True
        else:
            final_is_yout = (len(phases) <= 2 and "ffn1" in phases)

        if not final_is_yout:
            cpbuf = sbt("cpbuf", [128, D], F32)

            @body("gpsimd")
            def _(E):
                srcd = x1sp
                for p in range(NT):
                    dma(E, "cp", cpbuf[:], srcd[p * 128:(p + 1) * 128, :])
                    if E is not _FAKE:
                        E.wait_ge(C["cp"].sem, C["cp"].v)
                    dma(E, "cp", yout[p * 128:(p + 1) * 128, :], cpbuf[:])
                    if E is not _FAKE:
                        E.wait_ge(C["cp"].sem, C["cp"].v)
            sbt_free("cpbuf")

        @body("gpsimd")
        def _(E):
            if E is not _FAKE:
                for n in dma_sems:
                    if C[n].v > 0:
                        E.wait_ge(C[n].sem, C[n].v)

        if dbg_d is not None:
            sbt_free("dbg_buf", "dbgc")

    emit(final=False)
    emit(final=True)
    nc._marks = dict(M.m0)
    ctx.close()
    return nc



def _tables(par):
    inv = 10000.0 ** (-(np.arange(32, dtype=np.float64)) / 32.0)
    posk = np.arange(T, dtype=np.float64)
    angk = posk[:, None] * inv[None, :]
    g = np.arange(32) * 2 + par
    posq = (g[:, None] * 64 + np.arange(64)[None, :]).reshape(-1).astype(np.float64)
    angq = posq[:, None] * inv[None, :]
    qcos = np.tile((np.cos(angq) * 0.125).astype(np.float32), (1, 16))
    qsin = np.tile((np.sin(angq) * 0.125).astype(np.float32), (1, 16))
    kcos = np.tile(np.cos(angk).astype(np.float32), (1, 8))
    ksin = np.tile(np.sin(angk).astype(np.float32), (1, 8))
    return (qcos.astype(ml_dtypes.bfloat16), qsin.astype(ml_dtypes.bfloat16),
            kcos.astype(ml_dtypes.bfloat16), ksin.astype(ml_dtypes.bfloat16))


def _masks(par):
    r = np.arange(128)
    kap = np.arange(128)[:, None]
    qpos = np.where(r < 64, 64 * par + r, 64 * par + r + 64)
    m0 = (kap <= qpos[None, :]).astype(np.float32)
    m1 = ((128 + kap) <= qpos[None, :]).astype(np.float32)
    return np.ascontiguousarray(
        np.concatenate([m0, m1], axis=1).astype(ml_dtypes.bfloat16))


_NC_CACHE = {}


def kernel(L0_attn_x, L0_attn_W_q, L0_attn_W_dkv, L0_attn_W_uk, L0_attn_W_uv,
           L0_attn_W_o, L0_attn_ln_gamma, L0_attn_ln_beta, L0_ffn_w_gate,
           L0_ffn_w_up, L0_ffn_w_down):
    x = np.ascontiguousarray(np.asarray(L0_attn_x, dtype=np.float32))
    wq = np.ascontiguousarray(np.asarray(L0_attn_W_q, dtype=np.float32))
    wdkv = np.ascontiguousarray(np.asarray(L0_attn_W_dkv, dtype=np.float32))
    wuk = np.ascontiguousarray(np.asarray(L0_attn_W_uk, dtype=np.float32))
    wuv = np.ascontiguousarray(np.asarray(L0_attn_W_uv, dtype=np.float32))
    wo = np.ascontiguousarray(np.asarray(L0_attn_W_o, dtype=np.float32))
    gam = np.asarray(L0_attn_ln_gamma, dtype=np.float32)
    bet = np.asarray(L0_attn_ln_beta, dtype=np.float32)
    wg = np.ascontiguousarray(np.asarray(L0_ffn_w_gate, dtype=np.float32))
    wu = np.ascontiguousarray(np.asarray(L0_ffn_w_up, dtype=np.float32))
    wd = np.ascontiguousarray(np.asarray(L0_ffn_w_down, dtype=np.float32))

    gam_t = np.ascontiguousarray(np.broadcast_to(gam, (128, LAT)))
    bet_t = np.ascontiguousarray(np.broadcast_to(bet, (128, LAT)))
    ident = np.eye(128, dtype=ml_dtypes.bfloat16)
    magic = np.random.randint(1, 2**31 - 1, size=(1, 2)).astype(np.int32)

    if "nc" not in _NC_CACHE:
        _NC_CACHE["nc"] = build_nc()
    nc = _NC_CACHE["nc"]

    in_maps = []
    for c in range(8):
        b, par = c // 2, c % 2
        xb = np.ascontiguousarray(x[b].reshape(32, 2, 64, D)[:, par]
                                  .reshape(2048, D))
        qcos, qsin, kcos, ksin = _tables(par)
        xoffs = np.array([[64 * par * LAT, par * 2, (1 - par) * 2,
                           0, 0, 0, 0, 0]], dtype=np.int32)
        in_maps.append({
            "xin": xb, "wq": wq, "wdkv": wdkv, "wuk": wuk, "wuv": wuv,
            "wo": wo, "gam": gam_t, "bet": bet_t, "wg": wg, "wu": wu,
            "wd": wd, "qcos": qcos, "qsin": qsin, "kcos": kcos,
            "ksin": ksin, "masks": _masks(par), "ident": ident,
            "xoffs": xoffs, "magic": magic,
        })

    res = run_bass_kernel_spmd(nc, in_maps, core_ids=list(range(8)))

    out = np.empty((B, T, D), dtype=np.float32)
    for c in range(8):
        b, par = c // 2, c % 2
        yc = np.asarray(res.results[c]["yout"])
        out[b].reshape(32, 2, 64, D)[:, par] = yc.reshape(32, 64, D)
    return out
